# version 01
# baseline (speedup 1.0000x reference)
import numpy as np
from contextlib import ExitStack

import concourse.bass as bass
import concourse.mybir as mybir
from concourse.bass_utils import run_bass_kernel_spmd

F32 = mybir.dt.float32
BF16 = mybir.dt.bfloat16
U8 = mybir.dt.uint8
AF = mybir.ActivationFunctionType
ALU = mybir.AluOpType
AXX = mybir.AxisListType.X

D = 1024
KD = 8
T = 4096
E = 64
HE = 768
KH = 6
SH = 1536
KSH = 12
FF = 4096
KFF = 32
NH = 16
SEQ = 2048
QTOK = 1024
NQ = T // QTOK
NTT = QTOK // 512


class Sem:
    def __init__(self, nc, name):
        self.h = nc.alloc_semaphore(name)
        self.v = 0


class Prog:
    def __init__(self):
        self.ops = {k: [] for k in "tvsgy"}

    def w(self, e, sem, val):
        if val and val > 0:
            self.ops[e].append(lambda eng, s=sem.h, v=val: eng.wait_ge(s, v))

    def do(self, e, build, inc=None, n=1):
        if inc is None:
            self.ops[e].append(build)
            return None
        inc.v += n
        after = inc.v
        self.ops[e].append(lambda eng, b=build, s=inc.h, m=n: b(eng).then_inc(s, m))
        return after


class NS:
    def __init__(self, **kw):
        self.__dict__.update(kw)


def build_nc(stages=("ffn1", "attn", "ffn2"), dbg=(), taps=(), tseq=0):
    nc = bass.Bass("TRN2", target_bir_lowering=False)
    _uid = [0]

    def un(nm):
        _uid[0] += 1
        return f"{nm}_{_uid[0]}"

    x_in = nc.dram_tensor("x_fm", [D, T], F32, kind="ExternalInput")
    wq = nc.dram_tensor("Wq", [D, D], F32, kind="ExternalInput")
    wk = nc.dram_tensor("Wk", [D, D], F32, kind="ExternalInput")
    wv = nc.dram_tensor("Wv", [D, D], F32, kind="ExternalInput")
    wo = nc.dram_tensor("Wo", [D, D], F32, kind="ExternalInput")
    w1 = nc.dram_tensor("W1", [D, FF], F32, kind="ExternalInput")
    w2 = nc.dram_tensor("W2", [FF, D], F32, kind="ExternalInput")
    vecs = {nm: nc.dram_tensor(nm, [D], F32, kind="ExternalInput")
            for nm in ("bq", "bk", "bv", "bo", "b2", "g1", "be1", "g2", "be2")}
    b1 = nc.dram_tensor("b1", [FF], F32, kind="ExternalInput")
    gw = nc.dram_tensor("gate_w", [D, E], F32, kind="ExternalInput")
    gbias = nc.dram_tensor("gate_bias", [E], F32, kind="ExternalInput")
    allg = nc.dram_tensor("all_g", [E, D, HE], F32, kind="ExternalInput")
    allu = nc.dram_tensor("all_u", [E, D, HE], F32, kind="ExternalInput")
    alld = nc.dram_tensor("all_d", [E, HE, D], F32, kind="ExternalInput")
    shg = nc.dram_tensor("sh_g", [D, SH], F32, kind="ExternalInput")
    shu = nc.dram_tensor("sh_u", [D, SH], F32, kind="ExternalInput")
    shd = nc.dram_tensor("sh_d", [SH, D], F32, kind="ExternalInput")
    cid32 = nc.dram_tensor("cid32", [128, 128], F32, kind="ExternalInput")

    out_fm = nc.dram_tensor("out_fm", [D, T], F32, kind="ExternalOutput")
    dbg_t = {name: nc.dram_tensor("dbg_" + name, [D, T], F32, kind="ExternalOutput")
             for name in dbg}
    tap_t = {}
    if "xln" in taps:
        tap_t["xln"] = nc.dram_tensor("tap_xln", [128, 8192], BF16, kind="ExternalOutput")
    if "scores" in taps:
        tap_t["scores"] = nc.dram_tensor("tap_scores", [128, 512], F32, kind="ExternalOutput")
    if "scraw" in taps:
        tap_t["scraw"] = nc.dram_tensor("tap_scraw", [128, 512], F32, kind="ExternalOutput")
    if "mask" in taps:
        tap_t["mask"] = nc.dram_tensor("tap_mask", [E, T], F32, kind="ExternalOutput")
    if "yacc" in taps:
        tap_t["yacc"] = nc.dram_tensor("tap_yacc", [128, 8192], F32, kind="ExternalOutput")
    if "hbuf" in taps:
        tap_t["hbuf"] = nc.dram_tensor("tap_hbuf", [128, 3072], BF16, kind="ExternalOutput")
    if "mb" in taps:
        tap_t["mb"] = nc.dram_tensor("tap_mb", [128, 1024], BF16, kind="ExternalOutput")
    for nm, shape, dt_ in (("aln", [128, 16384], BF16), ("akn", [128, 16384], BF16),
                           ("avn", [128, 16384], BF16), ("akv", [128, 512], BF16),
                           ("aqn", [128, 8192], BF16), ("actx", [128, 16384], BF16),
                           ("amid", [128, 16384], BF16), ("ahff", [128, 16384], BF16),
                           ("apre2", [128, 4096], F32)):
        if nm in taps:
            tap_t[nm] = nc.dram_tensor("tap_" + nm, shape, dt_, kind="ExternalOutput")

    xr1 = nc.dram_tensor("xr1", [D, T], F32, kind="Internal")
    xr2 = nc.dram_tensor("xr2", [D, T], F32, kind="Internal")
    maskT = nc.dram_tensor("maskTd", [E, T], F32, kind="Internal")

    allg_r = allg.ap().rearrange("e (k p) h -> e p k h", p=128)
    allu_r = allu.ap().rearrange("e (k p) h -> e p k h", p=128)
    alld_r = alld.ap().rearrange("e (k p) h -> e p k h", p=128)
    shg_r = shg.ap().rearrange("(k p) h -> p k h", p=128)
    shu_r = shu.ap().rearrange("(k p) h -> p k h", p=128)
    shd_r = shd.ap().rearrange("(k p) h -> p k h", p=128)
    gw_r = gw.ap().rearrange("(k p) e -> p k e", p=128)
    wr = {k: w.ap().rearrange("(k p) n -> p k n", p=128)
          for k, w in (("q", wq), ("k", wk), ("v", wv), ("o", wo))}
    w1_r = w1.ap().rearrange("(k p) n -> p k n", p=128)
    w2_r = w2.ap().rearrange("(s p) n -> p s n", p=128)
    x_in3 = x_in.ap().rearrange("(k p) t -> p k t", p=128)
    xr1_3 = xr1.ap().rearrange("(k p) t -> p k t", p=128)
    xr2_3 = xr2.ap().rearrange("(k p) t -> p k t", p=128)
    out3 = out_fm.ap().rearrange("(k p) t -> p k t", p=128)
    dbg3 = {k: v.ap().rearrange("(k p) t -> p k t", p=128) for k, v in dbg_t.items()}

    def addr_of(h):
        return nc.lookup_mloc(h).addr

    cid32_sb = nc.alloc_sbuf_tensor("cid32_sb", [128, 128], F32)
    cid16_sb = nc.alloc_sbuf_tensor("cid16_sb", [128, 128], BF16)
    onescol_bf = nc.alloc_sbuf_tensor("onescol_bf", [128, 8], BF16)
    onesrow_f = nc.alloc_sbuf_tensor("onesrow_f", [1, 128], F32)
    onesrow_bf = nc.alloc_sbuf_tensor("onesrow_bf", [1, 128], BF16)
    eps5 = nc.alloc_sbuf_tensor("eps5", [128, 1], F32)
    eps12 = nc.alloc_sbuf_tensor("eps12", [128, 1], F32)
    bqkv_bc = nc.alloc_sbuf_tensor("bqkv_bc", [128, 3072], F32)
    gbias_bc = nc.alloc_sbuf_tensor("gbias_bc", [128, 64], F32)
    gw_sb = nc.alloc_sbuf_tensor("gw_sb", [128, 512], F32)
    fmv_t = {nm: nc.alloc_sbuf_tensor(nm + "_fm", [128, 8], F32)
             for nm in ("bo", "b2", "g1", "be1", "g2", "be2")}
    b1_fm = nc.alloc_sbuf_tensor("b1_fm", [128, 32], F32)

    arn_w = nc.alloc_sbuf_tensor("arn_w", [128, 73728], U8)
    arn_xln = nc.alloc_sbuf_tensor("arn_xln", [128, 16384], U8)
    arn_y = nc.alloc_sbuf_tensor("arn_y", [128, 32768], U8)
    arn_a = nc.alloc_sbuf_tensor("arn_a", [128, 32768], U8)
    arn_b = nc.alloc_sbuf_tensor("arn_b", [128, 24576], U8)
    AW, AX, AY, AA, AB = (addr_of(h) for h in (arn_w, arn_xln, arn_y, arn_a, arn_b))

    def at(name, shape, dtype, base, off):
        return nc.alloc_sbuf_tensor_at(name, shape, dtype, offset=base + off)

    gbuf = [at(f"gbuf{p}", [128, 6144], BF16, AW, p * 36864) for p in (0, 1)]
    ubuf = [at(f"ubuf{p}", [128, 6144], BF16, AW, p * 36864 + 12288) for p in (0, 1)]
    dbuf = [at(f"dbuf{p}", [128, 6144], BF16, AW, p * 36864 + 24576) for p in (0, 1)]
    shg_sb = at("shg_sb", [128, 12288], BF16, AW, 0)
    shu_sb = at("shu_sb", [128, 12288], BF16, AW, 24576)
    shd_sb = at("shd_sb", [128, 12288], BF16, AW, 49152)
    xln = at("xln", [128, 8192], BF16, AX, 0)
    yacc = at("yacc", [128, 8192], F32, AY, 0)
    xstage = at("xstage", [128, 4096], F32, AA, 0)
    astage = at("astage", [128, 1024], F32, AA, 16384)
    xbx = at("xbx", [128, 1024], BF16, AA, 20480)
    xsqx = at("xsqx", [128, 1024], BF16, AA, 22528)
    scores_tm = at("scores_tm", [128, 512], F32, AA, 24576)
    tk_m8 = at("tk_m8", [128, 8], F32, AA, 26624)
    tk_sel = at("tk_sel", [128, 64], F32, AA, 26656)
    tk_row = at("tk_row", [128, 8], F32, AA, 26912)
    maskfm_ch = at("maskfm_ch", [64, 256], F32, AA, 26944)
    mbcast = [at(f"mbcast{p}", [128, 1024], BF16, AB, p * 2048) for p in (0, 1)]
    mrow = [at(f"mrow{p}", [1, 1024], BF16, AB, 4096 + p * 2048) for p in (0, 1)]
    hbuf = [at(f"hbuf{p}", [128, 3072], BF16, AB, 8192 + p * 6144) for p in (0, 1)]
    silug = [at(f"silug{p}", [128, 512], BF16, AB, 20480 + p * 1024) for p in (0, 1)]
    umb = [at(f"umb{p}", [128, 512], BF16, AB, 22528 + p * 1024) for p in (0, 1)]
    lnrow = at("lnrow", [1, 2048], F32, AB, 16384)

    x1ln = at("x1ln", [128, 16384], BF16, AY, 0)
    kn_tm = at("kn_tm", [128, 16384], BF16, AW, 36864)
    ctx_fm = at("ctx_fm", [128, 16384], BF16, AW, 36864)
    v_tm = at("v_tm", [128, 16384], BF16, AA, 0)
    qn_fm = at("qn_fm", [128, 8192], BF16, AX, 0)
    pre1 = at("pre1", [128, 4096], F32, AX, 0)
    joinx = at("joinx", [128, 4096], F32, AX, 0)
    wslot = [at(f"wslot{p}", [128, 8192], BF16, AW, p * 16384) for p in (0, 1)]
    w1slot = [at(f"w1slot{p}", [128, 4096], BF16, AW, p * 8192) for p in (0, 1)]
    w2slot = [at(f"w2slot{p}", [128, 4096], BF16, AW, 16384 + p * 8192) for p in (0, 1)]
    hff = at("hff", [128, 16384], BF16, AA, 0)
    kf = at("kf", [128, 1024], F32, AB, 0)
    rbuf = at("rbuf", [128, 1024], F32, AB, 4096)
    exbuf = at("exbuf", [128, 1024], F32, AB, 8192)
    qn_ch = at("qn_ch", [128, 1024], BF16, AB, 12288)
    nrmA = at("nrmA", [128, 16], F32, AB, 14336)
    nrmB = at("nrmB", [128, 16], F32, AB, 14400)
    kv_sb = at("kv_sb", [128, 512], BF16, AB, 14848)
    pre2 = at("pre2", [128, 4096], F32, AB, 0)

    S = NS(wld=Sem(nc, "wld"), xld=Sem(nc, "xld"), xst=Sem(nc, "xst"),
           out=Sem(nc, "outst"), mm=Sem(nc, "mm"), ev=Sem(nc, "ev"),
           sc=Sem(nc, "sc"), wfr=Sem(nc, "wfr"), mr=Sem(nc, "mr"))
    P = Prog()

    def soft_barrier():
        mv, ev, sv = S.mm.v, S.ev.v, S.sc.v
        P.w('t', S.ev, ev); P.w('t', S.sc, sv)
        P.w('v', S.mm, mv); P.w('v', S.sc, sv)
        P.w('s', S.mm, mv); P.w('s', S.ev, ev)

    with ExitStack() as pctx:
        row = pctx.enter_context(nc.sbuf_tensor("p_row", [1, 1024], F32))
        vtile = pctx.enter_context(nc.sbuf_tensor("p_vt", [32, 128], F32))
        ppr = pctx.enter_context(nc.psum_tensor(un("ppr"), [128, 512], F32))
        ppr2 = pctx.enter_context(nc.psum_tensor(un("ppr2"), [128, 512], F32))

        def yld(build):
            return P.do('y', build, inc=S.xld, n=16)

        yld(lambda s: s.dma_start(cid32_sb[:], cid32.ap()))
        yld(lambda s: s.dma_start(gw_sb.ap().rearrange("p (k e) -> p k e", k=KD), gw_r))
        P.do('g', lambda g: g.dma_start(cid16_sb[:], cid32.ap()), inc=S.wld, n=16)
        P.do('v', lambda v: v.memset(eps5[:], 1e-5))
        P.do('v', lambda v: v.memset(eps12[:], 1e-12))
        P.do('v', lambda v: v.memset(onesrow_f[:], 1.0))
        P.do('v', lambda v: v.memset(onesrow_bf[:], 1.0))
        ms = P.do('v', lambda v: v.memset(onescol_bf[:], 1.0), inc=S.ev)
        P.w('t', S.ev, ms)

        for vec, off in ((vecs["bq"], 0), (vecs["bk"], 1024), (vecs["bv"], 2048)):
            P.w('y', S.mm, S.mm.v)
            lv = yld(lambda s, vec=vec: s.dma_start(row[:], vec.ap().rearrange("(a n) -> a n", a=1)))
            P.w('t', S.xld, lv)
            for h in range(2):
                mv = P.do('t', lambda t, h=h: t.matmul(ppr[:], onesrow_f[:],
                                                       row[:, h * 512:(h + 1) * 512],
                                                       start=True, stop=True), inc=S.mm)
                P.w('v', S.mm, mv)
                evv = P.do('v', lambda v, h=h, off=off: v.tensor_copy(
                    bqkv_bc[:, off + h * 512: off + (h + 1) * 512], ppr[:]), inc=S.ev)
                P.w('t', S.ev, evv)
        P.w('y', S.mm, S.mm.v)
        lv = yld(lambda s: s.dma_start(row[:, 0:64], gbias.ap().rearrange("(a n) -> a n", a=1)))
        P.w('t', S.xld, lv)
        mv = P.do('t', lambda t: t.matmul(ppr[:, 0:64], onesrow_f[:], row[:, 0:64],
                                          start=True, stop=True), inc=S.mm)
        P.w('v', S.mm, mv)
        evv = P.do('v', lambda v: v.tensor_copy(gbias_bc[:], ppr[:, 0:64]), inc=S.ev)
        P.w('t', S.ev, evv)

        fmv = [(vecs["bo"], fmv_t["bo"], 8), (vecs["b2"], fmv_t["b2"], 8),
               (vecs["g1"], fmv_t["g1"], 8), (vecs["be1"], fmv_t["be1"], 8),
               (vecs["g2"], fmv_t["g2"], 8), (vecs["be2"], fmv_t["be2"], 8),
               (b1, b1_fm, 32)]
        for vec, dst, c in fmv:
            P.w('y', S.mm, S.mm.v)
            lv = yld(lambda s, vec=vec, c=c: s.dma_start(
                vtile[0:c, :], vec.ap().rearrange("(c p) -> c p", p=128)))
            P.w('t', S.xld, lv)
            mv = P.do('t', lambda t, c=c: t.transpose(ppr2[:, 0:c], vtile[0:c, :],
                                                      cid32_sb[0:c, 0:c]), inc=S.mm)
            P.w('v', S.mm, mv)
            evv = P.do('v', lambda v, dst=dst, c=c: v.tensor_copy(dst[:, 0:c], ppr2[:, 0:c]),
                       inc=S.ev)
            P.w('t', S.ev, evv)

        P.w('t', S.wld, S.wld.v); P.w('v', S.wld, S.wld.v); P.w('s', S.wld, S.wld.v)
        P.w('v', S.xld, S.xld.v); P.w('s', S.xld, S.xld.v)
        soft_barrier()

    def emit_ln_tile(ps, src, state, *, eps_ap, dst_bf=None, keep_f32=False,
                     gains=None, dst_gain=None):
        n = 512
        stats_vals = []
        P.w('v', S.mm, state.get("bc_free", 0))
        for kd in range(KD):
            slot = (kd % 2) * 512
            if kd >= 2:
                P.w('v', S.mm, stats_vals[kd - 2])
            P.do('v', lambda v, kd=kd, slot=slot: v.tensor_copy(xbx[:, slot:slot + n], src(kd)))
            evv = P.do('v', lambda v, slot=slot: v.tensor_mul(
                xsqx[:, slot:slot + n], xbx[:, slot:slot + n], xbx[:, slot:slot + n]), inc=S.ev)
            P.w('t', S.ev, evv)
            if kd == 0:
                P.w('t', S.ev, state.get("row_read", 0))
            P.do('t', lambda t, kd=kd, slot=slot: t.matmul(
                ps.mean[0:1, :], onescol_bf[:, 0:1], xbx[:, slot:slot + n],
                start=(kd == 0), stop=(kd == KD - 1)))
            sv = P.do('t', lambda t, kd=kd, slot=slot: t.matmul(
                ps.sq[0:1, :], onescol_bf[:, 0:1], xsqx[:, slot:slot + n],
                start=(kd == 0), stop=(kd == KD - 1)), inc=S.mm)
            stats_vals.append(sv)
        mu = lnrow[0:1, 0:512]
        msq = lnrow[0:1, 512:1024]
        tmp = lnrow[0:1, 1024:1536]
        rstd = lnrow[0:1, 1536:2048]
        P.w('v', S.mm, stats_vals[-1])
        P.do('v', lambda v: v.tensor_scalar_mul(mu, ps.mean[0:1, :], 1.0 / D))
        P.do('v', lambda v: v.tensor_scalar_mul(msq, ps.sq[0:1, :], 1.0 / D))
        P.do('v', lambda v: v.tensor_mul(tmp, mu, mu))
        rv = P.do('v', lambda v: v.tensor_sub(tmp, msq, tmp), inc=S.ev)
        state["row_read"] = rv
        P.w('s', S.ev, rv)
        scv = P.do('s', lambda s: s.activation(msq, tmp, AF.Sqrt, bias=eps_ap[0:1, 0:1]), inc=S.sc)
        P.w('v', S.sc, scv)
        P.do('v', lambda v: v.reciprocal(rstd, msq))
        rv2 = P.do('v', lambda v: v.tensor_mul(mu, mu, rstd), inc=S.ev)
        P.w('t', S.ev, rv2)
        P.w('t', S.ev, state.get("norm_done", 0))
        P.do('t', lambda t: t.matmul(ps.rstd[:], onesrow_f[:], rstd, start=True, stop=True))
        bcv = P.do('t', lambda t: t.matmul(ps.mmul[:], onesrow_f[:], mu, start=True, stop=True),
                   inc=S.mm)
        state["bc_free"] = bcv
        P.w('v', S.mm, bcv)
        last = 0
        for kd in range(KD):
            a = astage[:, (kd % 2) * 512:(kd % 2) * 512 + n]
            P.do('v', lambda v, kd=kd, a=a: v.tensor_mul(a, src(kd), ps.rstd[:]))
            if keep_f32:
                P.do('v', lambda v, kd=kd, a=a: v.tensor_sub(src(kd), a, ps.mmul[:]))
                last = P.do('v', lambda v, kd=kd: v.tensor_copy(dst_bf(kd), src(kd)), inc=S.ev)
            else:
                P.do('v', lambda v, a=a: v.tensor_sub(a, a, ps.mmul[:]))
                if gains is not None:
                    gt, bt = gains
                    last = P.do('v', lambda v, kd=kd, a=a, gt=gt, bt=bt: v.tensor_scalar(
                        dst_gain(kd), a, gt[:, kd:kd + 1], bt[:, kd:kd + 1],
                        op0=ALU.mult, op1=ALU.add), inc=S.ev)
                else:
                    last = P.do('v', lambda v, kd=kd, a=a: v.tensor_copy(dst_bf(kd), a), inc=S.ev)
        state["norm_done"] = last
        return last

    qcarry = {"store_sem": None, "store_val": 0}

    def emit_ffn(src3, dst3, dbg_key=None):
        for q in range(NQ):
            with ExitStack() as qctx:
                en = qctx.enter_context
                ps = NS(mean=en(nc.psum_tensor(un("pmean"), [1, 512], F32)),
                        sq=en(nc.psum_tensor(un("psq"), [1, 512], F32)),
                        rstd=en(nc.psum_tensor(un("prstd"), [128, 512], F32)),
                        mmul=en(nc.psum_tensor(un("pmmul"), [128, 512], F32)),
                        sc=en(nc.psum_tensor(un("pscore"), [128, 256], F32)),
                        tp=en(nc.psum_tensor(un("ptpm"), [128, 128], F32)))
                state = {}
                mask_last_ev = 0
                for tt in range(NTT):
                    t0 = q * QTOK + tt * 512
                    P.w('y', S.mm, state.get("xstage_free_mm", 0))
                    P.w('y', S.ev, state.get("xstage_free_ev", 0))
                    if qcarry["store_sem"] is not None:
                        P.w('y', qcarry["store_sem"], qcarry["store_val"])
                        qcarry["store_sem"] = None
                    lv = P.do('y', lambda s, t0=t0: s.dma_start(
                        xstage.ap().rearrange("p (k t) -> p k t", k=KD),
                        src3[:, :, t0:t0 + 512]), inc=S.xld, n=16)
                    P.w('v', S.xld, lv)
                    xln_done = emit_ln_tile(
                        ps, lambda kd: xstage[:, kd * 512:(kd + 1) * 512], state,
                        eps_ap=eps5, keep_f32=True,
                        dst_bf=lambda kd, tt=tt: xln[:, kd * 1024 + tt * 512: kd * 1024 + (tt + 1) * 512])
                    P.w('t', S.ev, xln_done)
                    P.w('t', S.sc, state.get("score_evac", 0))
                    gmv = 0
                    for sc4 in range(4):
                        for kd in range(KD):
                            gmv = P.do('t', lambda t, sc4=sc4, kd=kd: t.matmul(
                                ps.sc[:, sc4 * 64:(sc4 + 1) * 64],
                                xstage[:, kd * 512 + sc4 * 128: kd * 512 + (sc4 + 1) * 128],
                                gw_sb[:, kd * 64:(kd + 1) * 64],
                                start=(kd == 0), stop=(kd == KD - 1)),
                                inc=(S.mm if kd == KD - 1 else None))
                    state["xstage_free_mm"] = gmv
                    state["xstage_free_ev"] = xln_done
                    P.w('s', S.mm, gmv)
                    scv = P.do('s', lambda s, tt=tt: s.activation(
                        scores_tm[:, tt * 256:(tt + 1) * 256], ps.sc[:], AF.Sigmoid), inc=S.sc)
                    state["score_evac"] = scv
                    if q == 0 and "scraw" in tap_t:
                        P.w('y', S.sc, scv)
                        P.do('y', lambda s, tt=tt: s.dma_start(
                            tap_t["scraw"].ap()[:, tt * 256:(tt + 1) * 256],
                            scores_tm[:, tt * 256:(tt + 1) * 256]), inc=S.out, n=16)
                    P.w('v', S.sc, scv)
                    for sc4 in range(4):
                        sco = scores_tm[:, tt * 256 + sc4 * 64: tt * 256 + (sc4 + 1) * 64]
                        P.do('v', lambda v, sco=sco: v.tensor_add(tk_sel[:], sco, gbias_bc[:]))
                        m8v = P.do('v', lambda v: v.max(out=tk_m8[:], in_=tk_sel[:]), inc=S.ev)
                        P.w('v', S.ev, m8v)
                        P.do('v', lambda v: v.tensor_scalar(tk_sel[:], tk_sel[:], tk_m8[:, 5:6],
                                                            None, op0=ALU.is_ge))
                        P.do('v', lambda v, sco=sco: v.tensor_mul(tk_sel[:], tk_sel[:], sco))
                        rsv = P.do('v', lambda v: v.reduce_sum(tk_row[:, 0:1], tk_sel[:], axis=AXX),
                                   inc=S.ev)
                        P.w('v', S.ev, rsv)
                        rcv = P.do('v', lambda v: v.reciprocal(tk_row[:, 2:3], tk_row[:, 0:1]),
                                   inc=S.ev)
                        P.w('v', S.ev, rcv)
                        mask_last_ev = P.do('v', lambda v, sco=sco: v.tensor_scalar(
                            sco, tk_sel[:], tk_row[:, 2:3], 1.25, op0=ALU.mult, op1=ALU.mult),
                            inc=S.ev)
                st_vals = []
                P.w('t', S.ev, mask_last_ev)
                for c in range(8):
                    P.w('t', S.ev, state.get("tp_free", 0))
                    mv = P.do('t', lambda t, c=c: t.transpose(
                        ps.tp[0:64, :], scores_tm[:, c * 64:(c + 1) * 64], cid32_sb[:]), inc=S.mm)
                    P.w('v', S.mm, mv)
                    if c >= 2:
                        P.w('v', S.xst, st_vals[c - 2])
                    evv = P.do('v', lambda v, c=c: v.tensor_copy(
                        maskfm_ch[:, (c % 2) * 128:(c % 2 + 1) * 128], ps.tp[0:64, :]), inc=S.ev)
                    state["tp_free"] = evv
                    P.w('y', S.ev, evv)
                    stv = P.do('y', lambda s, c=c, q=q: s.dma_start(
                        maskT.ap()[:, q * QTOK + c * 128: q * QTOK + (c + 1) * 128],
                        maskfm_ch[:, (c % 2) * 128:(c % 2 + 1) * 128]), inc=S.xst, n=16)
                    st_vals.append(stv)
                mask_store_done = S.xst.v
                if q == 0 and "xln" in tap_t:
                    P.w('y', S.ev, S.ev.v)
                    P.do('y', lambda s: s.dma_start(tap_t["xln"].ap(), xln[:]), inc=S.out, n=16)
                    P.w('t', S.out, S.out.v)
                    P.w('v', S.out, S.out.v)
                    P.w('s', S.out, S.out.v)
                    P.w('g', S.out, S.out.v)
                if q == 0 and "scores" in tap_t:
                    P.w('y', S.ev, S.ev.v)
                    P.do('y', lambda s: s.dma_start(tap_t["scores"].ap(), scores_tm[:]), inc=S.out, n=16)
                    P.w('t', S.out, S.out.v)
                    P.w('v', S.out, S.out.v)
                    P.w('s', S.out, S.out.v)
                    P.w('g', S.out, S.out.v)
                soft_barrier()

            with ExitStack() as ectx:
                en = ectx.enter_context
                pG = [en(nc.psum_tensor(un(f"pG{i}"), [128, 512], F32)) for i in (0, 1)]
                pU = [en(nc.psum_tensor(un(f"pU{i}"), [128, 512], F32)) for i in (0, 1)]
                pD = [en(nc.psum_tensor(un(f"pD{i}"), [128, 1024], F32)) for i in (0, 1)]
                glast = {}
                estate = {"hfree": {0: 0, 1: 0}}
                d_evs = {0: 0, 1: 0}

                P.w('g', S.wfr, estate.get("wprev", S.wfr.v))
                P.do('g', lambda g: g.dma_start(shg_sb.ap().rearrange("p (k h) -> p k h", k=KD), shg_r), inc=S.wld, n=16)
                P.do('g', lambda g: g.dma_start(shu_sb.ap().rearrange("p (k h) -> p k h", k=KD), shu_r), inc=S.wld, n=16)
                P.do('g', lambda g: g.dma_start(shd_sb.ap().rearrange("p (k h) -> p k h", k=KSH), shd_r), inc=S.wld, n=16)
                glast["sh"] = S.wld.v

                def stream_expert(e, wait_vals):
                    par = e % 2
                    for wv_ in wait_vals:
                        P.w('g', S.wfr, wv_)
                    P.do('g', lambda g, e=e, par=par: g.dma_start(
                        gbuf[par].ap().rearrange("p (k h) -> p k h", k=KD), allg_r[e]),
                        inc=S.wld, n=16)
                    P.do('g', lambda g, e=e, par=par: g.dma_start(
                        ubuf[par].ap().rearrange("p (k h) -> p k h", k=KD), allu_r[e]),
                        inc=S.wld, n=16)
                    P.do('g', lambda g, e=e, par=par: g.dma_start(
                        dbuf[par].ap().rearrange("p (k h) -> p k h", k=KH), alld_r[e]),
                        inc=S.wld, n=16)
                    P.w('g', S.mr, estate.get(f"mrfree{par}", 0))
                    P.w('g', S.xst, mask_store_done)
                    P.do('g', lambda g, e=e, par=par, q=q: g.dma_start(
                        mrow[par][:], maskT.ap()[e:e + 1, q * QTOK:(q + 1) * QTOK]),
                        inc=S.wld, n=16)
                    glast[e] = S.wld.v

                P.w('t', S.wld, glast["sh"])
                shgu_val = None
                for tt in range(NTT):
                    for hc in range(KSH):
                        bp = hc % 2
                        P.w('t', S.sc, estate.get(f"silu_free{bp}", 0))
                        P.w('t', S.sc, estate.get(f"ush_free{bp}", 0))
                        for kd in range(KD):
                            P.do('t', lambda t, tt=tt, hc=hc, kd=kd, bp=bp: t.matmul(
                                pG[bp][:],
                                shg_sb[:, kd * 1536 + hc * 128: kd * 1536 + (hc + 1) * 128],
                                xln[:, kd * 1024 + tt * 512: kd * 1024 + (tt + 1) * 512],
                                start=(kd == 0), stop=(kd == KD - 1)))
                        guv = None
                        for kd in range(KD):
                            guv = P.do('t', lambda t, tt=tt, hc=hc, kd=kd, bp=bp: t.matmul(
                                pU[bp][:],
                                shu_sb[:, kd * 1536 + hc * 128: kd * 1536 + (hc + 1) * 128],
                                xln[:, kd * 1024 + tt * 512: kd * 1024 + (tt + 1) * 512],
                                start=(kd == 0), stop=(kd == KD - 1)),
                                inc=(S.mm if kd == KD - 1 else None))
                        if tt == NTT - 1 and hc == KSH - 1:
                            shgu_val = P.do('t', lambda t: t.nop(), inc=S.wfr)
                        P.w('s', S.mm, guv)
                        sgv = P.do('s', lambda s, bp=bp: s.activation(silug[bp][:], pG[bp][:],
                                                                      AF.Silu), inc=S.sc)
                        estate[f"silu_free{bp}"] = sgv
                        uev = P.do('s', lambda s, bp=bp: s.activation(umb[bp][:], pU[bp][:],
                                                                      AF.Copy, scale=0.5), inc=S.sc)
                        estate[f"ush_free{bp}"] = uev
                        P.w('v', S.sc, uev)
                        if hc == 0:
                            P.w('v', S.mm, estate["hfree"][tt % 2])
                        hv = P.do('v', lambda v, bp=bp, tt=tt, hc=hc: v.tensor_mul(
                            hbuf[tt % 2][:, (hc % KH) * 512:((hc % KH) + 1) * 512],
                            silug[bp][:], umb[bp][:]), inc=S.ev)
                        if hc == KH - 1 or hc == KSH - 1:
                            part = 0 if hc == KH - 1 else 1
                            first = (part == 0)
                            dm_last = None
                            for grp in range(4):
                                gpar = grp % 2
                                P.w('t', S.ev, hv)
                                P.w('t', S.ev, d_evs[gpar])
                                for sub in range(2):
                                    mo = grp * 2 + sub
                                    for kh in range(KH):
                                        khg = part * KH + kh
                                        dm_last = P.do('t', lambda t, tt=tt, gpar=gpar, sub=sub, mo=mo, khg=khg, kh=kh: t.matmul(
                                            pD[gpar][:, sub * 512:(sub + 1) * 512],
                                            shd_sb[:, khg * 1024 + mo * 128: khg * 1024 + (mo + 1) * 128],
                                            hbuf[tt % 2][:, kh * 512:(kh + 1) * 512],
                                            start=(kh == 0), stop=(kh == KH - 1)),
                                            inc=(S.mm if (sub == 1 and kh == KH - 1) else None))
                                P.w('v', S.mm, dm_last)
                                for sub in range(2):
                                    mo = grp * 2 + sub
                                    ysl = yacc[:, mo * 1024 + tt * 512: mo * 1024 + (tt + 1) * 512]
                                    if first:
                                        ev_ = P.do('v', lambda v, gpar=gpar, sub=sub, ysl=ysl: v.tensor_copy(
                                            ysl, pD[gpar][:, sub * 512:(sub + 1) * 512]), inc=S.ev)
                                    else:
                                        ev_ = P.do('v', lambda v, gpar=gpar, sub=sub, ysl=ysl: v.tensor_add(
                                            ysl, pD[gpar][:, sub * 512:(sub + 1) * 512], ysl), inc=S.ev)
                                d_evs[gpar] = ev_
                            estate["hfree"][tt % 2] = dm_last
                shdown_val = P.do('t', lambda t: t.nop(), inc=S.wfr)

                stream_expert(0, [shgu_val])
                stream_expert(1, [shgu_val, shdown_val])

                for e in range(E):
                    par = e % 2
                    P.w('t', S.wld, glast[e])
                    mb_evs = []
                    for tt in range(NTT):
                        P.w('t', S.sc, estate.get(f"silu_free{tt}", 0))
                        P.w('t', S.ev, estate.get(f"um_free{tt}", 0))
                        bv_ = P.do('t', lambda t, tt=tt, par=par: t.matmul(
                            pG[tt][:], onesrow_bf[:], mrow[par][:, tt * 512:(tt + 1) * 512],
                            start=True, stop=True), inc=S.mm)
                        P.w('s', S.mm, bv_)
                        sev = P.do('s', lambda s, tt=tt, par=par: s.activation(
                            mbcast[par][:, tt * 512:(tt + 1) * 512], pG[tt][:], AF.Copy), inc=S.sc)
                        estate[f"silu_free{tt}"] = sev
                        mb_evs.append(sev)
                    mrv = P.do('t', lambda t: t.nop(), inc=S.mr)
                    estate[f"mrfree{par}"] = mrv
                    for tt in range(NTT):
                        for hc in range(KH):
                            bp = hc % 2
                            P.w('t', S.sc, estate.get(f"silu_free{bp}", 0))
                            P.w('t', S.ev, estate.get(f"um_free{bp}", 0))
                            for kd in range(KD):
                                P.do('t', lambda t, tt=tt, hc=hc, kd=kd, bp=bp, par=par: t.matmul(
                                    pG[bp][:],
                                    gbuf[par][:, kd * 768 + hc * 128: kd * 768 + (hc + 1) * 128],
                                    xln[:, kd * 1024 + tt * 512: kd * 1024 + (tt + 1) * 512],
                                    start=(kd == 0), stop=(kd == KD - 1)))
                            guv = None
                            for kd in range(KD):
                                guv = P.do('t', lambda t, tt=tt, hc=hc, kd=kd, bp=bp, par=par: t.matmul(
                                    pU[bp][:],
                                    ubuf[par][:, kd * 768 + hc * 128: kd * 768 + (hc + 1) * 128],
                                    xln[:, kd * 1024 + tt * 512: kd * 1024 + (tt + 1) * 512],
                                    start=(kd == 0), stop=(kd == KD - 1)),
                                    inc=(S.mm if kd == KD - 1 else None))
                            P.w('s', S.mm, guv)
                            sgv = P.do('s', lambda s, bp=bp: s.activation(
                                silug[bp][:], pG[bp][:], AF.Silu), inc=S.sc)
                            estate[f"silu_free{bp}"] = sgv
                            P.w('v', S.mm, guv)
                            P.w('v', S.sc, mb_evs[tt])
                            umv = P.do('v', lambda v, bp=bp, tt=tt, par=par: v.tensor_mul(
                                umb[bp][:], pU[bp][:], mbcast[par][:, tt * 512:(tt + 1) * 512]),
                                inc=S.ev)
                            estate[f"um_free{bp}"] = umv
                            P.w('v', S.sc, sgv)
                            if hc == 0:
                                P.w('v', S.mm, estate["hfree"][tt % 2])
                            hv = P.do('v', lambda v, bp=bp, tt=tt, hc=hc: v.tensor_mul(
                                hbuf[tt % 2][:, hc * 512:(hc + 1) * 512], silug[bp][:], umb[bp][:]),
                                inc=S.ev)
                        dm_last = None
                        P.w('t', S.ev, hv)
                        for grp in range(4):
                            gpar = grp % 2
                            P.w('t', S.ev, d_evs[gpar])
                            for sub in range(2):
                                mo = grp * 2 + sub
                                for kh in range(KH):
                                    dm_last = P.do('t', lambda t, tt=tt, gpar=gpar, sub=sub, mo=mo, kh=kh, par=par: t.matmul(
                                        pD[gpar][:, sub * 512:(sub + 1) * 512],
                                        dbuf[par][:, kh * 1024 + mo * 128: kh * 1024 + (mo + 1) * 128],
                                        hbuf[tt % 2][:, kh * 512:(kh + 1) * 512],
                                        start=(kh == 0), stop=(kh == KH - 1)),
                                        inc=(S.mm if (sub == 1 and kh == KH - 1) else None))
                            P.w('v', S.mm, dm_last)
                            for sub in range(2):
                                mo = grp * 2 + sub
                                ysl = yacc[:, mo * 1024 + tt * 512: mo * 1024 + (tt + 1) * 512]
                                ev_ = P.do('v', lambda v, gpar=gpar, sub=sub, ysl=ysl: v.tensor_add(
                                    ysl, pD[gpar][:, sub * 512:(sub + 1) * 512], ysl), inc=S.ev)
                            d_evs[gpar] = ev_
                        estate["hfree"][tt % 2] = dm_last
                    fv = P.do('t', lambda t: t.nop(), inc=S.wfr)
                    estate[f"wfree{par}"] = fv
                    if e + 2 < E:
                        stream_expert(e + 2, [fv])
                estate["wprev"] = S.wfr.v
                y_done = max(d_evs.values())
                if q == 0 and "yacc" in tap_t:
                    P.w('y', S.ev, S.ev.v)
                    P.do('y', lambda s: s.dma_start(tap_t["yacc"].ap(), yacc[:]), inc=S.out, n=16)
                    P.w('t', S.out, S.out.v)
                    P.w('v', S.out, S.out.v)
                    P.w('s', S.out, S.out.v)
                    P.w('g', S.out, S.out.v)
                if q == 0 and "hbuf" in tap_t:
                    P.w('y', S.ev, S.ev.v)
                    P.do('y', lambda s: s.dma_start(tap_t["hbuf"].ap(), hbuf[1][:]), inc=S.out, n=16)
                    P.w('t', S.out, S.out.v)
                    P.w('v', S.out, S.out.v)
                    P.w('s', S.out, S.out.v)
                    P.w('g', S.out, S.out.v)
                if q == 0 and "mb" in tap_t:
                    P.w('y', S.ev, S.ev.v)
                    P.do('y', lambda s: s.dma_start(tap_t["mb"].ap(), mbcast[1][:]), inc=S.out, n=16)
                    P.w('t', S.out, S.out.v)
                    P.w('v', S.out, S.out.v)
                    P.w('s', S.out, S.out.v)
                    P.w('g', S.out, S.out.v)
                soft_barrier()

            jfree = 0
            for tt in range(NTT):
                t0 = q * QTOK + tt * 512
                P.w('y', S.ev, jfree)
                lv = P.do('y', lambda s, t0=t0: s.dma_start(
                    xstage.ap().rearrange("p (k t) -> p k t", k=KD),
                    src3[:, :, t0:t0 + 512]), inc=S.xld, n=16)
                P.w('v', S.xld, lv)
                P.w('v', S.ev, y_done)
                jv = None
                for kd in range(KD):
                    jv = P.do('v', lambda v, kd=kd, tt=tt: v.tensor_add(
                        xstage[:, kd * 512:(kd + 1) * 512],
                        yacc[:, kd * 1024 + tt * 512: kd * 1024 + (tt + 1) * 512],
                        xstage[:, kd * 512:(kd + 1) * 512]),
                        inc=(S.ev if kd == KD - 1 else None))
                jfree = jv
                P.w('y', S.ev, jv)
                osem = S.out if dst3 is out3 else S.xst
                P.do('y', lambda s, t0=t0: s.dma_start(
                    dst3[:, :, t0:t0 + 512],
                    xstage.ap().rearrange("p (k t) -> p k t", k=KD)), inc=osem, n=16)
                if dbg_key is not None:
                    P.do('y', lambda s, t0=t0, k=dbg_key: s.dma_start(
                        dbg3[k][:, :, t0:t0 + 512],
                        xstage.ap().rearrange("p (k t) -> p k t", k=KD)), inc=S.out, n=16)
                qcarry["store_sem"], qcarry["store_val"] = osem, osem.v
            soft_barrier()

    def emit_attn(src3, dst3, dbg_key=None):
        astate = {}

        def eln(bset, bias_off, dst_fn):
            for hh in range(2):
                P.do('v', lambda v, hh=hh, bias_off=bias_off: v.tensor_add(
                    kf[:, hh * 512:(hh + 1) * 512], bset[hh][:],
                    bqkv_bc[:, bias_off + hh * 512: bias_off + (hh + 1) * 512]))
            P.do('v', lambda v: v.tensor_relu(rbuf[:], kf[:]))
            mv_ = P.do('v', lambda v: v.tensor_sub(exbuf[:], kf[:], rbuf[:]), inc=S.ev)
            P.w('s', S.ev, mv_)
            scv = P.do('s', lambda s: s.activation(exbuf[:], exbuf[:], AF.Exp), inc=S.sc)
            P.w('v', S.sc, scv)
            P.do('v', lambda v: v.scalar_tensor_tensor(kf[:], exbuf[:], -1.0, rbuf[:],
                                                       op0=ALU.add, op1=ALU.add))
            P.do('v', lambda v: v.tensor_mul(rbuf[:], kf[:], kf[:]))
            nv = None
            for h in range(NH):
                nv = P.do('v', lambda v, h=h: v.reduce_sum(
                    nrmA[:, h:h + 1], rbuf[:, h * 64:(h + 1) * 64], axis=AXX),
                    inc=(S.ev if h == NH - 1 else None))
            P.w('s', S.ev, nv)
            sqv = P.do('s', lambda s: s.activation(nrmB[:], nrmA[:], AF.Sqrt), inc=S.sc)
            P.w('v', S.sc, sqv)
            rv = P.do('v', lambda v: v.reciprocal(nrmA[:], nrmB[:]), inc=S.ev)
            P.w('v', S.ev, rv)
            last = None
            for h in range(NH):
                last = P.do('v', lambda v, h=h: v.tensor_scalar_mul(
                    dst_fn(h), kf[:, h * 64:(h + 1) * 64], nrmA[:, h:h + 1]),
                    inc=(S.ev if h == NH - 1 else None))
            return last

        for s_i in range(2):
            base = s_i * SEQ
            with ExitStack() as actx:
                en = actx.enter_context
                ps = NS(mean=en(nc.psum_tensor(un("amean"), [1, 512], F32)),
                        sq=en(nc.psum_tensor(un("asq"), [1, 512], F32)),
                        rstd=en(nc.psum_tensor(un("arstd"), [128, 512], F32)),
                        mmul=en(nc.psum_tensor(un("ammul"), [128, 512], F32)))
                st = {}
                for tt in range(4):
                    t0 = base + tt * 512
                    P.w('y', S.ev, st.get("norm_done", 0))
                    lv = P.do('y', lambda s, t0=t0: s.dma_start(
                        xstage.ap().rearrange("p (k t) -> p k t", k=KD),
                        src3[:, :, t0:t0 + 512]), inc=S.xld, n=16)
                    P.w('v', S.xld, lv)
                    emit_ln_tile(ps, lambda kd: xstage[:, kd * 512:(kd + 1) * 512], st,
                                 eps_ap=eps5, keep_f32=False,
                                 dst_bf=lambda kd, tt=tt: x1ln[:, kd * 2048 + tt * 512:
                                                               kd * 2048 + (tt + 1) * 512])
                soft_barrier()
                if s_i == tseq and "aln" in tap_t:
                    P.w('y', S.ev, S.ev.v)
                    P.w('y', S.sc, S.sc.v)
                    P.do('y', lambda s: s.dma_start(tap_t["aln"].ap(), x1ln[:]), inc=S.out, n=16)
                    P.w('t', S.out, S.out.v)
                    P.w('v', S.out, S.out.v)
                    P.w('s', S.out, S.out.v)
                    P.w('g', S.out, S.out.v)

            P.w('g', S.wfr, S.wfr.v)
            P.do('g', lambda g: g.dma_start(wslot[0].ap().rearrange("p (k n) -> p k n", k=KD), wr["k"]), inc=S.wld, n=16)
            wk_ld = S.wld.v
            P.do('g', lambda g: g.dma_start(wslot[1].ap().rearrange("p (k n) -> p k n", k=KD), wr["v"]), inc=S.wld, n=16)
            wv_ld = S.wld.v
            with ExitStack() as actx:
                en = actx.enter_context
                pP = [en(nc.psum_tensor(un(f"pP{i}"), [128, 512], F32)) for i in range(4)]
                ev_done = {}
                for c in range(16):
                    for w_i in range(2):
                        P.w('t', S.wld, wk_ld if w_i == 0 else wv_ld)
                        bset = [pP[2 * w_i], pP[2 * w_i + 1]]
                        P.w('t', S.ev, ev_done.get((c - 1, w_i), 0))
                        prv = None
                        for hh in range(2):
                            for kd in range(KD):
                                prv = P.do('t', lambda t, c=c, hh=hh, kd=kd, w_i=w_i, bset=bset: t.matmul(
                                    bset[hh][:],
                                    x1ln[:, kd * 2048 + c * 128: kd * 2048 + (c + 1) * 128],
                                    wslot[w_i][:, kd * 1024 + hh * 512: kd * 1024 + (hh + 1) * 512],
                                    start=(kd == 0), stop=(kd == KD - 1)),
                                    inc=(S.mm if (hh == 1 and kd == KD - 1) else None))
                        P.w('v', S.mm, prv)
                        if w_i == 0:
                            d = eln(bset, 1024,
                                    lambda h, c=c: kn_tm[:, c * 1024 + h * 64: c * 1024 + (h + 1) * 64])
                        else:
                            d = None
                            for hh in range(2):
                                d = P.do('v', lambda v, c=c, hh=hh, bset=bset: v.tensor_add(
                                    v_tm[:, c * 1024 + hh * 512: c * 1024 + (hh + 1) * 512],
                                    bset[hh][:], bqkv_bc[:, 2048 + hh * 512: 2048 + (hh + 1) * 512]),
                                    inc=(S.ev if hh == 1 else None))
                        ev_done[(c, w_i)] = d
                wfv = P.do('t', lambda t: t.nop(), inc=S.wfr)
                soft_barrier()
                if s_i == tseq and "akn" in tap_t:
                    P.w('y', S.ev, S.ev.v)
                    P.w('y', S.sc, S.sc.v)
                    P.do('y', lambda s: s.dma_start(tap_t["akn"].ap(), kn_tm[:]), inc=S.out, n=16)
                    P.w('t', S.out, S.out.v)
                    P.w('v', S.out, S.out.v)
                    P.w('s', S.out, S.out.v)
                    P.w('g', S.out, S.out.v)
                if s_i == tseq and "avn" in tap_t:
                    P.w('y', S.ev, S.ev.v)
                    P.w('y', S.sc, S.sc.v)
                    P.do('y', lambda s: s.dma_start(tap_t["avn"].ap(), v_tm[:]), inc=S.out, n=16)
                    P.w('t', S.out, S.out.v)
                    P.w('v', S.out, S.out.v)
                    P.w('s', S.out, S.out.v)
                    P.w('g', S.out, S.out.v)

            with ExitStack() as actx:
                en = actx.enter_context
                pKV = [en(nc.psum_tensor(un(f"pKV{i}"), [128, 64], F32)) for i in range(8)]
                kvv = None
                for c in range(16):
                    for h in range(NH):
                        bank = pKV[h // 2]
                        rh = (h % 2) * 64
                        kvv = P.do('t', lambda t, c=c, h=h, bank=bank, rh=rh: t.matmul(
                            bank[rh:rh + 64, :],
                            kn_tm[:, c * 1024 + h * 64: c * 1024 + (h + 1) * 64],
                            v_tm[:, c * 1024 + h * 64: c * 1024 + (h + 1) * 64],
                            start=(c == 0), stop=(c == 15)),
                            inc=(S.mm if (c == 15 and h == NH - 1) else None))
                P.w('s', S.mm, kvv)
                kve = None
                for h in range(NH):
                    bank = pKV[h // 2]
                    rh = (h % 2) * 64
                    kve = P.do('s', lambda s, h=h, bank=bank, rh=rh: s.activation(
                        kv_sb[rh:rh + 64, (h // 2) * 64:(h // 2 + 1) * 64],
                        bank[rh:rh + 64, :], AF.Copy, scale=0.125),
                        inc=(S.sc if h == NH - 1 else None))
                soft_barrier()
                if s_i == tseq and "akv" in tap_t:
                    P.w('y', S.ev, S.ev.v)
                    P.w('y', S.sc, S.sc.v)
                    P.do('y', lambda s: s.dma_start(tap_t["akv"].ap(), kv_sb[:]), inc=S.out, n=16)
                    P.w('t', S.out, S.out.v)
                    P.w('v', S.out, S.out.v)
                    P.w('s', S.out, S.out.v)
                    P.w('g', S.out, S.out.v)

            P.w('g', S.wfr, S.wfr.v)
            P.do('g', lambda g: g.dma_start(wslot[0].ap().rearrange("p (k n) -> p k n", k=KD), wr["q"]), inc=S.wld, n=16)
            wq_ld = S.wld.v
            P.do('g', lambda g: g.dma_start(wslot[1].ap().rearrange("p (k n) -> p k n", k=KD), wr["o"]), inc=S.wld, n=16)
            wo_ld = S.wld.v
            for hs in range(2):
                with ExitStack() as actx:
                    en = actx.enter_context
                    pP = [en(nc.psum_tensor(un(f"pQ{i}"), [128, 512], F32)) for i in range(4)]
                    pT = en(nc.psum_tensor(un("pT"), [128, 128], BF16))
                    ev_done = {}
                    tp_done = {}
                    for c8 in range(8):
                        c = hs * 8 + c8
                        P.w('t', S.wld, wq_ld)
                        bset = [pP[2 * (c8 % 2)], pP[2 * (c8 % 2) + 1]]
                        P.w('t', S.ev, ev_done.get(c8 - 2, 0))
                        prv = None
                        for hh in range(2):
                            for kd in range(KD):
                                prv = P.do('t', lambda t, c=c, hh=hh, kd=kd, bset=bset: t.matmul(
                                    bset[hh][:],
                                    x1ln[:, kd * 2048 + c * 128: kd * 2048 + (c + 1) * 128],
                                    wslot[0][:, kd * 1024 + hh * 512: kd * 1024 + (hh + 1) * 512],
                                    start=(kd == 0), stop=(kd == KD - 1)),
                                    inc=(S.mm if (hh == 1 and kd == KD - 1) else None))
                        P.w('v', S.mm, prv)
                        d = eln(bset, 0, lambda h: qn_ch[:, h * 64:(h + 1) * 64])
                        ev_done[c8] = d
                        for kd in range(KD):
                            if kd == 0:
                                P.w('t', S.ev, d)
                            prev_tp = tp_done.get((c8, kd - 1)) if kd > 0 else tp_done.get((c8 - 1, KD - 1))
                            P.w('t', S.ev, prev_tp or 0)
                            mv_ = P.do('t', lambda t, kd=kd: t.transpose(
                                pT[:], qn_ch[:, kd * 128:(kd + 1) * 128], cid16_sb[:]), inc=S.mm)
                            P.w('v', S.mm, mv_)
                            tv = P.do('v', lambda v, kd=kd, c8=c8: v.tensor_copy(
                                qn_fm[:, kd * 1024 + c8 * 128: kd * 1024 + (c8 + 1) * 128], pT[:]),
                                inc=S.ev)
                            tp_done[(c8, kd)] = tv
                    if hs == 1:
                        astate["wq_free"] = P.do('t', lambda t: t.nop(), inc=S.wfr)
                    soft_barrier()
                with ExitStack() as actx:
                    en = actx.enter_context
                    pC = [en(nc.psum_tensor(un(f"pC{i}"), [128, 512], F32)) for i in range(4)]
                    last_ev = {}
                    for tt in range(2):
                        for h in range(NH):
                            slot = (h // 2) % 4
                            rh = (h % 2) * 64
                            P.w('t', S.ev, last_ev.get((slot, rh), 0))
                            cv = P.do('t', lambda t, tt=tt, h=h, slot=slot, rh=rh: t.matmul(
                                pC[slot][rh:rh + 64, :],
                                kv_sb[rh:rh + 64, (h // 2) * 64:(h // 2 + 1) * 64],
                                qn_fm[rh:rh + 64, (h // 2) * 1024 + tt * 512:
                                      (h // 2) * 1024 + (tt + 1) * 512],
                                start=True, stop=True), inc=S.mm)
                            P.w('v', S.mm, cv)
                            ev_ = P.do('v', lambda v, tt=tt, h=h, slot=slot, rh=rh, hs=hs: v.tensor_copy(
                                ctx_fm[rh:rh + 64,
                                       (h // 2) * 2048 + hs * 1024 + tt * 512:
                                       (h // 2) * 2048 + hs * 1024 + (tt + 1) * 512]
                                if rh else
                                ctx_fm[0:64,
                                       (h // 2) * 2048 + hs * 1024 + tt * 512:
                                       (h // 2) * 2048 + hs * 1024 + (tt + 1) * 512],
                                pC[slot][rh:rh + 64, :]), inc=S.ev)
                            last_ev[(slot, rh)] = ev_
                    soft_barrier()
                if s_i == tseq and hs == 0 and "aqn" in tap_t:
                    P.w('y', S.ev, S.ev.v)
                    P.do('y', lambda s: s.dma_start(tap_t["aqn"].ap(), qn_fm[:]), inc=S.out, n=16)
                    P.w('t', S.out, S.out.v)
                    P.w('v', S.out, S.out.v)
                    P.w('s', S.out, S.out.v)
                    P.w('g', S.out, S.out.v)

            with ExitStack() as actx:
                en = actx.enter_context
                pW = [en(nc.psum_tensor(un(f"pW{i}"), [128, 512], F32)) for i in range(2)]
                ps = NS(mean=en(nc.psum_tensor(un("wmean"), [1, 512], F32)),
                        sq=en(nc.psum_tensor(un("wsq"), [1, 512], F32)),
                        rstd=en(nc.psum_tensor(un("wrstd"), [128, 512], F32)),
                        mmul=en(nc.psum_tensor(un("wmmul"), [128, 512], F32)))
                st = {}
                for tt in range(4):
                    P.w('t', S.wld, wo_ld)
                    for mo in range(KD):
                        bp = mo % 2
                        P.w('t', S.ev, st.get(f"wo_ev{bp}", 0))
                        wvv = None
                        for kd in range(KD):
                            wvv = P.do('t', lambda t, tt=tt, mo=mo, kd=kd, bp=bp: t.matmul(
                                pW[bp][:],
                                wslot[1][:, kd * 1024 + mo * 128: kd * 1024 + (mo + 1) * 128],
                                ctx_fm[:, kd * 2048 + tt * 512: kd * 2048 + (tt + 1) * 512],
                                start=(kd == 0), stop=(kd == KD - 1)),
                                inc=(S.mm if kd == KD - 1 else None))
                        P.w('v', S.mm, wvv)
                        evv = P.do('v', lambda v, mo=mo, tt=tt, bp=bp: v.scalar_tensor_tensor(
                            pre1[:, mo * 512:(mo + 1) * 512], pW[bp][:],
                            fmv_t["bo"][:, mo:mo + 1],
                            x1ln[:, mo * 2048 + tt * 512: mo * 2048 + (tt + 1) * 512],
                            op0=ALU.add, op1=ALU.add), inc=S.ev)
                        st[f"wo_ev{bp}"] = evv
                    emit_ln_tile(ps, lambda kd: pre1[:, kd * 512:(kd + 1) * 512], st,
                                 eps_ap=eps12, keep_f32=False,
                                 gains=(fmv_t["g1"], fmv_t["be1"]),
                                 dst_gain=lambda kd, tt=tt: x1ln[:, kd * 2048 + tt * 512:
                                                                 kd * 2048 + (tt + 1) * 512])
                astate["wo_free"] = P.do('t', lambda t: t.nop(), inc=S.wfr)
                astate["wo_done_ev"] = S.ev.v
                soft_barrier()
                if s_i == tseq and "actx" in tap_t:
                    P.w('y', S.ev, S.ev.v)
                    P.w('y', S.sc, S.sc.v)
                    P.do('y', lambda s: s.dma_start(tap_t["actx"].ap(), ctx_fm[:]), inc=S.out, n=16)
                    P.w('t', S.out, S.out.v)
                    P.w('v', S.out, S.out.v)
                    P.w('s', S.out, S.out.v)
                    P.w('g', S.out, S.out.v)
                if s_i == tseq and "amid" in tap_t:
                    P.w('y', S.ev, S.ev.v)
                    P.w('y', S.sc, S.sc.v)
                    P.do('y', lambda s: s.dma_start(tap_t["amid"].ap(), x1ln[:]), inc=S.out, n=16)
                    P.w('t', S.out, S.out.v)
                    P.w('v', S.out, S.out.v)
                    P.w('s', S.out, S.out.v)
                    P.w('g', S.out, S.out.v)

            for f_i in range(4):
                fbase = base + f_i * 512
                with ExitStack() as actx:
                    en = actx.enter_context
                    pF = [en(nc.psum_tensor(un(f"pF{i}"), [128, 512], F32)) for i in range(2)]
                    for mos in range(8):
                        wp = mos % 2
                        P.w('g', S.wfr, astate.get("wq_free", 0))
                        P.w('g', S.wfr, astate.get(f"w1free{wp}", 0))
                        P.do('g', lambda g, mos=mos, wp=wp: g.dma_start(
                            w1slot[wp].ap().rearrange("p (k n) -> p k n", n=512),
                            w1_r[:, :, mos * 512:(mos + 1) * 512]), inc=S.wld, n=16)
                        w1_ld = S.wld.v
                        P.w('t', S.wld, w1_ld)
                        for moc in range(4):
                            bp = moc % 2
                            P.w('t', S.sc, astate.get(f"ff_sc{bp}", 0))
                            fvv = None
                            for kd in range(KD):
                                fvv = P.do('t', lambda t, mos=mos, moc=moc, kd=kd, bp=bp, wp=wp, f_i=f_i: t.matmul(
                                    pF[bp][:],
                                    w1slot[wp][:, kd * 512 + moc * 128: kd * 512 + (moc + 1) * 128],
                                    x1ln[:, kd * 2048 + f_i * 512: kd * 2048 + (f_i + 1) * 512],
                                    start=(kd == 0), stop=(kd == KD - 1)),
                                    inc=(S.mm if kd == KD - 1 else None))
                            P.w('s', S.mm, fvv)
                            mo = mos * 4 + moc
                            scv = P.do('s', lambda s, mo=mo, bp=bp: s.activation(
                                hff[:, mo * 512:(mo + 1) * 512], pF[bp][:], AF.Gelu,
                                bias=b1_fm[:, mo:mo + 1]), inc=S.sc)
                            astate[f"ff_sc{bp}"] = scv
                        w1f = P.do('t', lambda t: t.nop(), inc=S.wfr)
                        astate[f"w1free{wp}"] = w1f
                    soft_barrier()
                if s_i == tseq and f_i == 0 and "ahff" in tap_t:
                    P.w('y', S.sc, S.sc.v)
                    P.do('y', lambda s: s.dma_start(tap_t["ahff"].ap(), hff[:]), inc=S.out, n=16)
                    P.w('t', S.out, S.out.v)
                    P.w('v', S.out, S.out.v)
                    P.w('s', S.out, S.out.v)
                    P.w('g', S.out, S.out.v)
                with ExitStack() as actx:
                    en = actx.enter_context
                    pM = [en(nc.psum_tensor(un(f"pM{i}"), [128, 512], F32)) for i in range(8)]
                    w2v = None
                    for khs in range(8):
                        wp = khs % 2
                        P.w('g', S.wfr, astate.get("wo_free", 0))
                        P.w('g', S.wfr, astate.get(f"w2free{wp}", 0))
                        P.do('g', lambda g, khs=khs, wp=wp: g.dma_start(
                            w2slot[wp].ap().rearrange("p (k n) -> p k n", n=1024),
                            w2_r[:, khs * 4:(khs + 1) * 4, :]), inc=S.wld, n=16)
                        w2_ld = S.wld.v
                        P.w('t', S.wld, w2_ld)
                        for k4 in range(4):
                            kh = khs * 4 + k4
                            for mo in range(KD):
                                w2v = P.do('t', lambda t, k4=k4, kh=kh, mo=mo, wp=wp: t.matmul(
                                    pM[mo][:],
                                    w2slot[wp][:, k4 * 1024 + mo * 128: k4 * 1024 + (mo + 1) * 128],
                                    hff[:, kh * 512:(kh + 1) * 512],
                                    start=(kh == 0), stop=(kh == KFF - 1)),
                                    inc=(S.mm if (kh == KFF - 1 and mo == KD - 1) else None))
                        w2f = P.do('t', lambda t: t.nop(), inc=S.wfr)
                        astate[f"w2free{wp}"] = w2f
                    P.w('v', S.mm, w2v)
                    for mo in range(KD):
                        P.do('v', lambda v, mo=mo, f_i=f_i: v.scalar_tensor_tensor(
                            pre2[:, mo * 512:(mo + 1) * 512], pM[mo][:],
                            fmv_t["b2"][:, mo:mo + 1],
                            x1ln[:, mo * 2048 + f_i * 512: mo * 2048 + (f_i + 1) * 512],
                            op0=ALU.add, op1=ALU.add), inc=(S.ev if mo == KD - 1 else None))
                    soft_barrier()
                    if s_i == tseq and f_i == 0 and "apre2" in tap_t:
                        P.w('y', S.ev, S.ev.v)
                        P.do('y', lambda s: s.dma_start(tap_t["apre2"].ap(), pre2[:]), inc=S.out, n=16)
                        P.w('t', S.out, S.out.v)
                        P.w('v', S.out, S.out.v)
                        P.w('s', S.out, S.out.v)
                        P.w('g', S.out, S.out.v)
                with ExitStack() as actx:
                    en = actx.enter_context
                    ps = NS(mean=en(nc.psum_tensor(un("fmean"), [1, 512], F32)),
                            sq=en(nc.psum_tensor(un("fsq"), [1, 512], F32)),
                            rstd=en(nc.psum_tensor(un("frstd"), [128, 512], F32)),
                            mmul=en(nc.psum_tensor(un("fmmul"), [128, 512], F32)))
                    st = {}
                    P.w('y', S.ev, astate["wo_done_ev"] if f_i == 0 else 0)
                    P.w('y', S.ev, astate.get("joinx_free", 0))
                    if astate.get("joinx_store", 0):
                        P.w('y', astate["joinx_sem"], astate["joinx_store"])
                    lv = P.do('y', lambda s, fbase=fbase: s.dma_start(
                        joinx.ap().rearrange("p (k t) -> p k t", k=KD),
                        src3[:, :, fbase:fbase + 512]), inc=S.xld, n=16)
                    d = emit_ln_tile(ps, lambda kd: pre2[:, kd * 512:(kd + 1) * 512], st,
                                     eps_ap=eps12, keep_f32=False,
                                     gains=(fmv_t["g2"], fmv_t["be2"]),
                                     dst_gain=lambda kd: pre2[:, kd * 512:(kd + 1) * 512])
                    P.w('v', S.ev, d)
                    P.w('v', S.xld, lv)
                    jv = None
                    for kd in range(KD):
                        jv = P.do('v', lambda v, kd=kd: v.tensor_add(
                            joinx[:, kd * 512:(kd + 1) * 512], pre2[:, kd * 512:(kd + 1) * 512],
                            joinx[:, kd * 512:(kd + 1) * 512]), inc=(S.ev if kd == KD - 1 else None))
                    astate["joinx_free"] = jv
                    P.w('y', S.ev, jv)
                    osem = S.out if dst3 is out3 else S.xst
                    P.do('y', lambda s, fbase=fbase: s.dma_start(
                        dst3[:, :, fbase:fbase + 512],
                        joinx.ap().rearrange("p (k t) -> p k t", k=KD)), inc=osem, n=16)
                    if dbg_key is not None:
                        P.do('y', lambda s, fbase=fbase, k=dbg_key: s.dma_start(
                            dbg3[k][:, :, fbase:fbase + 512],
                            joinx.ap().rearrange("p (k t) -> p k t", k=KD)), inc=S.out, n=16)
                    astate["joinx_sem"], astate["joinx_store"] = osem, osem.v
                    soft_barrier()

    chain = []
    if "ffn1" in stages:
        chain.append(("ffn1", emit_ffn))
    if "attn" in stages:
        chain.append(("attn", emit_attn))
    if "ffn2" in stages:
        chain.append(("ffn2", emit_ffn))
    bufs = [x_in3]
    internals = [xr1_3, xr2_3]
    for i in range(len(chain)):
        bufs.append(out3 if i == len(chain) - 1 else internals[i])
    for i, (nm, fn) in enumerate(chain):
        P.w('y', S.xst, S.xst.v)
        fn(bufs[i], bufs[i + 1], dbg_key=(nm if nm in dbg3 else None))

    if "mask" in tap_t:
        P.w('y', S.xst, S.xst.v)
        P.do('y', lambda s: s.dma_start(tap_t["mask"].ap(), maskT.ap()), inc=S.out, n=16)
        P.w('t', S.out, S.out.v)
        P.w('v', S.out, S.out.v)
        P.w('s', S.out, S.out.v)
        P.w('g', S.out, S.out.v)
    P.w('y', S.out, S.out.v)

    with nc.Block() as block:
        @block.tensor
        def _(t):
            for f in P.ops['t']:
                f(t)

        @block.vector
        def _(v):
            for f in P.ops['v']:
                f(v)

        @block.scalar
        def _(s):
            for f in P.ops['s']:
                f(s)

        @block.gpsimd
        def _(g):
            for f in P.ops['g']:
                f(g)

        @block.sync
        def _(s):
            for f in P.ops['y']:
                f(s)

    return nc


def _in_maps(inputs, nc_dummy=None):
    x = inputs["L0_attn_x"]
    consts = dict(cid32=np.eye(128, dtype=np.float32))
    wmap = dict(
        Wq=inputs["L0_attn_Wq"], Wk=inputs["L0_attn_Wk"], Wv=inputs["L0_attn_Wv"],
        Wo=inputs["L0_attn_Wo"], W1=inputs["L0_attn_W1"], W2=inputs["L0_attn_W2"],
        bq=inputs["L0_attn_bq"], bk=inputs["L0_attn_bk"], bv=inputs["L0_attn_bv"],
        bo=inputs["L0_attn_bo"], b1=inputs["L0_attn_b1"], b2=inputs["L0_attn_b2"],
        g1=inputs["L0_attn_g1"], be1=inputs["L0_attn_be1"],
        g2=inputs["L0_attn_g2"], be2=inputs["L0_attn_be2"],
        gate_w=inputs["L0_ffn_gate_w"], gate_bias=inputs["L0_ffn_gate_bias"],
        all_g=inputs["L0_ffn_all_gate_proj"], all_u=inputs["L0_ffn_all_up_proj"],
        all_d=inputs["L0_ffn_all_down_proj"],
        sh_g=inputs["L0_ffn_sh_gate"], sh_u=inputs["L0_ffn_sh_up"],
        sh_d=inputs["L0_ffn_sh_down"],
    )
    wmap = {k: np.ascontiguousarray(v, dtype=np.float32) for k, v in wmap.items()}
    maps = []
    for c in range(8):
        xc = np.ascontiguousarray(np.asarray(x[2 * c:2 * c + 2], dtype=np.float32).reshape(T, D).T)
        maps.append(dict(x_fm=xc, **wmap, **consts))
    return maps


def kernel(**inputs):
    nc = build_nc()
    maps = _in_maps(inputs)
    res = run_bass_kernel_spmd(nc, maps, core_ids=list(range(8)))
    outs = [res.results[c]["out_fm"].T.reshape(2, SEQ, D) for c in range(8)]
    return np.ascontiguousarray(np.concatenate(outs, axis=0), dtype=np.float32)


# version 02
# speedup vs baseline: 1.0124x; 1.0124x over previous
import numpy as np
from contextlib import ExitStack

import concourse.bass as bass
import concourse.mybir as mybir
from concourse.bass_utils import run_bass_kernel_spmd

F32 = mybir.dt.float32
BF16 = mybir.dt.bfloat16
U8 = mybir.dt.uint8
AF = mybir.ActivationFunctionType
ALU = mybir.AluOpType
AXX = mybir.AxisListType.X

D = 1024
KD = 8
T = 4096
E = 64
HE = 768
KH = 6
SH = 1536
KSH = 12
FF = 4096
KFF = 32
NH = 16
SEQ = 2048
QTOK = 1024
NQ = T // QTOK
NTT = QTOK // 512


class Sem:
    def __init__(self, nc, name):
        self.h = nc.alloc_semaphore(name)
        self.v = 0


class Prog:
    def __init__(self):
        self.ops = {k: [] for k in "tvsgy"}

    def w(self, e, sem, val):
        if val and val > 0:
            self.ops[e].append(lambda eng, s=sem.h, v=val: eng.wait_ge(s, v))

    def do(self, e, build, inc=None, n=1):
        if inc is None:
            self.ops[e].append(build)
            return None
        inc.v += n
        after = inc.v
        self.ops[e].append(lambda eng, b=build, s=inc.h, m=n: b(eng).then_inc(s, m))
        return after


class NS:
    def __init__(self, **kw):
        self.__dict__.update(kw)


def build_nc(stages=("ffn1", "attn", "ffn2"), dbg=(), taps=(), tseq=0):
    nc = bass.Bass("TRN2", target_bir_lowering=False)
    _uid = [0]

    def un(nm):
        _uid[0] += 1
        return f"{nm}_{_uid[0]}"

    x_in = nc.dram_tensor("x_fm", [D, T], F32, kind="ExternalInput")
    wq = nc.dram_tensor("Wq", [D, D], F32, kind="ExternalInput")
    wk = nc.dram_tensor("Wk", [D, D], F32, kind="ExternalInput")
    wv = nc.dram_tensor("Wv", [D, D], F32, kind="ExternalInput")
    wo = nc.dram_tensor("Wo", [D, D], F32, kind="ExternalInput")
    w1 = nc.dram_tensor("W1", [D, FF], F32, kind="ExternalInput")
    w2 = nc.dram_tensor("W2", [FF, D], F32, kind="ExternalInput")
    vecs = {nm: nc.dram_tensor(nm, [D], F32, kind="ExternalInput")
            for nm in ("bq", "bk", "bv", "bo", "b2", "g1", "be1", "g2", "be2")}
    b1 = nc.dram_tensor("b1", [FF], F32, kind="ExternalInput")
    gw = nc.dram_tensor("gate_w", [D, E], F32, kind="ExternalInput")
    gbias = nc.dram_tensor("gate_bias", [E], F32, kind="ExternalInput")
    allg = nc.dram_tensor("all_g", [E, D, HE], F32, kind="ExternalInput")
    allu = nc.dram_tensor("all_u", [E, D, HE], F32, kind="ExternalInput")
    alld = nc.dram_tensor("all_d", [E, HE, D], F32, kind="ExternalInput")
    shg = nc.dram_tensor("sh_g", [D, SH], F32, kind="ExternalInput")
    shu = nc.dram_tensor("sh_u", [D, SH], F32, kind="ExternalInput")
    shd = nc.dram_tensor("sh_d", [SH, D], F32, kind="ExternalInput")
    cid32 = nc.dram_tensor("cid32", [128, 128], F32, kind="ExternalInput")

    out_fm = nc.dram_tensor("out_fm", [D, T], F32, kind="ExternalOutput")
    dbg_t = {name: nc.dram_tensor("dbg_" + name, [D, T], F32, kind="ExternalOutput")
             for name in dbg}
    tap_t = {}
    if "xln" in taps:
        tap_t["xln"] = nc.dram_tensor("tap_xln", [128, 8192], BF16, kind="ExternalOutput")
    if "scores" in taps:
        tap_t["scores"] = nc.dram_tensor("tap_scores", [128, 512], F32, kind="ExternalOutput")
    if "scraw" in taps:
        tap_t["scraw"] = nc.dram_tensor("tap_scraw", [128, 512], F32, kind="ExternalOutput")
    if "mask" in taps:
        tap_t["mask"] = nc.dram_tensor("tap_mask", [E, T], F32, kind="ExternalOutput")
    if "yacc" in taps:
        tap_t["yacc"] = nc.dram_tensor("tap_yacc", [128, 8192], F32, kind="ExternalOutput")
    if "hbuf" in taps:
        tap_t["hbuf"] = nc.dram_tensor("tap_hbuf", [128, 3072], BF16, kind="ExternalOutput")
    if "mb" in taps:
        tap_t["mb"] = nc.dram_tensor("tap_mb", [128, 1024], BF16, kind="ExternalOutput")
    for nm, shape, dt_ in (("aln", [128, 16384], BF16), ("akn", [128, 16384], BF16),
                           ("avn", [128, 16384], BF16), ("akv", [128, 512], BF16),
                           ("aqn", [128, 8192], BF16), ("actx", [128, 16384], BF16),
                           ("amid", [128, 16384], BF16), ("ahff", [128, 16384], BF16),
                           ("apre2", [128, 4096], F32)):
        if nm in taps:
            tap_t[nm] = nc.dram_tensor("tap_" + nm, shape, dt_, kind="ExternalOutput")

    xr1 = nc.dram_tensor("xr1", [D, T], F32, kind="Internal")
    xr2 = nc.dram_tensor("xr2", [D, T], F32, kind="Internal")
    maskT = nc.dram_tensor("maskTd", [E, T], F32, kind="Internal")

    allg_r = allg.ap().rearrange("e (k p) h -> e p k h", p=128)
    allu_r = allu.ap().rearrange("e (k p) h -> e p k h", p=128)
    alld_r = alld.ap().rearrange("e (k p) h -> e p k h", p=128)
    shg_r = shg.ap().rearrange("(k p) h -> p k h", p=128)
    shu_r = shu.ap().rearrange("(k p) h -> p k h", p=128)
    shd_r = shd.ap().rearrange("(k p) h -> p k h", p=128)
    gw_r = gw.ap().rearrange("(k p) e -> p k e", p=128)
    wr = {k: w.ap().rearrange("(k p) n -> p k n", p=128)
          for k, w in (("q", wq), ("k", wk), ("v", wv), ("o", wo))}
    w1_r = w1.ap().rearrange("(k p) n -> p k n", p=128)
    w2_r = w2.ap().rearrange("(s p) n -> p s n", p=128)
    x_in3 = x_in.ap().rearrange("(k p) t -> p k t", p=128)
    xr1_3 = xr1.ap().rearrange("(k p) t -> p k t", p=128)
    xr2_3 = xr2.ap().rearrange("(k p) t -> p k t", p=128)
    out3 = out_fm.ap().rearrange("(k p) t -> p k t", p=128)
    dbg3 = {k: v.ap().rearrange("(k p) t -> p k t", p=128) for k, v in dbg_t.items()}

    def addr_of(h):
        return nc.lookup_mloc(h).addr

    cid32_sb = nc.alloc_sbuf_tensor("cid32_sb", [128, 128], F32)
    cid16_sb = nc.alloc_sbuf_tensor("cid16_sb", [128, 128], BF16)
    onescol_bf = nc.alloc_sbuf_tensor("onescol_bf", [128, 8], BF16)
    onesrow_f = nc.alloc_sbuf_tensor("onesrow_f", [1, 128], F32)
    onesrow_bf = nc.alloc_sbuf_tensor("onesrow_bf", [1, 128], BF16)
    eps5 = nc.alloc_sbuf_tensor("eps5", [128, 1], F32)
    eps12 = nc.alloc_sbuf_tensor("eps12", [128, 1], F32)
    bqkv_bc = nc.alloc_sbuf_tensor("bqkv_bc", [128, 3072], F32)
    gbias_bc = nc.alloc_sbuf_tensor("gbias_bc", [128, 64], F32)
    gw_sb = nc.alloc_sbuf_tensor("gw_sb", [128, 512], F32)
    fmv_t = {nm: nc.alloc_sbuf_tensor(nm + "_fm", [128, 8], F32)
             for nm in ("bo", "b2", "g1", "be1", "g2", "be2")}
    b1_fm = nc.alloc_sbuf_tensor("b1_fm", [128, 32], F32)

    arn_w = nc.alloc_sbuf_tensor("arn_w", [128, 73728], U8)
    arn_xln = nc.alloc_sbuf_tensor("arn_xln", [128, 16384], U8)
    arn_y = nc.alloc_sbuf_tensor("arn_y", [128, 32768], U8)
    arn_a = nc.alloc_sbuf_tensor("arn_a", [128, 32768], U8)
    arn_b = nc.alloc_sbuf_tensor("arn_b", [128, 24576], U8)
    AW, AX, AY, AA, AB = (addr_of(h) for h in (arn_w, arn_xln, arn_y, arn_a, arn_b))

    def at(name, shape, dtype, base, off):
        return nc.alloc_sbuf_tensor_at(name, shape, dtype, offset=base + off)

    gbuf = [at(f"gbuf{p}", [128, 6144], BF16, AW, p * 36864) for p in (0, 1)]
    ubuf = [at(f"ubuf{p}", [128, 6144], BF16, AW, p * 36864 + 12288) for p in (0, 1)]
    dbuf = [at(f"dbuf{p}", [128, 6144], BF16, AW, p * 36864 + 24576) for p in (0, 1)]
    shg_sb = at("shg_sb", [128, 12288], BF16, AW, 0)
    shu_sb = at("shu_sb", [128, 12288], BF16, AW, 24576)
    shd_sb = at("shd_sb", [128, 12288], BF16, AW, 49152)
    xln = at("xln", [128, 8192], BF16, AX, 0)
    yacc = at("yacc", [128, 8192], F32, AY, 0)
    xstage = at("xstage", [128, 4096], F32, AA, 0)
    astage = at("astage", [128, 1024], F32, AA, 16384)
    xbx = at("xbx", [128, 1024], BF16, AA, 20480)
    xsqx = at("xsqx", [128, 1024], BF16, AA, 22528)
    scores_tm = at("scores_tm", [128, 512], F32, AA, 24576)
    tk_m8 = at("tk_m8", [128, 8], F32, AA, 26624)
    tk_sel = at("tk_sel", [128, 64], F32, AA, 26656)
    tk_row = at("tk_row", [128, 8], F32, AA, 26912)
    maskfm_ch = at("maskfm_ch", [64, 256], F32, AA, 26944)
    mbcast = [at(f"mbcast{p}", [128, 1024], BF16, AB, p * 2048) for p in (0, 1)]
    mrow = [at(f"mrow{p}", [1, 1024], BF16, AB, 4096 + p * 2048) for p in (0, 1)]
    hbuf = [at(f"hbuf{p}", [128, 3072], BF16, AB, 8192 + p * 6144) for p in (0, 1)]
    silug = [at(f"silug{p}", [128, 512], BF16, AB, 20480 + p * 1024) for p in (0, 1)]
    umb = [at(f"umb{p}", [128, 512], BF16, AB, 22528 + p * 1024) for p in (0, 1)]
    lnrow = at("lnrow", [1, 2048], F32, AB, 16384)

    x1ln = at("x1ln", [128, 16384], BF16, AY, 0)
    kn_tm = at("kn_tm", [128, 16384], BF16, AW, 36864)
    ctx_fm = at("ctx_fm", [128, 16384], BF16, AW, 36864)
    v_tm = at("v_tm", [128, 16384], BF16, AA, 0)
    qn_fm = at("qn_fm", [128, 8192], BF16, AX, 0)
    pre1 = at("pre1", [128, 4096], F32, AX, 0)
    joinx = at("joinx", [128, 4096], F32, AX, 0)
    wslot = [at(f"wslot{p}", [128, 8192], BF16, AW, p * 16384) for p in (0, 1)]
    w1slot = [at(f"w1slot{p}", [128, 4096], BF16, AW, p * 8192) for p in (0, 1)]
    w2slot = [at(f"w2slot{p}", [128, 4096], BF16, AW, 16384 + p * 8192) for p in (0, 1)]
    hff = at("hff", [128, 16384], BF16, AA, 0)
    kf = at("kf", [128, 1024], F32, AB, 0)
    rbuf = at("rbuf", [128, 1024], F32, AB, 4096)
    exbuf = at("exbuf", [128, 1024], F32, AB, 8192)
    qn_ch = at("qn_ch", [128, 1024], BF16, AB, 12288)
    nrmA = at("nrmA", [128, 16], F32, AB, 14336)
    nrmB = at("nrmB", [128, 16], F32, AB, 14400)
    kv_sb = at("kv_sb", [128, 512], BF16, AB, 14848)
    pre2 = at("pre2", [128, 4096], F32, AB, 0)

    S = NS(wld=Sem(nc, "wld"), wldp=[Sem(nc, "wld0"), Sem(nc, "wld1")],
           wldsh=Sem(nc, "wldsh"), xld=Sem(nc, "xld"), xst=Sem(nc, "xst"),
           out=Sem(nc, "outst"), mm=Sem(nc, "mm"), ev=Sem(nc, "ev"),
           sc=Sem(nc, "sc"), wfr=Sem(nc, "wfr"), mr=Sem(nc, "mr"))
    P = Prog()

    def soft_barrier():
        mv, ev, sv = S.mm.v, S.ev.v, S.sc.v
        P.w('t', S.ev, ev); P.w('t', S.sc, sv)
        P.w('v', S.mm, mv); P.w('v', S.sc, sv)
        P.w('s', S.mm, mv); P.w('s', S.ev, ev)

    with ExitStack() as pctx:
        row = pctx.enter_context(nc.sbuf_tensor("p_row", [1, 1024], F32))
        vtile = pctx.enter_context(nc.sbuf_tensor("p_vt", [32, 128], F32))
        ppr = pctx.enter_context(nc.psum_tensor(un("ppr"), [128, 512], F32))
        ppr2 = pctx.enter_context(nc.psum_tensor(un("ppr2"), [128, 512], F32))

        def yld(build):
            return P.do('y', build, inc=S.xld, n=16)

        yld(lambda s: s.dma_start(cid32_sb[:], cid32.ap()))
        yld(lambda s: s.dma_start(gw_sb.ap().rearrange("p (k e) -> p k e", k=KD), gw_r))
        P.w('y', S.xld, S.xld.v)
        P.do('g', lambda g: g.dma_start(cid16_sb[:], cid32.ap()), inc=S.wldsh, n=16)
        P.do('v', lambda v: v.memset(eps5[:], 1e-5))
        P.do('v', lambda v: v.memset(eps12[:], 1e-12))
        P.do('v', lambda v: v.memset(onesrow_f[:], 1.0))
        P.do('v', lambda v: v.memset(onesrow_bf[:], 1.0))
        ms = P.do('v', lambda v: v.memset(onescol_bf[:], 1.0), inc=S.ev)
        P.w('t', S.ev, ms)

        for vec, off in ((vecs["bq"], 0), (vecs["bk"], 1024), (vecs["bv"], 2048)):
            P.w('y', S.mm, S.mm.v)
            lv = yld(lambda s, vec=vec: s.dma_start(row[:], vec.ap().rearrange("(a n) -> a n", a=1)))
            P.w('t', S.xld, lv)
            for h in range(2):
                mv = P.do('t', lambda t, h=h: t.matmul(ppr[:], onesrow_f[:],
                                                       row[:, h * 512:(h + 1) * 512],
                                                       start=True, stop=True), inc=S.mm)
                P.w('v', S.mm, mv)
                evv = P.do('v', lambda v, h=h, off=off: v.tensor_copy(
                    bqkv_bc[:, off + h * 512: off + (h + 1) * 512], ppr[:]), inc=S.ev)
                P.w('t', S.ev, evv)
        P.w('y', S.mm, S.mm.v)
        lv = yld(lambda s: s.dma_start(row[:, 0:64], gbias.ap().rearrange("(a n) -> a n", a=1)))
        P.w('t', S.xld, lv)
        mv = P.do('t', lambda t: t.matmul(ppr[:, 0:64], onesrow_f[:], row[:, 0:64],
                                          start=True, stop=True), inc=S.mm)
        P.w('v', S.mm, mv)
        evv = P.do('v', lambda v: v.tensor_copy(gbias_bc[:], ppr[:, 0:64]), inc=S.ev)
        P.w('t', S.ev, evv)

        fmv = [(vecs["bo"], fmv_t["bo"], 8), (vecs["b2"], fmv_t["b2"], 8),
               (vecs["g1"], fmv_t["g1"], 8), (vecs["be1"], fmv_t["be1"], 8),
               (vecs["g2"], fmv_t["g2"], 8), (vecs["be2"], fmv_t["be2"], 8),
               (b1, b1_fm, 32)]
        for vec, dst, c in fmv:
            P.w('y', S.mm, S.mm.v)
            lv = yld(lambda s, vec=vec, c=c: s.dma_start(
                vtile[0:c, :], vec.ap().rearrange("(c p) -> c p", p=128)))
            P.w('t', S.xld, lv)
            mv = P.do('t', lambda t, c=c: t.transpose(ppr2[:, 0:c], vtile[0:c, :],
                                                      cid32_sb[0:c, 0:c]), inc=S.mm)
            P.w('v', S.mm, mv)
            evv = P.do('v', lambda v, dst=dst, c=c: v.tensor_copy(dst[:, 0:c], ppr2[:, 0:c]),
                       inc=S.ev)
            P.w('t', S.ev, evv)

        P.w('t', S.wldsh, S.wldsh.v); P.w('v', S.wldsh, S.wldsh.v); P.w('s', S.wldsh, S.wldsh.v)
        P.w('v', S.xld, S.xld.v); P.w('s', S.xld, S.xld.v)
        P.w('y', S.xld, S.xld.v)
        soft_barrier()

    def emit_ln_tile(ps, src, state, *, eps_ap, dst_bf=None, keep_f32=False,
                     gains=None, dst_gain=None):
        n = 512
        stats_vals = []
        P.w('v', S.mm, state.get("bc_free", 0))
        for kd in range(KD):
            slot = (kd % 2) * 512
            if kd >= 2:
                P.w('v', S.mm, stats_vals[kd - 2])
            P.do('v', lambda v, kd=kd, slot=slot: v.tensor_copy(xbx[:, slot:slot + n], src(kd)))
            evv = P.do('v', lambda v, slot=slot: v.tensor_mul(
                xsqx[:, slot:slot + n], xbx[:, slot:slot + n], xbx[:, slot:slot + n]), inc=S.ev)
            P.w('t', S.ev, evv)
            if kd == 0:
                P.w('t', S.ev, state.get("row_read", 0))
            P.do('t', lambda t, kd=kd, slot=slot: t.matmul(
                ps.mean[0:1, :], onescol_bf[:, 0:1], xbx[:, slot:slot + n],
                start=(kd == 0), stop=(kd == KD - 1)))
            sv = P.do('t', lambda t, kd=kd, slot=slot: t.matmul(
                ps.sq[0:1, :], onescol_bf[:, 0:1], xsqx[:, slot:slot + n],
                start=(kd == 0), stop=(kd == KD - 1)), inc=S.mm)
            stats_vals.append(sv)
        mu = lnrow[0:1, 0:512]
        msq = lnrow[0:1, 512:1024]
        tmp = lnrow[0:1, 1024:1536]
        rstd = lnrow[0:1, 1536:2048]
        P.w('v', S.mm, stats_vals[-1])
        P.do('v', lambda v: v.tensor_scalar_mul(mu, ps.mean[0:1, :], 1.0 / D))
        P.do('v', lambda v: v.tensor_scalar_mul(msq, ps.sq[0:1, :], 1.0 / D))
        P.do('v', lambda v: v.tensor_mul(tmp, mu, mu))
        rv = P.do('v', lambda v: v.tensor_sub(tmp, msq, tmp), inc=S.ev)
        state["row_read"] = rv
        P.w('s', S.ev, rv)
        scv = P.do('s', lambda s: s.activation(msq, tmp, AF.Sqrt, bias=eps_ap[0:1, 0:1]), inc=S.sc)
        P.w('v', S.sc, scv)
        P.do('v', lambda v: v.reciprocal(rstd, msq))
        rv2 = P.do('v', lambda v: v.tensor_mul(mu, mu, rstd), inc=S.ev)
        P.w('t', S.ev, rv2)
        P.w('t', S.ev, state.get("norm_done", 0))
        P.do('t', lambda t: t.matmul(ps.rstd[:], onesrow_f[:], rstd, start=True, stop=True))
        bcv = P.do('t', lambda t: t.matmul(ps.mmul[:], onesrow_f[:], mu, start=True, stop=True),
                   inc=S.mm)
        state["bc_free"] = bcv
        P.w('v', S.mm, bcv)
        last = 0
        for kd in range(KD):
            a = astage[:, (kd % 2) * 512:(kd % 2) * 512 + n]
            P.do('v', lambda v, kd=kd, a=a: v.tensor_mul(a, src(kd), ps.rstd[:]))
            if keep_f32:
                P.do('v', lambda v, kd=kd, a=a: v.tensor_sub(src(kd), a, ps.mmul[:]))
                last = P.do('v', lambda v, kd=kd: v.tensor_copy(dst_bf(kd), src(kd)), inc=S.ev)
            else:
                P.do('v', lambda v, a=a: v.tensor_sub(a, a, ps.mmul[:]))
                if gains is not None:
                    gt, bt = gains
                    last = P.do('v', lambda v, kd=kd, a=a, gt=gt, bt=bt: v.tensor_scalar(
                        dst_gain(kd), a, gt[:, kd:kd + 1], bt[:, kd:kd + 1],
                        op0=ALU.mult, op1=ALU.add), inc=S.ev)
                else:
                    last = P.do('v', lambda v, kd=kd, a=a: v.tensor_copy(dst_bf(kd), a), inc=S.ev)
        state["norm_done"] = last
        return last

    qcarry = {"store_sem": None, "store_val": 0}

    def emit_ffn(src3, dst3, dbg_key=None):
        for q in range(NQ):
            with ExitStack() as qctx:
                en = qctx.enter_context
                ps = NS(mean=en(nc.psum_tensor(un("pmean"), [1, 512], F32)),
                        sq=en(nc.psum_tensor(un("psq"), [1, 512], F32)),
                        rstd=en(nc.psum_tensor(un("prstd"), [128, 512], F32)),
                        mmul=en(nc.psum_tensor(un("pmmul"), [128, 512], F32)),
                        sc=en(nc.psum_tensor(un("pscore"), [128, 256], F32)),
                        tp=en(nc.psum_tensor(un("ptpm"), [128, 128], F32)))
                state = {}
                mask_last_ev = 0
                for tt in range(NTT):
                    t0 = q * QTOK + tt * 512
                    P.w('y', S.mm, state.get("xstage_free_mm", 0))
                    P.w('y', S.ev, state.get("xstage_free_ev", 0))
                    if qcarry["store_sem"] is not None:
                        P.w('y', qcarry["store_sem"], qcarry["store_val"])
                        qcarry["store_sem"] = None
                    lv = P.do('y', lambda s, t0=t0: s.dma_start(
                        xstage.ap().rearrange("p (k t) -> p k t", k=KD),
                        src3[:, :, t0:t0 + 512]), inc=S.xld, n=16)
                    P.w('v', S.xld, lv)
                    xln_done = emit_ln_tile(
                        ps, lambda kd: xstage[:, kd * 512:(kd + 1) * 512], state,
                        eps_ap=eps5, keep_f32=True,
                        dst_bf=lambda kd, tt=tt: xln[:, kd * 1024 + tt * 512: kd * 1024 + (tt + 1) * 512])
                    P.w('t', S.ev, xln_done)
                    P.w('t', S.sc, state.get("score_evac", 0))
                    gmv = 0
                    for sc4 in range(4):
                        for kd in range(KD):
                            gmv = P.do('t', lambda t, sc4=sc4, kd=kd: t.matmul(
                                ps.sc[:, sc4 * 64:(sc4 + 1) * 64],
                                xstage[:, kd * 512 + sc4 * 128: kd * 512 + (sc4 + 1) * 128],
                                gw_sb[:, kd * 64:(kd + 1) * 64],
                                start=(kd == 0), stop=(kd == KD - 1)),
                                inc=(S.mm if kd == KD - 1 else None))
                    state["xstage_free_mm"] = gmv
                    state["xstage_free_ev"] = xln_done
                    P.w('s', S.mm, gmv)
                    scv = P.do('s', lambda s, tt=tt: s.activation(
                        scores_tm[:, tt * 256:(tt + 1) * 256], ps.sc[:], AF.Sigmoid), inc=S.sc)
                    state["score_evac"] = scv
                    if q == 0 and "scraw" in tap_t:
                        P.w('y', S.sc, scv)
                        P.do('y', lambda s, tt=tt: s.dma_start(
                            tap_t["scraw"].ap()[:, tt * 256:(tt + 1) * 256],
                            scores_tm[:, tt * 256:(tt + 1) * 256]), inc=S.out, n=16)
                    P.w('v', S.sc, scv)
                    for sc4 in range(4):
                        sco = scores_tm[:, tt * 256 + sc4 * 64: tt * 256 + (sc4 + 1) * 64]
                        P.do('v', lambda v, sco=sco: v.tensor_add(tk_sel[:], sco, gbias_bc[:]))
                        m8v = P.do('v', lambda v: v.max(out=tk_m8[:], in_=tk_sel[:]), inc=S.ev)
                        P.w('v', S.ev, m8v)
                        P.do('v', lambda v: v.tensor_scalar(tk_sel[:], tk_sel[:], tk_m8[:, 5:6],
                                                            None, op0=ALU.is_ge))
                        P.do('v', lambda v, sco=sco: v.tensor_mul(tk_sel[:], tk_sel[:], sco))
                        rsv = P.do('v', lambda v: v.reduce_sum(tk_row[:, 0:1], tk_sel[:], axis=AXX),
                                   inc=S.ev)
                        P.w('v', S.ev, rsv)
                        rcv = P.do('v', lambda v: v.reciprocal(tk_row[:, 2:3], tk_row[:, 0:1]),
                                   inc=S.ev)
                        P.w('v', S.ev, rcv)
                        mask_last_ev = P.do('v', lambda v, sco=sco: v.tensor_scalar(
                            sco, tk_sel[:], tk_row[:, 2:3], 1.25, op0=ALU.mult, op1=ALU.mult),
                            inc=S.ev)
                st_vals = []
                P.w('t', S.ev, mask_last_ev)
                for c in range(8):
                    P.w('t', S.ev, state.get("tp_free", 0))
                    mv = P.do('t', lambda t, c=c: t.transpose(
                        ps.tp[0:64, :], scores_tm[:, c * 64:(c + 1) * 64], cid32_sb[:]), inc=S.mm)
                    P.w('v', S.mm, mv)
                    if c >= 2:
                        P.w('v', S.xst, st_vals[c - 1])
                    evv = P.do('v', lambda v, c=c: v.tensor_copy(
                        maskfm_ch[:, (c % 2) * 128:(c % 2 + 1) * 128], ps.tp[0:64, :]), inc=S.ev)
                    state["tp_free"] = evv
                    P.w('y', S.ev, evv)
                    stv = P.do('y', lambda s, c=c, q=q: s.dma_start(
                        maskT.ap()[:, q * QTOK + c * 128: q * QTOK + (c + 1) * 128],
                        maskfm_ch[:, (c % 2) * 128:(c % 2 + 1) * 128]), inc=S.xst, n=16)
                    st_vals.append(stv)
                mask_store_done = S.xst.v
                if q == 0 and "xln" in tap_t:
                    P.w('y', S.ev, S.ev.v)
                    P.do('y', lambda s: s.dma_start(tap_t["xln"].ap(), xln[:]), inc=S.out, n=16)
                    P.w('t', S.out, S.out.v)
                    P.w('v', S.out, S.out.v)
                    P.w('s', S.out, S.out.v)
                    P.w('g', S.out, S.out.v)
                if q == 0 and "scores" in tap_t:
                    P.w('y', S.ev, S.ev.v)
                    P.do('y', lambda s: s.dma_start(tap_t["scores"].ap(), scores_tm[:]), inc=S.out, n=16)
                    P.w('t', S.out, S.out.v)
                    P.w('v', S.out, S.out.v)
                    P.w('s', S.out, S.out.v)
                    P.w('g', S.out, S.out.v)
                soft_barrier()

            with ExitStack() as ectx:
                en = ectx.enter_context
                pG = [en(nc.psum_tensor(un(f"pG{i}"), [128, 512], F32)) for i in (0, 1)]
                pU = [en(nc.psum_tensor(un(f"pU{i}"), [128, 512], F32)) for i in (0, 1)]
                pD = [en(nc.psum_tensor(un(f"pD{i}"), [128, 1024], F32)) for i in (0, 1)]
                glast = {}
                estate = {"hfree": {0: 0, 1: 0}}
                d_evs = {0: 0, 1: 0}

                P.w('g', S.wfr, estate.get("wprev", S.wfr.v))
                P.do('g', lambda g: g.dma_start(shg_sb.ap().rearrange("p (k h) -> p k h", k=KD), shg_r), inc=S.wldsh, n=16)
                P.do('g', lambda g: g.dma_start(shu_sb.ap().rearrange("p (k h) -> p k h", k=KD), shu_r), inc=S.wldsh, n=16)
                P.do('g', lambda g: g.dma_start(shd_sb.ap().rearrange("p (k h) -> p k h", k=KSH), shd_r), inc=S.wldsh, n=16)
                glast["sh"] = S.wldsh.v

                def stream_expert(e, wait_vals):
                    par = e % 2
                    sem = S.wldp[par]
                    for wv_ in wait_vals:
                        P.w('g', S.wfr, wv_)
                    P.do('g', lambda g, e=e, par=par: g.dma_start(
                        gbuf[par].ap().rearrange("p (k h) -> p k h", k=KD), allg_r[e]),
                        inc=sem, n=16)
                    P.do('g', lambda g, e=e, par=par: g.dma_start(
                        ubuf[par].ap().rearrange("p (k h) -> p k h", k=KD), allu_r[e]),
                        inc=sem, n=16)
                    P.do('g', lambda g, e=e, par=par: g.dma_start(
                        dbuf[par].ap().rearrange("p (k h) -> p k h", k=KH), alld_r[e]),
                        inc=sem, n=16)
                    P.w('g', S.mr, estate.get(f"mrfree{par}", 0))
                    P.w('g', S.xst, mask_store_done)
                    P.do('g', lambda g, e=e, par=par, q=q: g.dma_start(
                        mrow[par][:], maskT.ap()[e:e + 1, q * QTOK:(q + 1) * QTOK]),
                        inc=sem, n=16)
                    glast[e] = sem.v

                P.w('t', S.wldsh, glast["sh"])
                shgu_val = None
                for tt in range(NTT):
                    for hc in range(KSH):
                        bp = hc % 2
                        P.w('t', S.sc, estate.get(f"silu_free{bp}", 0))
                        P.w('t', S.sc, estate.get(f"ush_free{bp}", 0))
                        for kd in range(KD):
                            P.do('t', lambda t, tt=tt, hc=hc, kd=kd, bp=bp: t.matmul(
                                pG[bp][:],
                                shg_sb[:, kd * 1536 + hc * 128: kd * 1536 + (hc + 1) * 128],
                                xln[:, kd * 1024 + tt * 512: kd * 1024 + (tt + 1) * 512],
                                start=(kd == 0), stop=(kd == KD - 1)))
                        guv = None
                        for kd in range(KD):
                            guv = P.do('t', lambda t, tt=tt, hc=hc, kd=kd, bp=bp: t.matmul(
                                pU[bp][:],
                                shu_sb[:, kd * 1536 + hc * 128: kd * 1536 + (hc + 1) * 128],
                                xln[:, kd * 1024 + tt * 512: kd * 1024 + (tt + 1) * 512],
                                start=(kd == 0), stop=(kd == KD - 1)),
                                inc=(S.mm if kd == KD - 1 else None))
                        if tt == NTT - 1 and hc == KSH - 1:
                            shgu_val = P.do('t', lambda t: t.nop(), inc=S.wfr)
                        P.w('s', S.mm, guv)
                        sgv = P.do('s', lambda s, bp=bp: s.activation(silug[bp][:], pG[bp][:],
                                                                      AF.Silu), inc=S.sc)
                        estate[f"silu_free{bp}"] = sgv
                        uev = P.do('s', lambda s, bp=bp: s.activation(umb[bp][:], pU[bp][:],
                                                                      AF.Copy, scale=0.5), inc=S.sc)
                        estate[f"ush_free{bp}"] = uev
                        P.w('v', S.sc, uev)
                        if hc == 0:
                            P.w('v', S.mm, estate["hfree"][tt % 2])
                        hv = P.do('v', lambda v, bp=bp, tt=tt, hc=hc: v.tensor_mul(
                            hbuf[tt % 2][:, (hc % KH) * 512:((hc % KH) + 1) * 512],
                            silug[bp][:], umb[bp][:]), inc=S.ev)
                        if hc == KH - 1 or hc == KSH - 1:
                            part = 0 if hc == KH - 1 else 1
                            first = (part == 0)
                            dm_last = None
                            for grp in range(4):
                                gpar = grp % 2
                                P.w('t', S.ev, hv)
                                P.w('t', S.ev, d_evs[gpar])
                                for sub in range(2):
                                    mo = grp * 2 + sub
                                    for kh in range(KH):
                                        khg = part * KH + kh
                                        dm_last = P.do('t', lambda t, tt=tt, gpar=gpar, sub=sub, mo=mo, khg=khg, kh=kh: t.matmul(
                                            pD[gpar][:, sub * 512:(sub + 1) * 512],
                                            shd_sb[:, khg * 1024 + mo * 128: khg * 1024 + (mo + 1) * 128],
                                            hbuf[tt % 2][:, kh * 512:(kh + 1) * 512],
                                            start=(kh == 0), stop=(kh == KH - 1)),
                                            inc=(S.mm if (sub == 1 and kh == KH - 1) else None))
                                P.w('v', S.mm, dm_last)
                                for sub in range(2):
                                    mo = grp * 2 + sub
                                    ysl = yacc[:, mo * 1024 + tt * 512: mo * 1024 + (tt + 1) * 512]
                                    if first:
                                        ev_ = P.do('v', lambda v, gpar=gpar, sub=sub, ysl=ysl: v.tensor_copy(
                                            ysl, pD[gpar][:, sub * 512:(sub + 1) * 512]), inc=S.ev)
                                    else:
                                        ev_ = P.do('v', lambda v, gpar=gpar, sub=sub, ysl=ysl: v.tensor_add(
                                            ysl, pD[gpar][:, sub * 512:(sub + 1) * 512], ysl), inc=S.ev)
                                d_evs[gpar] = ev_
                            estate["hfree"][tt % 2] = dm_last
                shdown_val = P.do('t', lambda t: t.nop(), inc=S.wfr)

                stream_expert(0, [shgu_val])
                stream_expert(1, [shgu_val, shdown_val])

                for e in range(E):
                    par = e % 2
                    P.w('t', S.wldp[par], glast[e])
                    mb_evs = []
                    for tt in range(NTT):
                        P.w('t', S.sc, estate.get(f"silu_free{tt}", 0))
                        P.w('t', S.ev, estate.get(f"um_free{tt}", 0))
                        bv_ = P.do('t', lambda t, tt=tt, par=par: t.matmul(
                            pG[tt][:], onesrow_bf[:], mrow[par][:, tt * 512:(tt + 1) * 512],
                            start=True, stop=True), inc=S.mm)
                        P.w('s', S.mm, bv_)
                        sev = P.do('s', lambda s, tt=tt, par=par: s.activation(
                            mbcast[par][:, tt * 512:(tt + 1) * 512], pG[tt][:], AF.Copy), inc=S.sc)
                        estate[f"silu_free{tt}"] = sev
                        mb_evs.append(sev)
                    mrv = P.do('t', lambda t: t.nop(), inc=S.mr)
                    estate[f"mrfree{par}"] = mrv
                    for tt in range(NTT):
                        for hc in range(KH):
                            bp = hc % 2
                            P.w('t', S.sc, estate.get(f"silu_free{bp}", 0))
                            P.w('t', S.ev, estate.get(f"um_free{bp}", 0))
                            for kd in range(KD):
                                P.do('t', lambda t, tt=tt, hc=hc, kd=kd, bp=bp, par=par: t.matmul(
                                    pG[bp][:],
                                    gbuf[par][:, kd * 768 + hc * 128: kd * 768 + (hc + 1) * 128],
                                    xln[:, kd * 1024 + tt * 512: kd * 1024 + (tt + 1) * 512],
                                    start=(kd == 0), stop=(kd == KD - 1)))
                            guv = None
                            for kd in range(KD):
                                guv = P.do('t', lambda t, tt=tt, hc=hc, kd=kd, bp=bp, par=par: t.matmul(
                                    pU[bp][:],
                                    ubuf[par][:, kd * 768 + hc * 128: kd * 768 + (hc + 1) * 128],
                                    xln[:, kd * 1024 + tt * 512: kd * 1024 + (tt + 1) * 512],
                                    start=(kd == 0), stop=(kd == KD - 1)),
                                    inc=(S.mm if kd == KD - 1 else None))
                            P.w('s', S.mm, guv)
                            sgv = P.do('s', lambda s, bp=bp: s.activation(
                                silug[bp][:], pG[bp][:], AF.Silu), inc=S.sc)
                            estate[f"silu_free{bp}"] = sgv
                            P.w('v', S.mm, guv)
                            P.w('v', S.sc, mb_evs[tt])
                            umv = P.do('v', lambda v, bp=bp, tt=tt, par=par: v.tensor_mul(
                                umb[bp][:], pU[bp][:], mbcast[par][:, tt * 512:(tt + 1) * 512]),
                                inc=S.ev)
                            estate[f"um_free{bp}"] = umv
                            P.w('v', S.sc, sgv)
                            if hc == 0:
                                P.w('v', S.mm, estate["hfree"][tt % 2])
                            hv = P.do('v', lambda v, bp=bp, tt=tt, hc=hc: v.tensor_mul(
                                hbuf[tt % 2][:, hc * 512:(hc + 1) * 512], silug[bp][:], umb[bp][:]),
                                inc=S.ev)
                        dm_last = None
                        P.w('t', S.ev, hv)
                        for grp in range(4):
                            gpar = grp % 2
                            P.w('t', S.ev, d_evs[gpar])
                            for sub in range(2):
                                mo = grp * 2 + sub
                                for kh in range(KH):
                                    dm_last = P.do('t', lambda t, tt=tt, gpar=gpar, sub=sub, mo=mo, kh=kh, par=par: t.matmul(
                                        pD[gpar][:, sub * 512:(sub + 1) * 512],
                                        dbuf[par][:, kh * 1024 + mo * 128: kh * 1024 + (mo + 1) * 128],
                                        hbuf[tt % 2][:, kh * 512:(kh + 1) * 512],
                                        start=(kh == 0), stop=(kh == KH - 1)),
                                        inc=(S.mm if (sub == 1 and kh == KH - 1) else None))
                            P.w('v', S.mm, dm_last)
                            for sub in range(2):
                                mo = grp * 2 + sub
                                ysl = yacc[:, mo * 1024 + tt * 512: mo * 1024 + (tt + 1) * 512]
                                ev_ = P.do('v', lambda v, gpar=gpar, sub=sub, ysl=ysl: v.tensor_add(
                                    ysl, pD[gpar][:, sub * 512:(sub + 1) * 512], ysl), inc=S.ev)
                            d_evs[gpar] = ev_
                        estate["hfree"][tt % 2] = dm_last
                    fv = P.do('t', lambda t: t.nop(), inc=S.wfr)
                    estate[f"wfree{par}"] = fv
                    if e + 2 < E:
                        stream_expert(e + 2, [fv])
                estate["wprev"] = S.wfr.v
                y_done = max(d_evs.values())
                if q == 0 and "yacc" in tap_t:
                    P.w('y', S.ev, S.ev.v)
                    P.do('y', lambda s: s.dma_start(tap_t["yacc"].ap(), yacc[:]), inc=S.out, n=16)
                    P.w('t', S.out, S.out.v)
                    P.w('v', S.out, S.out.v)
                    P.w('s', S.out, S.out.v)
                    P.w('g', S.out, S.out.v)
                if q == 0 and "hbuf" in tap_t:
                    P.w('y', S.ev, S.ev.v)
                    P.do('y', lambda s: s.dma_start(tap_t["hbuf"].ap(), hbuf[1][:]), inc=S.out, n=16)
                    P.w('t', S.out, S.out.v)
                    P.w('v', S.out, S.out.v)
                    P.w('s', S.out, S.out.v)
                    P.w('g', S.out, S.out.v)
                if q == 0 and "mb" in tap_t:
                    P.w('y', S.ev, S.ev.v)
                    P.do('y', lambda s: s.dma_start(tap_t["mb"].ap(), mbcast[1][:]), inc=S.out, n=16)
                    P.w('t', S.out, S.out.v)
                    P.w('v', S.out, S.out.v)
                    P.w('s', S.out, S.out.v)
                    P.w('g', S.out, S.out.v)
                soft_barrier()

            jfree = 0
            for tt in range(NTT):
                t0 = q * QTOK + tt * 512
                P.w('y', S.ev, jfree)
                lv = P.do('y', lambda s, t0=t0: s.dma_start(
                    xstage.ap().rearrange("p (k t) -> p k t", k=KD),
                    src3[:, :, t0:t0 + 512]), inc=S.xld, n=16)
                P.w('v', S.xld, lv)
                P.w('v', S.ev, y_done)
                jv = None
                for kd in range(KD):
                    jv = P.do('v', lambda v, kd=kd, tt=tt: v.tensor_add(
                        xstage[:, kd * 512:(kd + 1) * 512],
                        yacc[:, kd * 1024 + tt * 512: kd * 1024 + (tt + 1) * 512],
                        xstage[:, kd * 512:(kd + 1) * 512]),
                        inc=(S.ev if kd == KD - 1 else None))
                jfree = jv
                P.w('y', S.ev, jv)
                osem = S.out if dst3 is out3 else S.xst
                P.do('y', lambda s, t0=t0: s.dma_start(
                    dst3[:, :, t0:t0 + 512],
                    xstage.ap().rearrange("p (k t) -> p k t", k=KD)), inc=osem, n=16)
                if dbg_key is not None:
                    P.do('y', lambda s, t0=t0, k=dbg_key: s.dma_start(
                        dbg3[k][:, :, t0:t0 + 512],
                        xstage.ap().rearrange("p (k t) -> p k t", k=KD)), inc=S.out, n=16)
                qcarry["store_sem"], qcarry["store_val"] = osem, osem.v
            soft_barrier()

    def emit_attn(src3, dst3, dbg_key=None):
        astate = {}

        def eln(bset, bias_off, dst_fn):
            for hh in range(2):
                P.do('v', lambda v, hh=hh, bias_off=bias_off: v.tensor_add(
                    kf[:, hh * 512:(hh + 1) * 512], bset[hh][:],
                    bqkv_bc[:, bias_off + hh * 512: bias_off + (hh + 1) * 512]))
            P.do('v', lambda v: v.tensor_relu(rbuf[:], kf[:]))
            mv_ = P.do('v', lambda v: v.tensor_sub(exbuf[:], kf[:], rbuf[:]), inc=S.ev)
            P.w('s', S.ev, mv_)
            scv = P.do('s', lambda s: s.activation(exbuf[:], exbuf[:], AF.Exp), inc=S.sc)
            P.w('v', S.sc, scv)
            P.do('v', lambda v: v.scalar_tensor_tensor(kf[:], exbuf[:], -1.0, rbuf[:],
                                                       op0=ALU.add, op1=ALU.add))
            P.do('v', lambda v: v.tensor_mul(rbuf[:], kf[:], kf[:]))
            nv = None
            for h in range(NH):
                nv = P.do('v', lambda v, h=h: v.reduce_sum(
                    nrmA[:, h:h + 1], rbuf[:, h * 64:(h + 1) * 64], axis=AXX),
                    inc=(S.ev if h == NH - 1 else None))
            P.w('s', S.ev, nv)
            sqv = P.do('s', lambda s: s.activation(nrmB[:], nrmA[:], AF.Sqrt), inc=S.sc)
            P.w('v', S.sc, sqv)
            rv = P.do('v', lambda v: v.reciprocal(nrmA[:], nrmB[:]), inc=S.ev)
            P.w('v', S.ev, rv)
            last = None
            for h in range(NH):
                last = P.do('v', lambda v, h=h: v.tensor_scalar_mul(
                    dst_fn(h), kf[:, h * 64:(h + 1) * 64], nrmA[:, h:h + 1]),
                    inc=(S.ev if h == NH - 1 else None))
            return last

        for s_i in range(2):
            base = s_i * SEQ
            with ExitStack() as actx:
                en = actx.enter_context
                ps = NS(mean=en(nc.psum_tensor(un("amean"), [1, 512], F32)),
                        sq=en(nc.psum_tensor(un("asq"), [1, 512], F32)),
                        rstd=en(nc.psum_tensor(un("arstd"), [128, 512], F32)),
                        mmul=en(nc.psum_tensor(un("ammul"), [128, 512], F32)))
                st = {}
                for tt in range(4):
                    t0 = base + tt * 512
                    P.w('y', S.ev, st.get("norm_done", 0))
                    lv = P.do('y', lambda s, t0=t0: s.dma_start(
                        xstage.ap().rearrange("p (k t) -> p k t", k=KD),
                        src3[:, :, t0:t0 + 512]), inc=S.xld, n=16)
                    P.w('v', S.xld, lv)
                    emit_ln_tile(ps, lambda kd: xstage[:, kd * 512:(kd + 1) * 512], st,
                                 eps_ap=eps5, keep_f32=False,
                                 dst_bf=lambda kd, tt=tt: x1ln[:, kd * 2048 + tt * 512:
                                                               kd * 2048 + (tt + 1) * 512])
                soft_barrier()
                if s_i == tseq and "aln" in tap_t:
                    P.w('y', S.ev, S.ev.v)
                    P.w('y', S.sc, S.sc.v)
                    P.do('y', lambda s: s.dma_start(tap_t["aln"].ap(), x1ln[:]), inc=S.out, n=16)
                    P.w('t', S.out, S.out.v)
                    P.w('v', S.out, S.out.v)
                    P.w('s', S.out, S.out.v)
                    P.w('g', S.out, S.out.v)

            P.w('g', S.wfr, S.wfr.v)
            P.do('g', lambda g: g.dma_start(wslot[0].ap().rearrange("p (k n) -> p k n", k=KD), wr["k"]), inc=S.wldp[0], n=16)
            wk_ld = S.wldp[0].v
            P.do('g', lambda g: g.dma_start(wslot[1].ap().rearrange("p (k n) -> p k n", k=KD), wr["v"]), inc=S.wldp[1], n=16)
            wv_ld = S.wldp[1].v
            with ExitStack() as actx:
                en = actx.enter_context
                pP = [en(nc.psum_tensor(un(f"pP{i}"), [128, 512], F32)) for i in range(4)]
                ev_done = {}
                for c in range(16):
                    for w_i in range(2):
                        P.w('t', S.wldp[0] if w_i == 0 else S.wldp[1], wk_ld if w_i == 0 else wv_ld)
                        bset = [pP[2 * w_i], pP[2 * w_i + 1]]
                        P.w('t', S.ev, ev_done.get((c - 1, w_i), 0))
                        prv = None
                        for hh in range(2):
                            for kd in range(KD):
                                prv = P.do('t', lambda t, c=c, hh=hh, kd=kd, w_i=w_i, bset=bset: t.matmul(
                                    bset[hh][:],
                                    x1ln[:, kd * 2048 + c * 128: kd * 2048 + (c + 1) * 128],
                                    wslot[w_i][:, kd * 1024 + hh * 512: kd * 1024 + (hh + 1) * 512],
                                    start=(kd == 0), stop=(kd == KD - 1)),
                                    inc=(S.mm if (hh == 1 and kd == KD - 1) else None))
                        P.w('v', S.mm, prv)
                        if w_i == 0:
                            d = eln(bset, 1024,
                                    lambda h, c=c: kn_tm[:, c * 1024 + h * 64: c * 1024 + (h + 1) * 64])
                        else:
                            d = None
                            for hh in range(2):
                                d = P.do('v', lambda v, c=c, hh=hh, bset=bset: v.tensor_add(
                                    v_tm[:, c * 1024 + hh * 512: c * 1024 + (hh + 1) * 512],
                                    bset[hh][:], bqkv_bc[:, 2048 + hh * 512: 2048 + (hh + 1) * 512]),
                                    inc=(S.ev if hh == 1 else None))
                        ev_done[(c, w_i)] = d
                wfv = P.do('t', lambda t: t.nop(), inc=S.wfr)
                soft_barrier()
                if s_i == tseq and "akn" in tap_t:
                    P.w('y', S.ev, S.ev.v)
                    P.w('y', S.sc, S.sc.v)
                    P.do('y', lambda s: s.dma_start(tap_t["akn"].ap(), kn_tm[:]), inc=S.out, n=16)
                    P.w('t', S.out, S.out.v)
                    P.w('v', S.out, S.out.v)
                    P.w('s', S.out, S.out.v)
                    P.w('g', S.out, S.out.v)
                if s_i == tseq and "avn" in tap_t:
                    P.w('y', S.ev, S.ev.v)
                    P.w('y', S.sc, S.sc.v)
                    P.do('y', lambda s: s.dma_start(tap_t["avn"].ap(), v_tm[:]), inc=S.out, n=16)
                    P.w('t', S.out, S.out.v)
                    P.w('v', S.out, S.out.v)
                    P.w('s', S.out, S.out.v)
                    P.w('g', S.out, S.out.v)

            with ExitStack() as actx:
                en = actx.enter_context
                pKV = [en(nc.psum_tensor(un(f"pKV{i}"), [128, 64], F32)) for i in range(8)]
                kvv = None
                for c in range(16):
                    for h in range(NH):
                        bank = pKV[h // 2]
                        rh = (h % 2) * 64
                        kvv = P.do('t', lambda t, c=c, h=h, bank=bank, rh=rh: t.matmul(
                            bank[rh:rh + 64, :],
                            kn_tm[:, c * 1024 + h * 64: c * 1024 + (h + 1) * 64],
                            v_tm[:, c * 1024 + h * 64: c * 1024 + (h + 1) * 64],
                            start=(c == 0), stop=(c == 15)),
                            inc=(S.mm if (c == 15 and h == NH - 1) else None))
                P.w('s', S.mm, kvv)
                kve = None
                for h in range(NH):
                    bank = pKV[h // 2]
                    rh = (h % 2) * 64
                    kve = P.do('s', lambda s, h=h, bank=bank, rh=rh: s.activation(
                        kv_sb[rh:rh + 64, (h // 2) * 64:(h // 2 + 1) * 64],
                        bank[rh:rh + 64, :], AF.Copy, scale=0.125),
                        inc=(S.sc if h == NH - 1 else None))
                soft_barrier()
                if s_i == tseq and "akv" in tap_t:
                    P.w('y', S.ev, S.ev.v)
                    P.w('y', S.sc, S.sc.v)
                    P.do('y', lambda s: s.dma_start(tap_t["akv"].ap(), kv_sb[:]), inc=S.out, n=16)
                    P.w('t', S.out, S.out.v)
                    P.w('v', S.out, S.out.v)
                    P.w('s', S.out, S.out.v)
                    P.w('g', S.out, S.out.v)

            P.w('g', S.wfr, S.wfr.v)
            P.do('g', lambda g: g.dma_start(wslot[0].ap().rearrange("p (k n) -> p k n", k=KD), wr["q"]), inc=S.wldp[0], n=16)
            wq_ld = S.wldp[0].v
            P.do('g', lambda g: g.dma_start(wslot[1].ap().rearrange("p (k n) -> p k n", k=KD), wr["o"]), inc=S.wldp[1], n=16)
            wo_ld = S.wldp[1].v
            for hs in range(2):
                with ExitStack() as actx:
                    en = actx.enter_context
                    pP = [en(nc.psum_tensor(un(f"pQ{i}"), [128, 512], F32)) for i in range(4)]
                    pT = en(nc.psum_tensor(un("pT"), [128, 128], BF16))
                    ev_done = {}
                    tp_done = {}
                    for c8 in range(8):
                        c = hs * 8 + c8
                        P.w('t', S.wldp[0], wq_ld)
                        bset = [pP[2 * (c8 % 2)], pP[2 * (c8 % 2) + 1]]
                        P.w('t', S.ev, ev_done.get(c8 - 2, 0))
                        prv = None
                        for hh in range(2):
                            for kd in range(KD):
                                prv = P.do('t', lambda t, c=c, hh=hh, kd=kd, bset=bset: t.matmul(
                                    bset[hh][:],
                                    x1ln[:, kd * 2048 + c * 128: kd * 2048 + (c + 1) * 128],
                                    wslot[0][:, kd * 1024 + hh * 512: kd * 1024 + (hh + 1) * 512],
                                    start=(kd == 0), stop=(kd == KD - 1)),
                                    inc=(S.mm if (hh == 1 and kd == KD - 1) else None))
                        P.w('v', S.mm, prv)
                        d = eln(bset, 0, lambda h: qn_ch[:, h * 64:(h + 1) * 64])
                        ev_done[c8] = d
                        for kd in range(KD):
                            if kd == 0:
                                P.w('t', S.ev, d)
                            prev_tp = tp_done.get((c8, kd - 1)) if kd > 0 else tp_done.get((c8 - 1, KD - 1))
                            P.w('t', S.ev, prev_tp or 0)
                            mv_ = P.do('t', lambda t, kd=kd: t.transpose(
                                pT[:], qn_ch[:, kd * 128:(kd + 1) * 128], cid16_sb[:]), inc=S.mm)
                            P.w('v', S.mm, mv_)
                            tv = P.do('v', lambda v, kd=kd, c8=c8: v.tensor_copy(
                                qn_fm[:, kd * 1024 + c8 * 128: kd * 1024 + (c8 + 1) * 128], pT[:]),
                                inc=S.ev)
                            tp_done[(c8, kd)] = tv
                    if hs == 1:
                        astate["wq_free"] = P.do('t', lambda t: t.nop(), inc=S.wfr)
                    soft_barrier()
                with ExitStack() as actx:
                    en = actx.enter_context
                    pC = [en(nc.psum_tensor(un(f"pC{i}"), [128, 512], F32)) for i in range(4)]
                    last_ev = {}
                    for tt in range(2):
                        for h in range(NH):
                            slot = (h // 2) % 4
                            rh = (h % 2) * 64
                            P.w('t', S.ev, last_ev.get((slot, rh), 0))
                            cv = P.do('t', lambda t, tt=tt, h=h, slot=slot, rh=rh: t.matmul(
                                pC[slot][rh:rh + 64, :],
                                kv_sb[rh:rh + 64, (h // 2) * 64:(h // 2 + 1) * 64],
                                qn_fm[rh:rh + 64, (h // 2) * 1024 + tt * 512:
                                      (h // 2) * 1024 + (tt + 1) * 512],
                                start=True, stop=True), inc=S.mm)
                            P.w('v', S.mm, cv)
                            ev_ = P.do('v', lambda v, tt=tt, h=h, slot=slot, rh=rh, hs=hs: v.tensor_copy(
                                ctx_fm[rh:rh + 64,
                                       (h // 2) * 2048 + hs * 1024 + tt * 512:
                                       (h // 2) * 2048 + hs * 1024 + (tt + 1) * 512]
                                if rh else
                                ctx_fm[0:64,
                                       (h // 2) * 2048 + hs * 1024 + tt * 512:
                                       (h // 2) * 2048 + hs * 1024 + (tt + 1) * 512],
                                pC[slot][rh:rh + 64, :]), inc=S.ev)
                            last_ev[(slot, rh)] = ev_
                    soft_barrier()
                if s_i == tseq and hs == 0 and "aqn" in tap_t:
                    P.w('y', S.ev, S.ev.v)
                    P.do('y', lambda s: s.dma_start(tap_t["aqn"].ap(), qn_fm[:]), inc=S.out, n=16)
                    P.w('t', S.out, S.out.v)
                    P.w('v', S.out, S.out.v)
                    P.w('s', S.out, S.out.v)
                    P.w('g', S.out, S.out.v)

            with ExitStack() as actx:
                en = actx.enter_context
                pW = [en(nc.psum_tensor(un(f"pW{i}"), [128, 512], F32)) for i in range(2)]
                ps = NS(mean=en(nc.psum_tensor(un("wmean"), [1, 512], F32)),
                        sq=en(nc.psum_tensor(un("wsq"), [1, 512], F32)),
                        rstd=en(nc.psum_tensor(un("wrstd"), [128, 512], F32)),
                        mmul=en(nc.psum_tensor(un("wmmul"), [128, 512], F32)))
                st = {}
                for tt in range(4):
                    P.w('t', S.wldp[1], wo_ld)
                    for mo in range(KD):
                        bp = mo % 2
                        P.w('t', S.ev, st.get(f"wo_ev{bp}", 0))
                        wvv = None
                        for kd in range(KD):
                            wvv = P.do('t', lambda t, tt=tt, mo=mo, kd=kd, bp=bp: t.matmul(
                                pW[bp][:],
                                wslot[1][:, kd * 1024 + mo * 128: kd * 1024 + (mo + 1) * 128],
                                ctx_fm[:, kd * 2048 + tt * 512: kd * 2048 + (tt + 1) * 512],
                                start=(kd == 0), stop=(kd == KD - 1)),
                                inc=(S.mm if kd == KD - 1 else None))
                        P.w('v', S.mm, wvv)
                        evv = P.do('v', lambda v, mo=mo, tt=tt, bp=bp: v.scalar_tensor_tensor(
                            pre1[:, mo * 512:(mo + 1) * 512], pW[bp][:],
                            fmv_t["bo"][:, mo:mo + 1],
                            x1ln[:, mo * 2048 + tt * 512: mo * 2048 + (tt + 1) * 512],
                            op0=ALU.add, op1=ALU.add), inc=S.ev)
                        st[f"wo_ev{bp}"] = evv
                    emit_ln_tile(ps, lambda kd: pre1[:, kd * 512:(kd + 1) * 512], st,
                                 eps_ap=eps12, keep_f32=False,
                                 gains=(fmv_t["g1"], fmv_t["be1"]),
                                 dst_gain=lambda kd, tt=tt: x1ln[:, kd * 2048 + tt * 512:
                                                                 kd * 2048 + (tt + 1) * 512])
                astate["wo_free"] = P.do('t', lambda t: t.nop(), inc=S.wfr)
                astate["wo_done_ev"] = S.ev.v
                soft_barrier()
                if s_i == tseq and "actx" in tap_t:
                    P.w('y', S.ev, S.ev.v)
                    P.w('y', S.sc, S.sc.v)
                    P.do('y', lambda s: s.dma_start(tap_t["actx"].ap(), ctx_fm[:]), inc=S.out, n=16)
                    P.w('t', S.out, S.out.v)
                    P.w('v', S.out, S.out.v)
                    P.w('s', S.out, S.out.v)
                    P.w('g', S.out, S.out.v)
                if s_i == tseq and "amid" in tap_t:
                    P.w('y', S.ev, S.ev.v)
                    P.w('y', S.sc, S.sc.v)
                    P.do('y', lambda s: s.dma_start(tap_t["amid"].ap(), x1ln[:]), inc=S.out, n=16)
                    P.w('t', S.out, S.out.v)
                    P.w('v', S.out, S.out.v)
                    P.w('s', S.out, S.out.v)
                    P.w('g', S.out, S.out.v)

            for f_i in range(4):
                fbase = base + f_i * 512
                with ExitStack() as actx:
                    en = actx.enter_context
                    pF = [en(nc.psum_tensor(un(f"pF{i}"), [128, 512], F32)) for i in range(2)]
                    for mos in range(8):
                        wp = mos % 2
                        P.w('g', S.wfr, astate.get("wq_free", 0))
                        P.w('g', S.wfr, astate.get(f"w1free{wp}", 0))
                        P.do('g', lambda g, mos=mos, wp=wp: g.dma_start(
                            w1slot[wp].ap().rearrange("p (k n) -> p k n", n=512),
                            w1_r[:, :, mos * 512:(mos + 1) * 512]), inc=S.wldp[wp], n=16)
                        w1_ld = S.wldp[wp].v
                        P.w('t', S.wldp[wp], w1_ld)
                        for moc in range(4):
                            bp = moc % 2
                            P.w('t', S.sc, astate.get(f"ff_sc{bp}", 0))
                            fvv = None
                            for kd in range(KD):
                                fvv = P.do('t', lambda t, mos=mos, moc=moc, kd=kd, bp=bp, wp=wp, f_i=f_i: t.matmul(
                                    pF[bp][:],
                                    w1slot[wp][:, kd * 512 + moc * 128: kd * 512 + (moc + 1) * 128],
                                    x1ln[:, kd * 2048 + f_i * 512: kd * 2048 + (f_i + 1) * 512],
                                    start=(kd == 0), stop=(kd == KD - 1)),
                                    inc=(S.mm if kd == KD - 1 else None))
                            P.w('s', S.mm, fvv)
                            mo = mos * 4 + moc
                            scv = P.do('s', lambda s, mo=mo, bp=bp: s.activation(
                                hff[:, mo * 512:(mo + 1) * 512], pF[bp][:], AF.Gelu,
                                bias=b1_fm[:, mo:mo + 1]), inc=S.sc)
                            astate[f"ff_sc{bp}"] = scv
                        w1f = P.do('t', lambda t: t.nop(), inc=S.wfr)
                        astate[f"w1free{wp}"] = w1f
                    soft_barrier()
                if s_i == tseq and f_i == 0 and "ahff" in tap_t:
                    P.w('y', S.sc, S.sc.v)
                    P.do('y', lambda s: s.dma_start(tap_t["ahff"].ap(), hff[:]), inc=S.out, n=16)
                    P.w('t', S.out, S.out.v)
                    P.w('v', S.out, S.out.v)
                    P.w('s', S.out, S.out.v)
                    P.w('g', S.out, S.out.v)
                with ExitStack() as actx:
                    en = actx.enter_context
                    pM = [en(nc.psum_tensor(un(f"pM{i}"), [128, 512], F32)) for i in range(8)]
                    w2v = None
                    for khs in range(8):
                        wp = khs % 2
                        P.w('g', S.wfr, astate.get("wo_free", 0))
                        P.w('g', S.wfr, astate.get(f"w2free{wp}", 0))
                        P.do('g', lambda g, khs=khs, wp=wp: g.dma_start(
                            w2slot[wp].ap().rearrange("p (k n) -> p k n", n=1024),
                            w2_r[:, khs * 4:(khs + 1) * 4, :]), inc=S.wldp[wp], n=16)
                        w2_ld = S.wldp[wp].v
                        P.w('t', S.wldp[wp], w2_ld)
                        for k4 in range(4):
                            kh = khs * 4 + k4
                            for mo in range(KD):
                                w2v = P.do('t', lambda t, k4=k4, kh=kh, mo=mo, wp=wp: t.matmul(
                                    pM[mo][:],
                                    w2slot[wp][:, k4 * 1024 + mo * 128: k4 * 1024 + (mo + 1) * 128],
                                    hff[:, kh * 512:(kh + 1) * 512],
                                    start=(kh == 0), stop=(kh == KFF - 1)),
                                    inc=(S.mm if (kh == KFF - 1 and mo == KD - 1) else None))
                        w2f = P.do('t', lambda t: t.nop(), inc=S.wfr)
                        astate[f"w2free{wp}"] = w2f
                    P.w('v', S.mm, w2v)
                    for mo in range(KD):
                        P.do('v', lambda v, mo=mo, f_i=f_i: v.scalar_tensor_tensor(
                            pre2[:, mo * 512:(mo + 1) * 512], pM[mo][:],
                            fmv_t["b2"][:, mo:mo + 1],
                            x1ln[:, mo * 2048 + f_i * 512: mo * 2048 + (f_i + 1) * 512],
                            op0=ALU.add, op1=ALU.add), inc=(S.ev if mo == KD - 1 else None))
                    soft_barrier()
                    if s_i == tseq and f_i == 0 and "apre2" in tap_t:
                        P.w('y', S.ev, S.ev.v)
                        P.do('y', lambda s: s.dma_start(tap_t["apre2"].ap(), pre2[:]), inc=S.out, n=16)
                        P.w('t', S.out, S.out.v)
                        P.w('v', S.out, S.out.v)
                        P.w('s', S.out, S.out.v)
                        P.w('g', S.out, S.out.v)
                with ExitStack() as actx:
                    en = actx.enter_context
                    ps = NS(mean=en(nc.psum_tensor(un("fmean"), [1, 512], F32)),
                            sq=en(nc.psum_tensor(un("fsq"), [1, 512], F32)),
                            rstd=en(nc.psum_tensor(un("frstd"), [128, 512], F32)),
                            mmul=en(nc.psum_tensor(un("fmmul"), [128, 512], F32)))
                    st = {}
                    P.w('y', S.ev, astate["wo_done_ev"] if f_i == 0 else 0)
                    P.w('y', S.ev, astate.get("joinx_free", 0))
                    if astate.get("joinx_store", 0):
                        P.w('y', astate["joinx_sem"], astate["joinx_store"])
                    lv = P.do('y', lambda s, fbase=fbase: s.dma_start(
                        joinx.ap().rearrange("p (k t) -> p k t", k=KD),
                        src3[:, :, fbase:fbase + 512]), inc=S.xld, n=16)
                    d = emit_ln_tile(ps, lambda kd: pre2[:, kd * 512:(kd + 1) * 512], st,
                                     eps_ap=eps12, keep_f32=False,
                                     gains=(fmv_t["g2"], fmv_t["be2"]),
                                     dst_gain=lambda kd: pre2[:, kd * 512:(kd + 1) * 512])
                    P.w('v', S.ev, d)
                    P.w('v', S.xld, lv)
                    jv = None
                    for kd in range(KD):
                        jv = P.do('v', lambda v, kd=kd: v.tensor_add(
                            joinx[:, kd * 512:(kd + 1) * 512], pre2[:, kd * 512:(kd + 1) * 512],
                            joinx[:, kd * 512:(kd + 1) * 512]), inc=(S.ev if kd == KD - 1 else None))
                    astate["joinx_free"] = jv
                    P.w('y', S.ev, jv)
                    osem = S.out if dst3 is out3 else S.xst
                    P.do('y', lambda s, fbase=fbase: s.dma_start(
                        dst3[:, :, fbase:fbase + 512],
                        joinx.ap().rearrange("p (k t) -> p k t", k=KD)), inc=osem, n=16)
                    if dbg_key is not None:
                        P.do('y', lambda s, fbase=fbase, k=dbg_key: s.dma_start(
                            dbg3[k][:, :, fbase:fbase + 512],
                            joinx.ap().rearrange("p (k t) -> p k t", k=KD)), inc=S.out, n=16)
                    astate["joinx_sem"], astate["joinx_store"] = osem, osem.v
                    soft_barrier()

    chain = []
    if "ffn1" in stages:
        chain.append(("ffn1", emit_ffn))
    if "attn" in stages:
        chain.append(("attn", emit_attn))
    if "ffn2" in stages:
        chain.append(("ffn2", emit_ffn))
    bufs = [x_in3]
    internals = [xr1_3, xr2_3]
    for i in range(len(chain)):
        bufs.append(out3 if i == len(chain) - 1 else internals[i])
    for i, (nm, fn) in enumerate(chain):
        P.w('y', S.xst, S.xst.v)
        fn(bufs[i], bufs[i + 1], dbg_key=(nm if nm in dbg3 else None))

    if "mask" in tap_t:
        P.w('y', S.xst, S.xst.v)
        P.do('y', lambda s: s.dma_start(tap_t["mask"].ap(), maskT.ap()), inc=S.out, n=16)
        P.w('t', S.out, S.out.v)
        P.w('v', S.out, S.out.v)
        P.w('s', S.out, S.out.v)
        P.w('g', S.out, S.out.v)
    P.w('y', S.out, S.out.v)

    with nc.Block() as block:
        @block.tensor
        def _(t):
            for f in P.ops['t']:
                f(t)

        @block.vector
        def _(v):
            for f in P.ops['v']:
                f(v)

        @block.scalar
        def _(s):
            for f in P.ops['s']:
                f(s)

        @block.gpsimd
        def _(g):
            for f in P.ops['g']:
                f(g)

        @block.sync
        def _(s):
            for f in P.ops['y']:
                f(s)

    return nc


def _in_maps(inputs, nc_dummy=None):
    x = inputs["L0_attn_x"]
    consts = dict(cid32=np.eye(128, dtype=np.float32))
    wmap = dict(
        Wq=inputs["L0_attn_Wq"], Wk=inputs["L0_attn_Wk"], Wv=inputs["L0_attn_Wv"],
        Wo=inputs["L0_attn_Wo"], W1=inputs["L0_attn_W1"], W2=inputs["L0_attn_W2"],
        bq=inputs["L0_attn_bq"], bk=inputs["L0_attn_bk"], bv=inputs["L0_attn_bv"],
        bo=inputs["L0_attn_bo"], b1=inputs["L0_attn_b1"], b2=inputs["L0_attn_b2"],
        g1=inputs["L0_attn_g1"], be1=inputs["L0_attn_be1"],
        g2=inputs["L0_attn_g2"], be2=inputs["L0_attn_be2"],
        gate_w=inputs["L0_ffn_gate_w"], gate_bias=inputs["L0_ffn_gate_bias"],
        all_g=inputs["L0_ffn_all_gate_proj"], all_u=inputs["L0_ffn_all_up_proj"],
        all_d=inputs["L0_ffn_all_down_proj"],
        sh_g=inputs["L0_ffn_sh_gate"], sh_u=inputs["L0_ffn_sh_up"],
        sh_d=inputs["L0_ffn_sh_down"],
    )
    wmap = {k: np.ascontiguousarray(v, dtype=np.float32) for k, v in wmap.items()}
    maps = []
    for c in range(8):
        xc = np.ascontiguousarray(np.asarray(x[2 * c:2 * c + 2], dtype=np.float32).reshape(T, D).T)
        maps.append(dict(x_fm=xc, **wmap, **consts))
    return maps


def kernel(**inputs):
    nc = build_nc()
    maps = _in_maps(inputs)
    res = run_bass_kernel_spmd(nc, maps, core_ids=list(range(8)))
    outs = [res.results[c]["out_fm"].T.reshape(2, SEQ, D) for c in range(8)]
    return np.ascontiguousarray(np.concatenate(outs, axis=0), dtype=np.float32)


# version 04
# speedup vs baseline: 2.6750x; 2.6422x over previous
import numpy as np
from contextlib import ExitStack

import concourse.bass as bass
import concourse.mybir as mybir
from concourse.bass_utils import run_bass_kernel_spmd
from concourse.bass_core_rust import IndirectOffsetOnAxis

F32 = mybir.dt.float32
BF16 = mybir.dt.bfloat16
U8 = mybir.dt.uint8
AF = mybir.ActivationFunctionType
ALU = mybir.AluOpType
AXX = mybir.AxisListType.X

D = 1024
KD = 8
T = 4096
E = 64
HE = 768
KH = 6
SH = 1536
KSH = 12
FF = 4096
KFF = 32
NH = 16
SEQ = 2048
QTOK = 1024
NQ = T // QTOK
NTT = QTOK // 512


class Sem:
    def __init__(self, nc, name):
        self.h = nc.alloc_semaphore(name)
        self.v = 0


class Prog:
    def __init__(self):
        self.ops = {k: [] for k in "tvsgy"}

    def w(self, e, sem, val):
        if val and val > 0:
            self.ops[e].append(lambda eng, s=sem.h, v=val: eng.wait_ge(s, v))

    def do(self, e, build, inc=None, n=1):
        if inc is None:
            self.ops[e].append(build)
            return None
        inc.v += n
        after = inc.v
        self.ops[e].append(lambda eng, b=build, s=inc.h, m=n: b(eng).then_inc(s, m))
        return after


class NS:
    def __init__(self, **kw):
        self.__dict__.update(kw)


def build_nc(stages=("ffn1", "attn", "ffn2"), dbg=(), taps=(), tseq=0, sparse=True):
    nc = bass.Bass("TRN2", target_bir_lowering=False)
    _uid = [0]

    def un(nm):
        _uid[0] += 1
        return f"{nm}_{_uid[0]}"

    x_in = nc.dram_tensor("x_fm", [D, T], F32, kind="ExternalInput")
    wq = nc.dram_tensor("Wq", [D, D], F32, kind="ExternalInput")
    wk = nc.dram_tensor("Wk", [D, D], F32, kind="ExternalInput")
    wv = nc.dram_tensor("Wv", [D, D], F32, kind="ExternalInput")
    wo = nc.dram_tensor("Wo", [D, D], F32, kind="ExternalInput")
    w1 = nc.dram_tensor("W1", [D, FF], F32, kind="ExternalInput")
    w2 = nc.dram_tensor("W2", [FF, D], F32, kind="ExternalInput")
    vecs = {nm: nc.dram_tensor(nm, [D], F32, kind="ExternalInput")
            for nm in ("bq", "bk", "bv", "bo", "b2", "g1", "be1", "g2", "be2")}
    b1 = nc.dram_tensor("b1", [FF], F32, kind="ExternalInput")
    gw = nc.dram_tensor("gate_w", [D, E], F32, kind="ExternalInput")
    gbias = nc.dram_tensor("gate_bias", [E], F32, kind="ExternalInput")
    allg = nc.dram_tensor("all_g", [E, D, HE], F32, kind="ExternalInput")
    allu = nc.dram_tensor("all_u", [E, D, HE], F32, kind="ExternalInput")
    alld = nc.dram_tensor("all_d", [E, HE, D], F32, kind="ExternalInput")
    shg = nc.dram_tensor("sh_g", [D, SH], F32, kind="ExternalInput")
    shu = nc.dram_tensor("sh_u", [D, SH], F32, kind="ExternalInput")
    shd = nc.dram_tensor("sh_d", [SH, D], F32, kind="ExternalInput")
    cid32 = nc.dram_tensor("cid32", [128, 128], F32, kind="ExternalInput")
    custrict = nc.dram_tensor("custrict", [128, 128], F32, kind="ExternalInput")
    ckeybase = nc.dram_tensor("ckeybase", [1, 64], F32, kind="ExternalInput")

    out_fm = nc.dram_tensor("out_fm", [D, T], F32, kind="ExternalOutput")
    dbg_t = {name: nc.dram_tensor("dbg_" + name, [D, T], F32, kind="ExternalOutput")
             for name in dbg}
    tap_t = {}
    if "xln" in taps:
        tap_t["xln"] = nc.dram_tensor("tap_xln", [128, 8192], BF16, kind="ExternalOutput")
    if "xst3" in taps:
        tap_t["xst3"] = nc.dram_tensor("tap_xst3", [128, 4096], F32, kind="ExternalOutput")
    if "xr1fin" in taps:
        tap_t["xr1fin"] = nc.dram_tensor("tap_xr1fin", [1024, 1024], F32, kind="ExternalOutput")
    if "scores" in taps:
        tap_t["scores"] = nc.dram_tensor("tap_scores", [128, 512], F32, kind="ExternalOutput")
    if "scraw" in taps:
        tap_t["scraw"] = nc.dram_tensor("tap_scraw", [128, 512], F32, kind="ExternalOutput")
    if "mask" in taps:
        tap_t["mask"] = nc.dram_tensor("tap_mask", [E, T], F32, kind="ExternalOutput")
    if "yacc" in taps:
        tap_t["yacc"] = nc.dram_tensor("tap_yacc", [128, 8192], F32, kind="ExternalOutput")
    if "hbuf" in taps:
        tap_t["hbuf"] = nc.dram_tensor("tap_hbuf", [128, 3072], BF16, kind="ExternalOutput")
    if "mb" in taps:
        tap_t["mb"] = nc.dram_tensor("tap_mb", [128, 1024], BF16, kind="ExternalOutput")
    if "w6" in taps:
        tap_t["w6"] = nc.dram_tensor("tap_w6", [128, 256], F32, kind="ExternalOutput")
        tap_t["s6"] = nc.dram_tensor("tap_s6", [128, 256], F32, kind="ExternalOutput")
        tap_t["kb"] = nc.dram_tensor("tap_kb", [128, 320], F32, kind="ExternalOutput")
    if "xsl" in taps:
        tap_t["xsl"] = nc.dram_tensor("tap_xsl", [2048, D], BF16, kind="ExternalOutput")
    if "od" in taps:
        tap_t["od"] = nc.dram_tensor("tap_od", [2048, D], BF16, kind="ExternalOutput")
    if "yq0" in taps:
        tap_t["yq0"] = nc.dram_tensor("tap_yq0", [128, 8192], F32, kind="ExternalOutput")
    if "ysh" in taps:
        tap_t["ysh"] = nc.dram_tensor("tap_ysh", [128, 8192], F32, kind="ExternalOutput")
    if "og0" in taps:
        tap_t["og0"] = nc.dram_tensor("tap_og0", [128, 6144], BF16, kind="ExternalOutput")
        tap_t["acc0"] = nc.dram_tensor("tap_acc0", [128, 1024], F32, kind="ExternalOutput")
    for nm, shape, dt_ in (("aln", [128, 16384], BF16), ("akn", [128, 16384], BF16),
                           ("avn", [128, 16384], BF16), ("akv", [128, 512], BF16),
                           ("aqn", [128, 8192], BF16), ("actx", [128, 16384], BF16),
                           ("amid", [128, 16384], BF16), ("ahff", [128, 16384], BF16),
                           ("apre2", [128, 4096], F32)):
        if nm in taps:
            tap_t[nm] = nc.dram_tensor("tap_" + nm, shape, dt_, kind="ExternalOutput")

    CAP = 1024
    NSLOT = E * CAP
    xslots = nc.dram_tensor("xslots", [NSLOT + 128, D], BF16, kind="Internal")
    odram = nc.dram_tensor("odram", [NSLOT + 128, D], BF16, kind="Internal")
    xlnd = nc.dram_tensor("xlnd", [D, T], BF16, kind="Internal")
    xlnd3 = xlnd.ap().rearrange("(k p) t -> p k t", p=128)
    xr1 = nc.dram_tensor("xr1", [D, T], F32, kind="Internal")
    xr2 = nc.dram_tensor("xr2", [D, T], F32, kind="Internal")
    maskT = nc.dram_tensor("maskTd", [E, T], F32, kind="Internal")

    allg_r = allg.ap().rearrange("e (k p) h -> e p k h", p=128)
    allu_r = allu.ap().rearrange("e (k p) h -> e p k h", p=128)
    alld_r = alld.ap().rearrange("e (k p) h -> e p k h", p=128)
    shg_r = shg.ap().rearrange("(k p) h -> p k h", p=128)
    shu_r = shu.ap().rearrange("(k p) h -> p k h", p=128)
    shd_r = shd.ap().rearrange("(k p) h -> p k h", p=128)
    gw_r = gw.ap().rearrange("(k p) e -> p k e", p=128)
    wr = {k: w.ap().rearrange("(k p) n -> p k n", p=128)
          for k, w in (("q", wq), ("k", wk), ("v", wv), ("o", wo))}
    w1_r = w1.ap().rearrange("(k p) n -> p k n", p=128)
    w2_r = w2.ap().rearrange("(s p) n -> p s n", p=128)
    x_in3 = x_in.ap().rearrange("(k p) t -> p k t", p=128)
    xr1_3 = xr1.ap().rearrange("(k p) t -> p k t", p=128)
    xr2_3 = xr2.ap().rearrange("(k p) t -> p k t", p=128)
    out3 = out_fm.ap().rearrange("(k p) t -> p k t", p=128)
    dbg3 = {k: v.ap().rearrange("(k p) t -> p k t", p=128) for k, v in dbg_t.items()}

    def addr_of(h):
        return nc.lookup_mloc(h).addr

    cid32_sb = nc.alloc_sbuf_tensor("cid32_sb", [128, 128], F32)
    cid16_sb = nc.alloc_sbuf_tensor("cid16_sb", [128, 128], BF16)
    onescol_bf = nc.alloc_sbuf_tensor("onescol_bf", [128, 8], BF16)
    onesrow_f = nc.alloc_sbuf_tensor("onesrow_f", [1, 128], F32)
    onesrow_bf = nc.alloc_sbuf_tensor("onesrow_bf", [1, 128], BF16)
    eps5 = nc.alloc_sbuf_tensor("eps5", [128, 1], F32)
    eps12 = nc.alloc_sbuf_tensor("eps12", [128, 1], F32)
    bqkv_bc = nc.alloc_sbuf_tensor("bqkv_bc", [128, 3072], F32)
    gbias_bc = nc.alloc_sbuf_tensor("gbias_bc", [128, 64], F32)
    gw_sb = nc.alloc_sbuf_tensor("gw_sb", [128, 512], F32)
    fmv_t = {nm: nc.alloc_sbuf_tensor(nm + "_fm", [128, 8], F32)
             for nm in ("bo", "b2", "g1", "be1", "g2", "be2")}
    b1_fm = nc.alloc_sbuf_tensor("b1_fm", [128, 32], F32)
    ustrict_bf = nc.alloc_sbuf_tensor("ustrict_bf", [128, 128], BF16)
    keybase_bc = nc.alloc_sbuf_tensor("keybase_bc", [128, 64], F32)
    base_row = nc.alloc_sbuf_tensor("base_row", [1, 64], F32)
    w6_all = nc.alloc_sbuf_tensor("w6_all", [128, 256], F32)
    slot6_all = nc.alloc_sbuf_tensor("slot6_all", [128, 256], mybir.dt.int32)
    onehot_f = nc.alloc_sbuf_tensor("onehot_f", [128, 64], F32)
    onehot_bf = nc.alloc_sbuf_tensor("onehot_bf", [128, 64], BF16)
    cnt_sb = nc.alloc_sbuf_tensor("cnt_sb", [128, 64], F32)
    keyt = nc.alloc_sbuf_tensor("keyt", [128, 128], F32)
    rt8 = nc.alloc_sbuf_tensor("rt8", [128, 64], F32)
    rt8i = nc.alloc_sbuf_tensor("rt8i", [128, 8], mybir.dt.int32)
    drain_scr = nc.alloc_sbuf_tensor("drain_scr", [128, 128], F32)

    arn_w = nc.alloc_sbuf_tensor("arn_w", [128, 73728], U8)
    arn_xln = nc.alloc_sbuf_tensor("arn_xln", [128, 16384], U8)
    arn_y = nc.alloc_sbuf_tensor("arn_y", [128, 32768], U8)
    arn_a = nc.alloc_sbuf_tensor("arn_a", [128, 32768], U8)
    arn_b = nc.alloc_sbuf_tensor("arn_b", [128, 24576], U8)
    AW, AX, AY, AA, AB = (addr_of(h) for h in (arn_w, arn_xln, arn_y, arn_a, arn_b))

    def at(name, shape, dtype, base, off):
        return nc.alloc_sbuf_tensor_at(name, shape, dtype, offset=base + off)

    gbuf = [at(f"gbuf{p}", [128, 6144], BF16, AW, p * 36864) for p in (0, 1)]
    ubuf = [at(f"ubuf{p}", [128, 6144], BF16, AW, p * 36864 + 12288) for p in (0, 1)]
    dbuf = [at(f"dbuf{p}", [128, 6144], BF16, AW, p * 36864 + 24576) for p in (0, 1)]
    shg_sb = at("shg_sb", [128, 12288], BF16, AW, 0)
    shu_sb = at("shu_sb", [128, 12288], BF16, AW, 24576)
    shd_sb = at("shd_sb", [128, 12288], BF16, AW, 49152)
    xln = at("xln", [128, 8192], BF16, AX, 0)
    yacc = at("yacc", [128, 8192], F32, AY, 0)
    xstage = at("xstage", [128, 4096], F32, AA, 0)
    astage = at("astage", [128, 1024], F32, AA, 16384)
    xbx = at("xbx", [128, 1024], BF16, AA, 20480)
    xsqx = at("xsqx", [128, 1024], BF16, AA, 22528)
    scores_tm = at("scores_tm", [128, 512], F32, AA, 24576)
    tk_m8 = at("tk_m8", [128, 8], F32, AA, 26624)
    tk_sel = at("tk_sel", [128, 64], F32, AA, 26656)
    tk_row = at("tk_row", [128, 8], F32, AA, 26912)
    maskfm_ch = at("maskfm_ch", [64, 256], F32, AA, 26944)
    mbcast = [at(f"mbcast{p}", [128, 1024], BF16, AB, p * 2048) for p in (0, 1)]
    mrow = [at(f"mrow{p}", [1, 1024], BF16, AB, 4096 + p * 2048) for p in (0, 1)]
    hbuf = [at(f"hbuf{p}", [128, 3072], BF16, AB, 8192 + p * 6144) for p in (0, 1)]
    silug = [at(f"silug{p}", [128, 512], BF16, AB, 20480 + p * 1024) for p in (0, 1)]
    umb = [at(f"umb{p}", [128, 512], BF16, AB, 22528 + p * 1024) for p in (0, 1)]
    lnrow = at("lnrow", [1, 2048], F32, AB, 16384)

    x1ln = at("x1ln", [128, 16384], BF16, AY, 0)
    kn_tm = at("kn_tm", [128, 16384], BF16, AW, 36864)
    ctx_fm = at("ctx_fm", [128, 16384], BF16, AW, 36864)
    v_tm = at("v_tm", [128, 16384], BF16, AA, 0)
    qn_fm = at("qn_fm", [128, 8192], BF16, AX, 0)
    pre1 = at("pre1", [128, 4096], F32, AX, 0)
    joinx = at("joinx", [128, 4096], F32, AX, 0)
    wslot = [at(f"wslot{p}", [128, 8192], BF16, AW, p * 16384) for p in (0, 1)]
    w1slot = [at(f"w1slot{p}", [128, 4096], BF16, AW, p * 8192) for p in (0, 1)]
    w2slot = [at(f"w2slot{p}", [128, 4096], BF16, AW, 16384 + p * 8192) for p in (0, 1)]
    hff = at("hff", [128, 16384], BF16, AA, 0)
    kf = at("kf", [128, 1024], F32, AB, 0)
    rbuf = at("rbuf", [128, 1024], F32, AB, 4096)
    exbuf = at("exbuf", [128, 1024], F32, AB, 8192)
    qn_ch = at("qn_ch", [128, 1024], BF16, AB, 12288)
    nrmA = at("nrmA", [128, 16], F32, AB, 14336)
    nrmB = at("nrmB", [128, 16], F32, AB, 14400)
    kv_sb = at("kv_sb", [128, 512], BF16, AB, 14848)
    pre2 = at("pre2", [128, 4096], F32, AB, 0)
    xtm2 = [at(f"xtm2_{p}", [128, 1024], BF16, AB, p * 2048) for p in (0, 1)]
    xg_fm = [at("xg_fm0", [128, 8192], BF16, AX, 0), at("xg_fm1", [128, 8192], BF16, AA, 0)]
    orow = [at(f"orow{p}", [128, 1024], BF16, AA, 16384 + p * 2048) for p in (0, 1)]
    og = [at(f"og{p}", [128, 6144], BF16, AA, p * 12288) for p in (0, 1)]
    acc_tm2 = [at(f"acc_tm{p}", [128, 1024], F32, AB, 4096 + p * 4096) for p in (0, 1)]
    acc_bf = [at(f"acc_bf{p}", [128, 1024], BF16, AB, 12288 + p * 2048) for p in (0, 1)]
    joins = at("joins", [128, 4096], F32, AX, 0)
    idx6 = [nc.alloc_sbuf_tensor(f"idx6_{p}", [128, 8], mybir.dt.int32) for p in (0, 1)]
    gid6 = [nc.alloc_sbuf_tensor(f"gid6_{p}", [128, 8], mybir.dt.int32) for p in (0, 1)]

    S = NS(wld=Sem(nc, "wld"), wldp=[Sem(nc, "wld0"), Sem(nc, "wld1")],
           wldsh=Sem(nc, "wldsh"), xld=Sem(nc, "xld"), xst=Sem(nc, "xst"),
           out=Sem(nc, "outst"), mm=Sem(nc, "mm"), ev=Sem(nc, "ev"),
           sc=Sem(nc, "sc"), wfr=Sem(nc, "wfr"), mr=Sem(nc, "mr"),
           xg=[Sem(nc, "xg0"), Sem(nc, "xg1")], gt=[Sem(nc, "gt0"), Sem(nc, "gt1")],
           xs=[Sem(nc, "xs0"), Sem(nc, "xs1")], xsg=Sem(nc, "xsg"), gtg=Sem(nc, "gtg"))
    mrt = nc.monotonic_semaphore(0, engine=nc.gpsimd)
    P = Prog()

    def soft_barrier():
        mv, ev, sv = S.mm.v, S.ev.v, S.sc.v
        P.w('t', S.ev, ev); P.w('t', S.sc, sv)
        P.w('v', S.mm, mv); P.w('v', S.sc, sv)
        P.w('s', S.mm, mv); P.w('s', S.ev, ev)

    def dve_drain(rounds=6):
        dv = 0
        for _ in range(rounds):
            dv = P.do('v', lambda v: v.tensor_copy(drain_scr[:], drain_scr[:]),
                      inc=S.ev)
            P.w('v', S.ev, dv)
        return P.do('v', lambda v: v.nop(), inc=S.ev)

    with ExitStack() as pctx:
        row = pctx.enter_context(nc.sbuf_tensor("p_row", [1, 1024], F32))
        vtile = pctx.enter_context(nc.sbuf_tensor("p_vt", [32, 128], F32))
        ppr = pctx.enter_context(nc.psum_tensor(un("ppr"), [128, 512], F32))
        ppr2 = pctx.enter_context(nc.psum_tensor(un("ppr2"), [128, 512], F32))

        def yld(build):
            return P.do('y', build, inc=S.xld, n=16)

        yld(lambda s: s.dma_start(cid32_sb[:], cid32.ap()))
        yld(lambda s: s.dma_start(gw_sb.ap().rearrange("p (k e) -> p k e", k=KD), gw_r))
        P.w('y', S.xld, S.xld.v)
        P.do('g', lambda g: g.dma_start(cid16_sb[:], cid32.ap()), inc=S.wldsh, n=16)
        P.do('g', lambda g: g.dma_start(ustrict_bf[:], custrict.ap()), inc=S.wldsh, n=16)
        P.do('v', lambda v: v.memset(eps5[:], 1e-5))
        P.do('v', lambda v: v.memset(eps12[:], 1e-12))
        P.do('v', lambda v: v.memset(drain_scr[:], 0.0))
        P.do('v', lambda v: v.memset(onesrow_f[:], 1.0))
        P.do('v', lambda v: v.memset(onesrow_bf[:], 1.0))
        ms = P.do('v', lambda v: v.memset(onescol_bf[:], 1.0), inc=S.ev)
        P.w('t', S.ev, ms)

        for vec, off in ((vecs["bq"], 0), (vecs["bk"], 1024), (vecs["bv"], 2048)):
            P.w('y', S.mm, S.mm.v)
            lv = yld(lambda s, vec=vec: s.dma_start(row[:], vec.ap().rearrange("(a n) -> a n", a=1)))
            P.w('t', S.xld, lv)
            for h in range(2):
                mv = P.do('t', lambda t, h=h: t.matmul(ppr[:], onesrow_f[:],
                                                       row[:, h * 512:(h + 1) * 512],
                                                       start=True, stop=True), inc=S.mm)
                P.w('v', S.mm, mv)
                evv = P.do('v', lambda v, h=h, off=off: v.tensor_copy(
                    bqkv_bc[:, off + h * 512: off + (h + 1) * 512], ppr[:]), inc=S.ev)
                P.w('t', S.ev, evv)
        P.w('y', S.mm, S.mm.v)
        lv = yld(lambda s: s.dma_start(row[:, 0:64], gbias.ap().rearrange("(a n) -> a n", a=1)))
        P.w('t', S.xld, lv)
        mv = P.do('t', lambda t: t.matmul(ppr[:, 0:64], onesrow_f[:], row[:, 0:64],
                                          start=True, stop=True), inc=S.mm)
        P.w('v', S.mm, mv)
        evv = P.do('v', lambda v: v.tensor_copy(gbias_bc[:], ppr[:, 0:64]), inc=S.ev)
        P.w('t', S.ev, evv)
        P.w('y', S.mm, S.mm.v)
        lv = yld(lambda s: s.dma_start(row[:, 0:64], ckeybase.ap()))
        P.w('t', S.xld, lv)
        mv = P.do('t', lambda t: t.matmul(ppr[:, 0:64], onesrow_f[:], row[:, 0:64],
                                          start=True, stop=True), inc=S.mm)
        P.w('v', S.mm, mv)
        evv = P.do('v', lambda v: v.tensor_copy(keybase_bc[:], ppr[:, 0:64]), inc=S.ev)
        P.w('t', S.ev, evv)

        fmv = [(vecs["bo"], fmv_t["bo"], 8), (vecs["b2"], fmv_t["b2"], 8),
               (vecs["g1"], fmv_t["g1"], 8), (vecs["be1"], fmv_t["be1"], 8),
               (vecs["g2"], fmv_t["g2"], 8), (vecs["be2"], fmv_t["be2"], 8),
               (b1, b1_fm, 32)]
        for vec, dst, c in fmv:
            P.w('y', S.mm, S.mm.v)
            lv = yld(lambda s, vec=vec, c=c: s.dma_start(
                vtile[0:c, :], vec.ap().rearrange("(c p) -> c p", p=128)))
            P.w('t', S.xld, lv)
            mv = P.do('t', lambda t, c=c: t.transpose(ppr2[:, 0:c], vtile[0:c, :],
                                                      cid32_sb[0:c, 0:c]), inc=S.mm)
            P.w('v', S.mm, mv)
            evv = P.do('v', lambda v, dst=dst, c=c: v.tensor_copy(dst[:, 0:c], ppr2[:, 0:c]),
                       inc=S.ev)
            P.w('t', S.ev, evv)

        zv = P.do('v', lambda v: v.memset(xtm2[0][:], 0.0), inc=S.ev)
        P.w('y', S.ev, zv)
        P.do('y', lambda s: s.dma_start(
            odram.ap().rearrange("(a p) d -> p a d", p=128)[:, NSLOT // 128:NSLOT // 128 + 1, :],
            xtm2[0].ap().rearrange("p (a d) -> p a d", a=1)), inc=S.xst, n=16)
        P.w('t', S.wldsh, S.wldsh.v); P.w('v', S.wldsh, S.wldsh.v); P.w('s', S.wldsh, S.wldsh.v)
        P.w('v', S.xld, S.xld.v); P.w('s', S.xld, S.xld.v)
        P.w('y', S.xld, S.xld.v)
        soft_barrier()

    def emit_ln_tile(ps, src, state, *, eps_ap, dst_bf=None, keep_f32=False,
                     gains=None, dst_gain=None):
        n = 512
        stats_vals = []
        P.w('v', S.mm, state.get("bc_free", 0))
        for kd in range(KD):
            slot = (kd % 2) * 512
            if kd >= 2:
                P.w('v', S.mm, stats_vals[kd - 2])
            P.do('v', lambda v, kd=kd, slot=slot: v.tensor_copy(xbx[:, slot:slot + n], src(kd)))
            evv = P.do('v', lambda v, slot=slot: v.tensor_mul(
                xsqx[:, slot:slot + n], xbx[:, slot:slot + n], xbx[:, slot:slot + n]), inc=S.ev)
            P.w('t', S.ev, evv)
            if kd == 0:
                P.w('t', S.ev, state.get("row_read", 0))
            P.do('t', lambda t, kd=kd, slot=slot: t.matmul(
                ps.mean[0:1, :], onescol_bf[:, 0:1], xbx[:, slot:slot + n],
                start=(kd == 0), stop=(kd == KD - 1)))
            sv = P.do('t', lambda t, kd=kd, slot=slot: t.matmul(
                ps.sq[0:1, :], onescol_bf[:, 0:1], xsqx[:, slot:slot + n],
                start=(kd == 0), stop=(kd == KD - 1)), inc=S.mm)
            stats_vals.append(sv)
        mu = lnrow[0:1, 0:512]
        msq = lnrow[0:1, 512:1024]
        tmp = lnrow[0:1, 1024:1536]
        rstd = lnrow[0:1, 1536:2048]
        P.w('v', S.mm, stats_vals[-1])
        P.do('v', lambda v: v.tensor_scalar_mul(mu, ps.mean[0:1, :], 1.0 / D))
        P.do('v', lambda v: v.tensor_scalar_mul(msq, ps.sq[0:1, :], 1.0 / D))
        P.do('v', lambda v: v.tensor_mul(tmp, mu, mu))
        rv = P.do('v', lambda v: v.tensor_sub(tmp, msq, tmp), inc=S.ev)
        state["row_read"] = rv
        P.w('s', S.ev, rv)
        scv = P.do('s', lambda s: s.activation(msq, tmp, AF.Sqrt, bias=eps_ap[0:1, 0:1]), inc=S.sc)
        P.w('v', S.sc, scv)
        P.do('v', lambda v: v.reciprocal(rstd, msq))
        rv2 = P.do('v', lambda v: v.tensor_mul(mu, mu, rstd), inc=S.ev)
        P.w('t', S.ev, rv2)
        P.w('t', S.ev, state.get("norm_done", 0))
        P.do('t', lambda t: t.matmul(ps.rstd[:], onesrow_f[:], rstd, start=True, stop=True))
        bcv = P.do('t', lambda t: t.matmul(ps.mmul[:], onesrow_f[:], mu, start=True, stop=True),
                   inc=S.mm)
        state["bc_free"] = bcv
        P.w('v', S.mm, bcv)
        last = 0
        for kd in range(KD):
            a = astage[:, (kd % 2) * 512:(kd % 2) * 512 + n]
            P.do('v', lambda v, kd=kd, a=a: v.tensor_mul(a, src(kd), ps.rstd[:]))
            if keep_f32:
                P.do('v', lambda v, kd=kd, a=a: v.tensor_sub(src(kd), a, ps.mmul[:]))
                last = P.do('v', lambda v, kd=kd: v.tensor_copy(dst_bf(kd), src(kd)), inc=S.ev)
            else:
                P.do('v', lambda v, a=a: v.tensor_sub(a, a, ps.mmul[:]))
                if gains is not None:
                    gt, bt = gains
                    last = P.do('v', lambda v, kd=kd, a=a, gt=gt, bt=bt: v.tensor_scalar(
                        dst_gain(kd), a, gt[:, kd:kd + 1], bt[:, kd:kd + 1],
                        op0=ALU.mult, op1=ALU.add), inc=S.ev)
                else:
                    last = P.do('v', lambda v, kd=kd, a=a: v.tensor_copy(dst_bf(kd), a), inc=S.ev)
        state["norm_done"] = last
        return last

    qcarry = {"store_sem": None, "store_val": 0}

    def emit_ffn(src3, dst3, dbg_key=None):
        for q in range(NQ):
            with ExitStack() as qctx:
                en = qctx.enter_context
                ps = NS(mean=en(nc.psum_tensor(un("pmean"), [1, 512], F32)),
                        sq=en(nc.psum_tensor(un("psq"), [1, 512], F32)),
                        rstd=en(nc.psum_tensor(un("prstd"), [128, 512], F32)),
                        mmul=en(nc.psum_tensor(un("pmmul"), [128, 512], F32)),
                        sc=en(nc.psum_tensor(un("pscore"), [128, 256], F32)),
                        tp=en(nc.psum_tensor(un("ptpm"), [128, 128], F32)))
                state = {}
                mask_last_ev = 0
                for tt in range(NTT):
                    t0 = q * QTOK + tt * 512
                    P.w('y', S.mm, state.get("xstage_free_mm", 0))
                    P.w('y', S.ev, state.get("xstage_free_ev", 0))
                    if qcarry["store_sem"] is not None:
                        P.w('y', qcarry["store_sem"], qcarry["store_val"])
                        qcarry["store_sem"] = None
                    lv = P.do('y', lambda s, t0=t0: s.dma_start(
                        xstage.ap().rearrange("p (k t) -> p k t", k=KD),
                        src3[:, :, t0:t0 + 512]), inc=S.xld, n=16)
                    if q == 3 and tt == 1 and "xst3" in tap_t:
                        P.w('y', S.xld, lv)
                        P.do('y', lambda s: s.dma_start(tap_t["xst3"].ap(), xstage[:]), inc=S.out, n=16)
                        P.w('t', S.out, S.out.v)
                        P.w('v', S.out, S.out.v)
                        P.w('s', S.out, S.out.v)
                        P.w('g', S.out, S.out.v)
                    P.w('v', S.xld, lv)
                    xln_done = emit_ln_tile(
                        ps, lambda kd: xstage[:, kd * 512:(kd + 1) * 512], state,
                        eps_ap=eps5, keep_f32=True,
                        dst_bf=lambda kd, tt=tt: xln[:, kd * 1024 + tt * 512: kd * 1024 + (tt + 1) * 512])
                    P.w('t', S.ev, xln_done)
                    P.w('t', S.sc, state.get("score_evac", 0))
                    gmv = 0
                    for sc4 in range(4):
                        for kd in range(KD):
                            gmv = P.do('t', lambda t, sc4=sc4, kd=kd: t.matmul(
                                ps.sc[:, sc4 * 64:(sc4 + 1) * 64],
                                xstage[:, kd * 512 + sc4 * 128: kd * 512 + (sc4 + 1) * 128],
                                gw_sb[:, kd * 64:(kd + 1) * 64],
                                start=(kd == 0), stop=(kd == KD - 1)),
                                inc=(S.mm if kd == KD - 1 else None))
                    state["xstage_free_mm"] = gmv
                    state["xstage_free_ev"] = xln_done
                    P.w('s', S.mm, gmv)
                    scv = P.do('s', lambda s, tt=tt: s.activation(
                        scores_tm[:, tt * 256:(tt + 1) * 256], ps.sc[:], AF.Sigmoid), inc=S.sc)
                    state["score_evac"] = scv
                    if q == 0 and "scraw" in tap_t:
                        P.w('y', S.sc, scv)
                        P.do('y', lambda s, tt=tt: s.dma_start(
                            tap_t["scraw"].ap()[:, tt * 256:(tt + 1) * 256],
                            scores_tm[:, tt * 256:(tt + 1) * 256]), inc=S.out, n=16)
                    P.w('v', S.sc, scv)
                    for sc4 in range(4):
                        sco = scores_tm[:, tt * 256 + sc4 * 64: tt * 256 + (sc4 + 1) * 64]
                        P.do('v', lambda v, sco=sco: v.tensor_add(tk_sel[:], sco, gbias_bc[:]))
                        m8v = P.do('v', lambda v: v.max(out=tk_m8[:], in_=tk_sel[:]), inc=S.ev)
                        P.w('v', S.ev, m8v)
                        P.do('v', lambda v: v.tensor_scalar(tk_sel[:], tk_sel[:], tk_m8[:, 5:6],
                                                            None, op0=ALU.is_ge))
                        P.do('v', lambda v, sco=sco: v.tensor_mul(tk_sel[:], tk_sel[:], sco))
                        rsv = P.do('v', lambda v: v.reduce_sum(tk_row[:, 0:1], tk_sel[:], axis=AXX),
                                   inc=S.ev)
                        P.w('v', S.ev, rsv)
                        rcv = P.do('v', lambda v: v.reciprocal(tk_row[:, 2:3], tk_row[:, 0:1]),
                                   inc=S.ev)
                        P.w('v', S.ev, rcv)
                        mask_last_ev = P.do('v', lambda v, sco=sco: v.tensor_scalar(
                            sco, tk_sel[:], tk_row[:, 2:3], 1.25, op0=ALU.mult, op1=ALU.mult),
                            inc=S.ev)
                st_vals = []
                P.w('t', S.ev, mask_last_ev)
                for c in range(8):
                    P.w('t', S.ev, state.get("tp_free", 0))
                    mv = P.do('t', lambda t, c=c: t.transpose(
                        ps.tp[0:64, :], scores_tm[:, c * 64:(c + 1) * 64], cid32_sb[:]), inc=S.mm)
                    P.w('v', S.mm, mv)
                    if c >= 2:
                        P.w('v', S.xst, st_vals[c - 1])
                    evv = P.do('v', lambda v, c=c: v.tensor_copy(
                        maskfm_ch[:, (c % 2) * 128:(c % 2 + 1) * 128], ps.tp[0:64, :]), inc=S.ev)
                    state["tp_free"] = evv
                    P.w('y', S.ev, evv)
                    stv = P.do('y', lambda s, c=c, q=q: s.dma_start(
                        maskT.ap()[:, q * QTOK + c * 128: q * QTOK + (c + 1) * 128],
                        maskfm_ch[:, (c % 2) * 128:(c % 2 + 1) * 128]), inc=S.xst, n=16)
                    st_vals.append(stv)
                mask_store_done = S.xst.v
                if q == 0 and "xln" in tap_t:
                    P.w('y', S.ev, S.ev.v)
                    P.do('y', lambda s: s.dma_start(tap_t["xln"].ap(), xln[:]), inc=S.out, n=16)
                    P.w('t', S.out, S.out.v)
                    P.w('v', S.out, S.out.v)
                    P.w('s', S.out, S.out.v)
                    P.w('g', S.out, S.out.v)
                if q == 0 and "scores" in tap_t:
                    P.w('y', S.ev, S.ev.v)
                    P.do('y', lambda s: s.dma_start(tap_t["scores"].ap(), scores_tm[:]), inc=S.out, n=16)
                    P.w('t', S.out, S.out.v)
                    P.w('v', S.out, S.out.v)
                    P.w('s', S.out, S.out.v)
                    P.w('g', S.out, S.out.v)
                soft_barrier()

            with ExitStack() as ectx:
                en = ectx.enter_context
                pG = [en(nc.psum_tensor(un(f"pG{i}"), [128, 512], F32)) for i in (0, 1)]
                pU = [en(nc.psum_tensor(un(f"pU{i}"), [128, 512], F32)) for i in (0, 1)]
                pD = [en(nc.psum_tensor(un(f"pD{i}"), [128, 1024], F32)) for i in (0, 1)]
                glast = {}
                estate = {"hfree": {0: 0, 1: 0}}
                d_evs = {0: 0, 1: 0}

                P.w('g', S.wfr, estate.get("wprev", S.wfr.v))
                P.do('g', lambda g: g.dma_start(shg_sb.ap().rearrange("p (k h) -> p k h", k=KD), shg_r), inc=S.wldsh, n=16)
                P.do('g', lambda g: g.dma_start(shu_sb.ap().rearrange("p (k h) -> p k h", k=KD), shu_r), inc=S.wldsh, n=16)
                P.do('g', lambda g: g.dma_start(shd_sb.ap().rearrange("p (k h) -> p k h", k=KSH), shd_r), inc=S.wldsh, n=16)
                glast["sh"] = S.wldsh.v

                def stream_expert(e, wait_vals):
                    par = e % 2
                    sem = S.wldp[par]
                    for wv_ in wait_vals:
                        P.w('g', S.wfr, wv_)
                    P.do('g', lambda g, e=e, par=par: g.dma_start(
                        gbuf[par].ap().rearrange("p (k h) -> p k h", k=KD), allg_r[e]),
                        inc=sem, n=16)
                    P.do('g', lambda g, e=e, par=par: g.dma_start(
                        ubuf[par].ap().rearrange("p (k h) -> p k h", k=KD), allu_r[e]),
                        inc=sem, n=16)
                    P.do('g', lambda g, e=e, par=par: g.dma_start(
                        dbuf[par].ap().rearrange("p (k h) -> p k h", k=KH), alld_r[e]),
                        inc=sem, n=16)
                    P.w('g', S.mr, estate.get(f"mrfree{par}", 0))
                    P.w('g', S.xst, mask_store_done)
                    P.do('g', lambda g, e=e, par=par, q=q: g.dma_start(
                        mrow[par][:], maskT.ap()[e:e + 1, q * QTOK:(q + 1) * QTOK]),
                        inc=sem, n=16)
                    glast[e] = sem.v

                P.w('t', S.wldsh, glast["sh"])
                shgu_val = None
                for tt in range(NTT):
                    for hc in range(KSH):
                        bp = hc % 2
                        P.w('t', S.sc, estate.get(f"silu_free{bp}", 0))
                        P.w('t', S.sc, estate.get(f"ush_free{bp}", 0))
                        for kd in range(KD):
                            P.do('t', lambda t, tt=tt, hc=hc, kd=kd, bp=bp: t.matmul(
                                pG[bp][:],
                                shg_sb[:, kd * 1536 + hc * 128: kd * 1536 + (hc + 1) * 128],
                                xln[:, kd * 1024 + tt * 512: kd * 1024 + (tt + 1) * 512],
                                start=(kd == 0), stop=(kd == KD - 1)))
                        guv = None
                        for kd in range(KD):
                            guv = P.do('t', lambda t, tt=tt, hc=hc, kd=kd, bp=bp: t.matmul(
                                pU[bp][:],
                                shu_sb[:, kd * 1536 + hc * 128: kd * 1536 + (hc + 1) * 128],
                                xln[:, kd * 1024 + tt * 512: kd * 1024 + (tt + 1) * 512],
                                start=(kd == 0), stop=(kd == KD - 1)),
                                inc=(S.mm if kd == KD - 1 else None))
                        if tt == NTT - 1 and hc == KSH - 1:
                            shgu_val = P.do('t', lambda t: t.nop(), inc=S.wfr)
                        P.w('s', S.mm, guv)
                        sgv = P.do('s', lambda s, bp=bp: s.activation(silug[bp][:], pG[bp][:],
                                                                      AF.Silu), inc=S.sc)
                        estate[f"silu_free{bp}"] = sgv
                        uev = P.do('s', lambda s, bp=bp: s.activation(umb[bp][:], pU[bp][:],
                                                                      AF.Copy, scale=0.5), inc=S.sc)
                        estate[f"ush_free{bp}"] = uev
                        P.w('v', S.sc, uev)
                        if hc == 0:
                            P.w('v', S.mm, estate["hfree"][tt % 2])
                        hv = P.do('v', lambda v, bp=bp, tt=tt, hc=hc: v.tensor_mul(
                            hbuf[tt % 2][:, (hc % KH) * 512:((hc % KH) + 1) * 512],
                            silug[bp][:], umb[bp][:]), inc=S.ev)
                        if hc == KH - 1 or hc == KSH - 1:
                            part = 0 if hc == KH - 1 else 1
                            first = (part == 0)
                            dm_last = None
                            for grp in range(4):
                                gpar = grp % 2
                                P.w('t', S.ev, hv)
                                P.w('t', S.ev, d_evs[gpar])
                                for sub in range(2):
                                    mo = grp * 2 + sub
                                    for kh in range(KH):
                                        khg = part * KH + kh
                                        dm_last = P.do('t', lambda t, tt=tt, gpar=gpar, sub=sub, mo=mo, khg=khg, kh=kh: t.matmul(
                                            pD[gpar][:, sub * 512:(sub + 1) * 512],
                                            shd_sb[:, khg * 1024 + mo * 128: khg * 1024 + (mo + 1) * 128],
                                            hbuf[tt % 2][:, kh * 512:(kh + 1) * 512],
                                            start=(kh == 0), stop=(kh == KH - 1)),
                                            inc=(S.mm if (sub == 1 and kh == KH - 1) else None))
                                P.w('v', S.mm, dm_last)
                                for sub in range(2):
                                    mo = grp * 2 + sub
                                    ysl = yacc[:, mo * 1024 + tt * 512: mo * 1024 + (tt + 1) * 512]
                                    if first:
                                        ev_ = P.do('v', lambda v, gpar=gpar, sub=sub, ysl=ysl: v.tensor_copy(
                                            ysl, pD[gpar][:, sub * 512:(sub + 1) * 512]), inc=S.ev)
                                    else:
                                        ev_ = P.do('v', lambda v, gpar=gpar, sub=sub, ysl=ysl: v.tensor_add(
                                            ysl, pD[gpar][:, sub * 512:(sub + 1) * 512], ysl), inc=S.ev)
                                d_evs[gpar] = ev_
                            estate["hfree"][tt % 2] = dm_last
                shdown_val = P.do('t', lambda t: t.nop(), inc=S.wfr)

                stream_expert(0, [shgu_val])
                stream_expert(1, [shgu_val, shdown_val])

                for e in range(E):
                    par = e % 2
                    P.w('t', S.wldp[par], glast[e])
                    mb_evs = []
                    for tt in range(NTT):
                        P.w('t', S.sc, estate.get(f"silu_free{tt}", 0))
                        P.w('t', S.ev, estate.get(f"um_free{tt}", 0))
                        bv_ = P.do('t', lambda t, tt=tt, par=par: t.matmul(
                            pG[tt][:], onesrow_bf[:], mrow[par][:, tt * 512:(tt + 1) * 512],
                            start=True, stop=True), inc=S.mm)
                        P.w('s', S.mm, bv_)
                        sev = P.do('s', lambda s, tt=tt, par=par: s.activation(
                            mbcast[par][:, tt * 512:(tt + 1) * 512], pG[tt][:], AF.Copy), inc=S.sc)
                        estate[f"silu_free{tt}"] = sev
                        mb_evs.append(sev)
                    mrv = P.do('t', lambda t: t.nop(), inc=S.mr)
                    estate[f"mrfree{par}"] = mrv
                    for tt in range(NTT):
                        for hc in range(KH):
                            bp = hc % 2
                            P.w('t', S.sc, estate.get(f"silu_free{bp}", 0))
                            P.w('t', S.ev, estate.get(f"um_free{bp}", 0))
                            for kd in range(KD):
                                P.do('t', lambda t, tt=tt, hc=hc, kd=kd, bp=bp, par=par: t.matmul(
                                    pG[bp][:],
                                    gbuf[par][:, kd * 768 + hc * 128: kd * 768 + (hc + 1) * 128],
                                    xln[:, kd * 1024 + tt * 512: kd * 1024 + (tt + 1) * 512],
                                    start=(kd == 0), stop=(kd == KD - 1)))
                            guv = None
                            for kd in range(KD):
                                guv = P.do('t', lambda t, tt=tt, hc=hc, kd=kd, bp=bp, par=par: t.matmul(
                                    pU[bp][:],
                                    ubuf[par][:, kd * 768 + hc * 128: kd * 768 + (hc + 1) * 128],
                                    xln[:, kd * 1024 + tt * 512: kd * 1024 + (tt + 1) * 512],
                                    start=(kd == 0), stop=(kd == KD - 1)),
                                    inc=(S.mm if kd == KD - 1 else None))
                            P.w('s', S.mm, guv)
                            sgv = P.do('s', lambda s, bp=bp: s.activation(
                                silug[bp][:], pG[bp][:], AF.Silu), inc=S.sc)
                            estate[f"silu_free{bp}"] = sgv
                            P.w('v', S.mm, guv)
                            P.w('v', S.sc, mb_evs[tt])
                            umv = P.do('v', lambda v, bp=bp, tt=tt, par=par: v.tensor_mul(
                                umb[bp][:], pU[bp][:], mbcast[par][:, tt * 512:(tt + 1) * 512]),
                                inc=S.ev)
                            estate[f"um_free{bp}"] = umv
                            P.w('v', S.sc, sgv)
                            if hc == 0:
                                P.w('v', S.mm, estate["hfree"][tt % 2])
                            hv = P.do('v', lambda v, bp=bp, tt=tt, hc=hc: v.tensor_mul(
                                hbuf[tt % 2][:, hc * 512:(hc + 1) * 512], silug[bp][:], umb[bp][:]),
                                inc=S.ev)
                        dm_last = None
                        P.w('t', S.ev, hv)
                        for grp in range(4):
                            gpar = grp % 2
                            P.w('t', S.ev, d_evs[gpar])
                            for sub in range(2):
                                mo = grp * 2 + sub
                                for kh in range(KH):
                                    dm_last = P.do('t', lambda t, tt=tt, gpar=gpar, sub=sub, mo=mo, kh=kh, par=par: t.matmul(
                                        pD[gpar][:, sub * 512:(sub + 1) * 512],
                                        dbuf[par][:, kh * 1024 + mo * 128: kh * 1024 + (mo + 1) * 128],
                                        hbuf[tt % 2][:, kh * 512:(kh + 1) * 512],
                                        start=(kh == 0), stop=(kh == KH - 1)),
                                        inc=(S.mm if (sub == 1 and kh == KH - 1) else None))
                            P.w('v', S.mm, dm_last)
                            for sub in range(2):
                                mo = grp * 2 + sub
                                ysl = yacc[:, mo * 1024 + tt * 512: mo * 1024 + (tt + 1) * 512]
                                ev_ = P.do('v', lambda v, gpar=gpar, sub=sub, ysl=ysl: v.tensor_add(
                                    ysl, pD[gpar][:, sub * 512:(sub + 1) * 512], ysl), inc=S.ev)
                            d_evs[gpar] = ev_
                        estate["hfree"][tt % 2] = dm_last
                    fv = P.do('t', lambda t: t.nop(), inc=S.wfr)
                    estate[f"wfree{par}"] = fv
                    if e + 2 < E:
                        stream_expert(e + 2, [fv])
                estate["wprev"] = S.wfr.v
                y_done = max(d_evs.values())
                if q == 0 and "yacc" in tap_t:
                    P.w('y', S.ev, S.ev.v)
                    P.do('y', lambda s: s.dma_start(tap_t["yacc"].ap(), yacc[:]), inc=S.out, n=16)
                    P.w('t', S.out, S.out.v)
                    P.w('v', S.out, S.out.v)
                    P.w('s', S.out, S.out.v)
                    P.w('g', S.out, S.out.v)
                if q == 0 and "hbuf" in tap_t:
                    P.w('y', S.ev, S.ev.v)
                    P.do('y', lambda s: s.dma_start(tap_t["hbuf"].ap(), hbuf[1][:]), inc=S.out, n=16)
                    P.w('t', S.out, S.out.v)
                    P.w('v', S.out, S.out.v)
                    P.w('s', S.out, S.out.v)
                    P.w('g', S.out, S.out.v)
                if q == 0 and "mb" in tap_t:
                    P.w('y', S.ev, S.ev.v)
                    P.do('y', lambda s: s.dma_start(tap_t["mb"].ap(), mbcast[1][:]), inc=S.out, n=16)
                    P.w('t', S.out, S.out.v)
                    P.w('v', S.out, S.out.v)
                    P.w('s', S.out, S.out.v)
                    P.w('g', S.out, S.out.v)
                soft_barrier()

            jfree = 0
            for tt in range(NTT):
                t0 = q * QTOK + tt * 512
                P.w('y', S.ev, jfree)
                lv = P.do('y', lambda s, t0=t0: s.dma_start(
                    xstage.ap().rearrange("p (k t) -> p k t", k=KD),
                    src3[:, :, t0:t0 + 512]), inc=S.xld, n=16)
                P.w('v', S.xld, lv)
                P.w('v', S.ev, y_done)
                jv = None
                for kd in range(KD):
                    jv = P.do('v', lambda v, kd=kd, tt=tt: v.tensor_add(
                        xstage[:, kd * 512:(kd + 1) * 512],
                        yacc[:, kd * 1024 + tt * 512: kd * 1024 + (tt + 1) * 512],
                        xstage[:, kd * 512:(kd + 1) * 512]),
                        inc=(S.ev if kd == KD - 1 else None))
                jfree = jv
                P.w('y', S.ev, jv)
                osem = S.out if dst3 is out3 else S.xst
                P.do('y', lambda s, t0=t0: s.dma_start(
                    dst3[:, :, t0:t0 + 512],
                    xstage.ap().rearrange("p (k t) -> p k t", k=KD)), inc=osem, n=16)
                if dbg_key is not None:
                    P.do('y', lambda s, t0=t0, k=dbg_key: s.dma_start(
                        dbg3[k][:, :, t0:t0 + 512],
                        xstage.ap().rearrange("p (k t) -> p k t", k=KD)), inc=S.out, n=16)
                qcarry["store_sem"], qcarry["store_val"] = osem, osem.v
            soft_barrier()



    def emit_ffn_sparse(src3, dst3, dbg_key=None):
        P.do('v', lambda v: v.memset(base_row[:], 0.0))
        brv = P.do('v', lambda v: v.nop(), inc=S.ev)
        rstate = {"base_upd": brv, "dec_free": 0}
        estate = {"hfree": {0: 0, 1: 0}}
        glast = {}

        def stream_w(e):
            par = e % 2
            sem = S.wldp[par]
            P.w('g', S.wfr, estate.get(f"wfree{par}", S.wfr.v if e < 2 else 0))
            P.do('g', lambda g, e=e, par=par: g.dma_start(
                gbuf[par].ap().rearrange("p (k h) -> p k h", k=KD), allg_r[e]),
                inc=sem, n=16)
            P.do('g', lambda g, e=e, par=par: g.dma_start(
                ubuf[par].ap().rearrange("p (k h) -> p k h", k=KD), allu_r[e]),
                inc=sem, n=16)
            P.do('g', lambda g, e=e, par=par: g.dma_start(
                dbuf[par].ap().rearrange("p (k h) -> p k h", k=KH), alld_r[e]),
                inc=sem, n=16)
            glast[e] = sem.v

        stream_w(0)
        stream_w(1)

        for q in range(NQ):
            with ExitStack() as qctx:
                en = qctx.enter_context
                ps = NS(mean=en(nc.psum_tensor(un("pmean"), [1, 512], F32)),
                        sq=en(nc.psum_tensor(un("psq"), [1, 512], F32)),
                        rstd=en(nc.psum_tensor(un("prstd"), [128, 512], F32)),
                        mmul=en(nc.psum_tensor(un("pmmul"), [128, 512], F32)),
                        sc=en(nc.psum_tensor(un("pscore"), [128, 256], F32)),
                        prt=en(nc.psum_tensor(un("prt"), [128, 192], F32)),
                        ptm=en(nc.psum_tensor(un("ptm"), [128, 128], BF16)))
                state = {"xstage_free_mm": rstate.get("prev_gmv", 0),
                         "xstage_free_ev": rstate.get("prev_xln", 0),
                         "score_evac": rstate.get("prev_scv", 0)}
                for tt in range(NTT):
                    t0 = q * QTOK + tt * 512
                    P.w('y', S.mm, state.get("xstage_free_mm", 0))
                    P.w('y', S.ev, state.get("xstage_free_ev", 0))
                    P.w('y', S.ev, rstate.get("dec_free", 0))
                    if qcarry["store_sem"] is not None:
                        P.w('y', qcarry["store_sem"], qcarry["store_val"])
                        qcarry["store_sem"] = None
                    lv = P.do('y', lambda s, t0=t0: s.dma_start(
                        xstage.ap().rearrange("p (k t) -> p k t", k=KD),
                        src3[:, :, t0:t0 + 512]), inc=S.xld, n=16)
                    if q == 3 and tt == 1 and "xst3" in tap_t:
                        P.w('y', S.xld, lv)
                        P.do('y', lambda s: s.dma_start(tap_t["xst3"].ap(), xstage[:]), inc=S.out, n=16)
                        P.w('t', S.out, S.out.v)
                        P.w('v', S.out, S.out.v)
                        P.w('s', S.out, S.out.v)
                        P.w('g', S.out, S.out.v)
                    P.w('v', S.xld, lv)
                    xln_done = emit_ln_tile(
                        ps, lambda kd: xstage[:, kd * 512:(kd + 1) * 512], state,
                        eps_ap=eps5, keep_f32=True,
                        dst_bf=lambda kd, tt=tt: xln[:, kd * 1024 + tt * 512: kd * 1024 + (tt + 1) * 512])
                    P.w('t', S.ev, xln_done)
                    P.w('t', S.sc, state.get("score_evac", 0))
                    gmv = 0
                    for sc4 in range(4):
                        for kd in range(KD):
                            gmv = P.do('t', lambda t, sc4=sc4, kd=kd: t.matmul(
                                ps.sc[:, sc4 * 64:(sc4 + 1) * 64],
                                xstage[:, kd * 512 + sc4 * 128: kd * 512 + (sc4 + 1) * 128],
                                gw_sb[:, kd * 64:(kd + 1) * 64],
                                start=(kd == 0), stop=(kd == KD - 1)),
                                inc=(S.mm if kd == KD - 1 else None))
                    state["xstage_free_mm"] = gmv
                    state["xstage_free_ev"] = xln_done
                    P.w('s', S.mm, gmv)
                    scv = P.do('s', lambda s, tt=tt: s.activation(
                        scores_tm[:, tt * 256:(tt + 1) * 256], ps.sc[:], AF.Sigmoid), inc=S.sc)
                    state["score_evac"] = scv
                    if q == 3 and "scraw" in tap_t:
                        P.w('y', S.sc, scv)
                        P.do('y', lambda s, tt=tt: s.dma_start(
                            tap_t["scraw"].ap()[:, tt * 256:(tt + 1) * 256],
                            scores_tm[:, tt * 256:(tt + 1) * 256]), inc=S.out, n=16)
                        P.w('t', S.out, S.out.v)
                        P.w('v', S.out, S.out.v)
                        P.w('s', S.out, S.out.v)
                        P.w('g', S.out, S.out.v)
                    P.w('v', S.sc, scv)
                    for sc4 in range(4):
                        ch = tt * 4 + sc4
                        chg = q * 8 + ch
                        sco = scores_tm[:, tt * 256 + sc4 * 64: tt * 256 + (sc4 + 1) * 64]
                        P.do('v', lambda v, sco=sco: v.tensor_add(tk_sel[:], sco, gbias_bc[:]))
                        m8v = P.do('v', lambda v: v.max(out=tk_m8[:], in_=tk_sel[:]), inc=S.ev)
                        P.w('v', S.ev, m8v)
                        P.do('v', lambda v: v.tensor_scalar(tk_sel[:], tk_sel[:], tk_m8[:, 5:6],
                                                            None, op0=ALU.is_ge))
                        P.do('v', lambda v: v.tensor_copy(onehot_f[:], tk_sel[:]))
                        ohv = P.do('v', lambda v: v.tensor_copy(onehot_bf[:], tk_sel[:]), inc=S.ev)
                        P.do('v', lambda v, sco=sco: v.tensor_mul(tk_sel[:], tk_sel[:], sco))
                        rsv = P.do('v', lambda v: v.reduce_sum(tk_row[:, 0:1], tk_sel[:], axis=AXX),
                                   inc=S.ev)
                        P.w('v', S.ev, rsv)
                        rcv = P.do('v', lambda v: v.reciprocal(tk_row[:, 2:3], tk_row[:, 0:1]),
                                   inc=S.ev)
                        P.w('v', S.ev, rcv)
                        P.do('v', lambda v, sco=sco: v.tensor_scalar(
                            sco, tk_sel[:], tk_row[:, 2:3], 1.25, op0=ALU.mult, op1=ALU.mult))
                        P.w('t', S.ev, ohv)
                        P.w('t', S.ev, rstate["dec_free"])
                        P.do('t', lambda t: t.matmul(ps.prt[:, 0:64], ustrict_bf[:], onehot_bf[:],
                                                     start=True, stop=True))
                        P.do('t', lambda t: t.matmul(ps.prt[0:1, 128:192], onescol_bf[:, 0:1],
                                                     onehot_bf[:], start=True, stop=True))
                        P.w('t', S.ev, rstate["base_upd"])
                        pmv = P.do('t', lambda t: t.matmul(ps.prt[:, 64:128], onesrow_f[:],
                                                           base_row[:], start=True, stop=True),
                                   inc=S.mm)
                        P.w('v', S.mm, pmv)
                        cv1 = P.do('v', lambda v: v.tensor_copy(cnt_sb[:], ps.prt[:, 0:64]), inc=S.ev)
                        P.w('v', S.ev, cv1)
                        cv2 = P.do('v', lambda v: v.tensor_add(cnt_sb[:], cnt_sb[:], ps.prt[:, 64:128]), inc=S.ev)
                        P.w('v', S.ev, cv2)
                        buv = P.do('v', lambda v: v.tensor_add(base_row[:], base_row[:],
                                                               ps.prt[0:1, 128:192]), inc=S.ev)
                        rstate["base_upd"] = buv
                        kv1 = P.do('v', lambda v: v.scalar_tensor_tensor(
                            keyt[:, 0:64], cnt_sb[:], 1.0 / 8192.0, keybase_bc[:],
                            op0=ALU.mult, op1=ALU.add), inc=S.ev)
                        P.w('v', S.ev, kv1)
                        kv2 = P.do('v', lambda v: v.tensor_mul(keyt[:, 0:64], keyt[:, 0:64],
                                                               onehot_f[:]), inc=S.ev)
                        P.w('v', S.ev, kv2)
                        kv3 = P.do('v', lambda v, sco=sco: v.scalar_tensor_tensor(
                            keyt[:, 64:128], sco, 0.25, keybase_bc[:],
                            op0=ALU.mult, op1=ALU.add), inc=S.ev)
                        P.w('v', S.ev, kv3)
                        kv4 = P.do('v', lambda v: v.tensor_mul(keyt[:, 64:128], keyt[:, 64:128],
                                                               onehot_f[:]), inc=S.ev)
                        P.w('v', S.ev, kv4)
                        k2v = P.do('v', lambda v: v.max(out=rt8[:, 0:8], in_=keyt[:, 0:64]),
                                   inc=S.ev)
                        P.w('v', S.ev, k2v)
                        k1v = P.do('v', lambda v: v.max(out=rt8[:, 8:16], in_=keyt[:, 64:128]),
                                   inc=S.ev)
                        P.w('v', S.ev, k1v)
                        def step(fn):
                            vv_ = P.do('v', fn, inc=S.ev)
                            P.w('v', S.ev, vv_)
                        step(lambda v: v.tensor_copy(rt8i[:], rt8[:, 0:8]))
                        step(lambda v: v.tensor_copy(rt8[:, 16:24], rt8i[:]))
                        step(lambda v: v.tensor_sub(rt8[:, 24:32], rt8[:, 0:8], rt8[:, 16:24]))
                        step(lambda v: v.tensor_scalar_mul(rt8[:, 24:32], rt8[:, 24:32], 8192.0))
                        step(lambda v: v.tensor_scalar(rt8[:, 32:40], rt8[:, 16:24], -1.0, 64.0,
                                                       op0=ALU.mult, op1=ALU.add))
                        step(lambda v: v.scalar_tensor_tensor(
                            rt8[:, 40:48], rt8[:, 32:40], float(CAP), rt8[:, 24:32],
                            op0=ALU.mult, op1=ALU.add))
                        step(lambda v: v.tensor_single_scalar(rt8[:, 48:56], rt8[:, 24:32],
                                                              float(CAP), op=ALU.is_lt))
                        step(lambda v: v.tensor_sub(rt8[:, 56:64], rt8[:, 8:16], rt8[:, 16:24]))
                        step(lambda v, chg=chg: v.tensor_scalar(
                            w6_all[:, chg * 8:(chg + 1) * 8], rt8[:, 56:64], 4.0, None, op0=ALU.mult))
                        step(lambda v, chg=chg: v.tensor_mul(
                            w6_all[:, chg * 8:(chg + 1) * 8], w6_all[:, chg * 8:(chg + 1) * 8],
                            rt8[:, 48:56]))
                        step(lambda v: v.tensor_mul(rt8[:, 40:48], rt8[:, 40:48], rt8[:, 48:56]))
                        step(lambda v: v.tensor_scalar(rt8[:, 48:56], rt8[:, 48:56],
                                                       -float(NSLOT), float(NSLOT),
                                                       op0=ALU.mult, op1=ALU.add))
                        step(lambda v: v.tensor_add(rt8[:, 40:48], rt8[:, 40:48], rt8[:, 48:56]))
                        step(lambda v: v.tensor_scalar_min(rt8[:, 40:48], rt8[:, 40:48],
                                                           float(NSLOT)))
                        step(lambda v: v.tensor_scalar_max(rt8[:, 40:48], rt8[:, 40:48], 0.0))
                        spar = chg % 2
                        P.w('v', S.xs[spar], rstate.get(f"xsfree{spar}", 0))
                        sc1 = P.do('v', lambda v, chg=chg: v.tensor_copy(
                            slot6_all[:, chg * 8:(chg + 1) * 8], rt8[:, 40:48]), inc=S.ev)
                        P.w('v', S.ev, sc1)
                        dcv = P.do('v', lambda v, spar=spar: v.tensor_copy(
                            idx6[spar][:], rt8[:, 40:48]), inc=S.ev)
                        P.w('v', S.ev, dcv)
                        rstate["dec_free"] = dcv
                        tp_evs = None
                        for kd in range(KD):
                            P.w('t', S.ev, dcv if kd == 0 else 0)
                            P.w('t', S.ev, rstate.get("ptm_free", 0))
                            mvv = P.do('t', lambda t, kd=kd, tt=tt, sc4=sc4: t.transpose(
                                ps.ptm[:], xln[:, kd * 1024 + tt * 512 + sc4 * 128:
                                               kd * 1024 + tt * 512 + (sc4 + 1) * 128],
                                cid16_sb[:]), inc=S.mm)
                            P.w('v', S.mm, mvv)
                            tp_evs = P.do('v', lambda v, kd=kd, spar=spar: v.tensor_copy(
                                xtm2[spar][:, kd * 128:(kd + 1) * 128], ps.ptm[:]), inc=S.ev)
                            rstate["ptm_free"] = tp_evs
                        P.w('g', S.ev, tp_evs)
                        for k6 in range(6):
                            P.do('g', lambda g, k6=k6, spar=spar: g.indirect_dma_start(
                                xslots.ap(), IndirectOffsetOnAxis(idx6[spar][:, k6:k6 + 1], 0),
                                xtm2[spar][:], None), inc=S.xs[spar], n=16)
                        rstate[f"xsfree{spar}"] = S.xs[spar].v
                rstate["prev_gmv"] = state["xstage_free_mm"]
                rstate["prev_xln"] = state["xstage_free_ev"]
                rstate["prev_scv"] = state["score_evac"]
                P.w('v', S.ev, state["norm_done"])
                nd_dr = dve_drain()
                P.w('y', S.ev, nd_dr)
                P.do('y', lambda s, q=q: s.dma_start(
                    xlnd3[:, :, q * QTOK:(q + 1) * QTOK],
                    xln.ap().rearrange("p (k t) -> p k t", t=QTOK)), inc=S.xst, n=16)
                soft_barrier()
        xlnd_done = S.xst.v
        if "w6" in tap_t:
            P.w('y', S.ev, S.ev.v)
            P.do('y', lambda s: s.dma_start(tap_t["w6"].ap(), w6_all[:]), inc=S.out, n=16)
            P.do('g', lambda g: g.dma_start(tap_t["s6"].ap(), slot6_all[:]), inc=S.out, n=16)
            P.do('y', lambda s: s.dma_start(tap_t["kb"].ap()[:, 0:64], keybase_bc[:]), inc=S.out, n=16)
            P.do('y', lambda s: s.dma_start(tap_t["kb"].ap()[:, 64:128], onehot_f[:]), inc=S.out, n=16)
            P.do('y', lambda s: s.dma_start(tap_t["kb"].ap()[:, 128:256], keyt[:]), inc=S.out, n=16)
            P.do('y', lambda s: s.dma_start(tap_t["kb"].ap()[:, 256:320], cnt_sb[:]), inc=S.out, n=16)
            P.w('t', S.out, S.out.v)
            P.w('v', S.out, S.out.v)
            P.w('s', S.out, S.out.v)
            P.w('g', S.out, S.out.v)

        with ExitStack() as ectx:
            en = ectx.enter_context
            pG = [en(nc.psum_tensor(un("sG"), [128, 512], F32)) for i in (0, 1)]
            pU = [en(nc.psum_tensor(un("sU"), [128, 512], F32)) for i in (0, 1)]
            pD = [en(nc.psum_tensor(un("sD"), [128, 512], F32)) for i in (0, 1)]
            ost_vals = {0: 0, 1: 0}

            P.w('y', S.xs[0], S.xs[0].v)
            P.w('y', S.xs[1], S.xs[1].v)
            P.w('y', S.ev, S.ev.v)
            P.w('y', S.sc, S.sc.v)
            P.w('y', S.mm, S.mm.v)
            P.w('y', S.xld, S.xld.v)
            P.w('y', S.xst, S.xst.v)
            P.w('g', S.ev, S.ev.v)
            P.w('g', S.mm, S.mm.v)
            soft_barrier()
            if "xsl" in tap_t:
                P.do('y', lambda s: s.dma_start(
                    tap_t["xsl"].ap().rearrange("(a p) d -> p a d", p=128),
                    xslots.ap().rearrange("(a p) d -> p a d", p=128)[:, 0:16, :]), inc=S.out, n=16)
                P.w('t', S.out, S.out.v)
                P.w('v', S.out, S.out.v)

            def load_xg(e):
                par = e % 2
                P.w('y', S.mr, estate.get(f"xgfree{par}", 0))
                for kd in range(KD):
                    P.do('y', lambda s, e=e, kd=kd, par=par: s.dma_start(
                        xg_fm[par][:, kd * 1024:(kd + 1) * 1024],
                        xslots.ap()[e * CAP:(e + 1) * CAP, kd * 128:(kd + 1) * 128],
                        transpose=True), inc=S.xg[par], n=16)
                estate[f"xgld{par}"] = S.xg[par].v

            load_xg(0)
            load_xg(1)
            for e in range(E):
                par = e % 2
                P.w('t', S.wldp[par], glast[e])
                P.w('t', S.xg[par], estate[f"xgld{par}"])
                hv = None
                for ct in range(2):
                    for hc in range(KH):
                        bp = hc % 2
                        P.w('t', S.sc, estate.get(f"silu_free{bp}", 0))
                        P.w('t', S.ev, estate.get(f"h_free{bp}", 0))
                        for kd in range(KD):
                            P.do('t', lambda t, ct=ct, hc=hc, kd=kd, bp=bp, par=par: t.matmul(
                                pG[bp][:],
                                gbuf[par][:, kd * 768 + hc * 128: kd * 768 + (hc + 1) * 128],
                                xg_fm[par][:, kd * 1024 + ct * 512: kd * 1024 + (ct + 1) * 512],
                                start=(kd == 0), stop=(kd == KD - 1)))
                        guv = None
                        for kd in range(KD):
                            guv = P.do('t', lambda t, ct=ct, hc=hc, kd=kd, bp=bp, par=par: t.matmul(
                                pU[bp][:],
                                ubuf[par][:, kd * 768 + hc * 128: kd * 768 + (hc + 1) * 128],
                                xg_fm[par][:, kd * 1024 + ct * 512: kd * 1024 + (ct + 1) * 512],
                                start=(kd == 0), stop=(kd == KD - 1)),
                                inc=(S.mm if kd == KD - 1 else None))
                        P.w('s', S.mm, guv)
                        sgv = P.do('s', lambda s, bp=bp: s.activation(
                            silug[bp][:], pG[bp][:], AF.Silu), inc=S.sc)
                        estate[f"silu_free{bp}"] = sgv
                        P.w('v', S.mm, guv)
                        P.w('v', S.sc, sgv)
                        if hc <= 1:
                            P.w('v', S.mm, estate["hfree"][ct])
                        hv = P.do('v', lambda v, bp=bp, ct=ct, hc=hc: v.tensor_mul(
                            hbuf[ct][:, hc * 512:(hc + 1) * 512], pU[bp][:], silug[bp][:]),
                            inc=S.ev)
                        estate[f"h_free{bp}"] = hv
                xgf = P.do('t', lambda t: t.nop(), inc=S.mr)
                estate[f"xgfree{par}"] = xgf
                dm_last = None
                for ct in range(2):
                    P.w('t', S.ev, hv)
                    for sc4 in range(4):
                        scg = ct * 4 + sc4
                        opar = scg % 2
                        for dh in range(2):
                            P.w('t', S.sc if dh == 0 else S.ev, estate.get(f"dfree{dh}", 0))
                            for kh in range(KH):
                                dm_last = P.do('t', lambda t, ct=ct, sc4=sc4, dh=dh, kh=kh, par=par: t.matmul(
                                    pD[dh][:],
                                    hbuf[ct][:, kh * 512 + sc4 * 128: kh * 512 + (sc4 + 1) * 128],
                                    dbuf[par][:, kh * 1024 + dh * 512: kh * 1024 + (dh + 1) * 512],
                                    start=(kh == 0), stop=(kh == KH - 1)),
                                    inc=(S.mm if kh == KH - 1 else None))
                            eng = 's' if dh == 0 else 'v'
                            P.w(eng, S.mm, dm_last)
                            P.w(eng, S.xs[opar], ost_vals[opar])
                            if dh == 0:
                                evv = P.do('s', lambda s, opar=opar: s.activation(
                                    orow[opar][:, 0:512], pD[0][:], AF.Copy), inc=S.sc)
                                estate["dfree0"] = evv
                            else:
                                evv = P.do('v', lambda v, opar=opar: v.tensor_copy(
                                    orow[opar][:, 512:1024], pD[1][:]), inc=S.ev)
                                estate["dfree1"] = evv
                        P.w('y', S.sc, estate["dfree0"])
                        P.w('y', S.ev, estate["dfree1"])
                        stv = P.do('y', lambda s, e=e, scg=scg, opar=opar: s.dma_start(
                            odram.ap()[e * CAP + scg * 128: e * CAP + (scg + 1) * 128, :],
                            orow[opar][:]), inc=S.xs[opar], n=16)
                        ost_vals[opar] = stv
                    estate["hfree"][ct] = dm_last
                fv = P.do('t', lambda t: t.nop(), inc=S.wfr)
                estate[f"wfree{par}"] = fv
                if e + 2 < E:
                    stream_w(e + 2)
                    load_xg(e + 2)
            exp_arena_free = S.wfr.v
            ost_totals = [S.xs[0].v, S.xs[1].v]
            soft_barrier()

        P.w('g', S.wfr, exp_arena_free)
        P.do('g', lambda g: g.dma_start(shg_sb.ap().rearrange("p (k h) -> p k h", k=KD), shg_r),
             inc=S.wldsh, n=16)
        P.do('g', lambda g: g.dma_start(shu_sb.ap().rearrange("p (k h) -> p k h", k=KD), shu_r),
             inc=S.wldsh, n=16)
        P.do('g', lambda g: g.dma_start(shd_sb.ap().rearrange("p (k h) -> p k h", k=KSH), shd_r),
             inc=S.wldsh, n=16)
        sh_ld = S.wldsh.v
        P.w('g', S.xs[0], ost_totals[0])
        P.w('g', S.xs[1], ost_totals[1])

        if "od" in tap_t:
            P.w('y', S.xs[0], ost_totals[0])
            P.w('y', S.xs[1], ost_totals[1])
            P.do('y', lambda s: s.dma_start(
                tap_t["od"].ap().rearrange("(a p) d -> p a d", p=128),
                odram.ap().rearrange("(a p) d -> p a d", p=128)[:, 0:16, :]), inc=S.out, n=16)
        cstate = {}
        for q in range(NQ):
            P.w('y', S.xst, xlnd_done)
            P.w('y', S.mr, S.mr.v)
            P.w('y', S.mm, cstate.get("xln_used_mm", 0))
            if "jstore" in cstate:
                P.w('y', cstate["jstore"][0], cstate["jstore"][1])
                P.w('y', S.ev, cstate.get("jfree", 0))
            xlv = P.do('y', lambda s, q=q: s.dma_start(
                xln.ap().rearrange("p (k t) -> p k t", t=QTOK),
                xlnd3[:, :, q * QTOK:(q + 1) * QTOK]), inc=S.xld, n=16)
            with ExitStack() as sctx:
                en = sctx.enter_context
                pG2 = [en(nc.psum_tensor(un("cG"), [128, 512], F32)) for i in (0, 1)]
                pU2 = [en(nc.psum_tensor(un("cU"), [128, 512], F32)) for i in (0, 1)]
                pD2 = [en(nc.psum_tensor(un("cD"), [128, 1024], F32)) for i in (0, 1)]
                d_evs = {0: 0, 1: 0}
                sstate = {"hfree": {0: 0, 1: 0}}
                P.w('t', S.wldsh, sh_ld)
                P.w('t', S.xld, xlv)
                for tt in range(NTT):
                    for hc in range(KSH):
                        bp = hc % 2
                        P.w('t', S.sc, sstate.get(f"silu_free{bp}", 0))
                        P.w('t', S.sc, sstate.get(f"ush_free{bp}", 0))
                        for kd in range(KD):
                            P.do('t', lambda t, tt=tt, hc=hc, kd=kd, bp=bp: t.matmul(
                                pG2[bp][:],
                                shg_sb[:, kd * 1536 + hc * 128: kd * 1536 + (hc + 1) * 128],
                                xln[:, kd * 1024 + tt * 512: kd * 1024 + (tt + 1) * 512],
                                start=(kd == 0), stop=(kd == KD - 1)))
                        guv = None
                        for kd in range(KD):
                            guv = P.do('t', lambda t, tt=tt, hc=hc, kd=kd, bp=bp: t.matmul(
                                pU2[bp][:],
                                shu_sb[:, kd * 1536 + hc * 128: kd * 1536 + (hc + 1) * 128],
                                xln[:, kd * 1024 + tt * 512: kd * 1024 + (tt + 1) * 512],
                                start=(kd == 0), stop=(kd == KD - 1)),
                                inc=(S.mm if kd == KD - 1 else None))
                        P.w('s', S.mm, guv)
                        sgv = P.do('s', lambda s, bp=bp: s.activation(silug[bp][:], pG2[bp][:],
                                                                      AF.Silu), inc=S.sc)
                        sstate[f"silu_free{bp}"] = sgv
                        uev = P.do('s', lambda s, bp=bp: s.activation(umb[bp][:], pU2[bp][:],
                                                                      AF.Copy, scale=0.5), inc=S.sc)
                        sstate[f"ush_free{bp}"] = uev
                        P.w('v', S.sc, uev)
                        if hc == 0:
                            P.w('v', S.mm, sstate["hfree"][tt % 2])
                        hv = P.do('v', lambda v, bp=bp, tt=tt, hc=hc: v.tensor_mul(
                            hbuf[tt % 2][:, (hc % KH) * 512:((hc % KH) + 1) * 512],
                            silug[bp][:], umb[bp][:]), inc=S.ev)
                        if hc == KH - 1 or hc == KSH - 1:
                            part = 0 if hc == KH - 1 else 1
                            first = (part == 0)
                            dm_last = None
                            for grp in range(4):
                                gpar = grp % 2
                                P.w('t', S.ev, hv)
                                P.w('t', S.ev, d_evs[gpar])
                                for sub in range(2):
                                    mo = grp * 2 + sub
                                    for kh in range(KH):
                                        khg = part * KH + kh
                                        dm_last = P.do('t', lambda t, tt=tt, gpar=gpar, sub=sub, mo=mo, khg=khg, kh=kh: t.matmul(
                                            pD2[gpar][:, sub * 512:(sub + 1) * 512],
                                            shd_sb[:, khg * 1024 + mo * 128: khg * 1024 + (mo + 1) * 128],
                                            hbuf[tt % 2][:, kh * 512:(kh + 1) * 512],
                                            start=(kh == 0), stop=(kh == KH - 1)),
                                            inc=(S.mm if (sub == 1 and kh == KH - 1) else None))
                                P.w('v', S.mm, dm_last)
                                for sub in range(2):
                                    mo = grp * 2 + sub
                                    ysl = yacc[:, mo * 1024 + tt * 512: mo * 1024 + (tt + 1) * 512]
                                    if first:
                                        ev_ = P.do('v', lambda v, gpar=gpar, sub=sub, ysl=ysl: v.tensor_copy(
                                            ysl, pD2[gpar][:, sub * 512:(sub + 1) * 512]), inc=S.ev)
                                    else:
                                        ev_ = P.do('v', lambda v, gpar=gpar, sub=sub, ysl=ysl: v.tensor_add(
                                            ysl, pD2[gpar][:, sub * 512:(sub + 1) * 512], ysl), inc=S.ev)
                                d_evs[gpar] = ev_
                            sstate["hfree"][tt % 2] = dm_last
                cstate["xln_used_mm"] = S.mm.v
                soft_barrier()
            if q == 1 and "ysh" in tap_t:
                P.w('y', S.ev, S.ev.v)
                P.do('y', lambda s: s.dma_start(tap_t["ysh"].ap(), yacc[:]), inc=S.out, n=16)
                P.w('t', S.out, S.out.v)
                P.w('v', S.out, S.out.v)
                P.w('s', S.out, S.out.v)
                P.w('g', S.out, S.out.v)
            with ExitStack() as cctx:
                en = cctx.enter_context
                ptr = [en(nc.psum_tensor(un("ptr"), [128, 128], BF16)) for i in (0, 1)]
                y_done = max(d_evs.values())
                yad = y_done
                def stage_gid(chg2):
                    half2 = chg2 % 2
                    P.w('v', S.ev, cstate["accdone"].get(chg2 - 2, 0))
                    gcv = P.do('v', lambda v, chg2=chg2, half2=half2: v.tensor_copy(
                        gid6[half2][:], slot6_all.ap()[:, chg2 * 8:(chg2 + 1) * 8]), inc=S.ev)
                    P.w('v', S.ev, gcv)
                    P.w('g', S.ev, gcv)
                    P.w('g', S.ev, cstate["accdone"].get(chg2 - 2, 0))
                    for k6 in range(6):
                        P.do('g', lambda g, k6=k6, half2=half2: g.indirect_dma_start(
                            og[half2][:, k6 * 1024:(k6 + 1) * 1024], None,
                            odram.ap(), IndirectOffsetOnAxis(gid6[half2][:, k6:k6 + 1], 0)),
                            inc=S.gt[half2], n=16)
                    cstate[f"gval{half2}"] = S.gt[half2].v
                cstate.setdefault("accdone", {})
                if q == 0:
                    stage_gid(0)
                for ch in range(8):
                    chg = q * 8 + ch
                    half = chg % 2
                    if chg + 1 < 32:
                        stage_gid(chg + 1)
                    gval = cstate[f"gval{half}"]
                    P.w('v', S.gt[half], gval)
                    P.w('v', S.mm, cstate.get(f"tr_done{half}", 0))
                    for k6 in range(6):
                        if k6 == 0:
                            P.do('v', lambda v, chg=chg, half=half: v.tensor_scalar_mul(
                                acc_tm2[half][:], og[half][:, 0:1024], w6_all[:, chg * 8: chg * 8 + 1]))
                        else:
                            P.do('v', lambda v, chg=chg, k6=k6, half=half: v.scalar_tensor_tensor(
                                acc_tm2[half][:], og[half][:, k6 * 1024:(k6 + 1) * 1024],
                                w6_all[:, chg * 8 + k6: chg * 8 + k6 + 1], acc_tm2[half][:],
                                op0=ALU.mult, op1=ALU.add))
                    bfv = P.do('v', lambda v, half=half: v.tensor_copy(acc_bf[half][:],
                                                                        acc_tm2[half][:]), inc=S.ev)
                    P.w('v', S.ev, bfv)
                    ogf = P.do('v', lambda v: v.nop(), inc=S.ev)
                    cstate["accdone"][chg] = ogf
                    if chg == 0 and "og0" in tap_t:
                        P.w('y', S.ev, S.ev.v)
                        P.do('y', lambda s: s.dma_start(tap_t["og0"].ap(), og[0][:]), inc=S.out, n=16)
                        P.do('y', lambda s: s.dma_start(tap_t["acc0"].ap(), acc_tm2[0][:]), inc=S.out, n=16)
                        P.w('t', S.out, S.out.v)
                        P.w('v', S.out, S.out.v)
                        P.w('s', S.out, S.out.v)
                        P.w('g', S.out, S.out.v)
                    for kd in range(KD):
                        P.w('t', S.ev, ogf if kd == 0 else 0)
                        P.w('t', S.ev, cstate.get(f"ptrfree{kd % 2}", 0))
                        if kd == 0 and ch == 0:
                            P.w('t', S.ev, y_done)
                        mvv = P.do('t', lambda t, kd=kd, half=half: t.transpose(
                            ptr[kd % 2][:], acc_bf[half][:, kd * 128:(kd + 1) * 128], cid16_sb[:]),
                            inc=S.mm)
                        cstate[f"tr_done{half}"] = mvv
                        P.w('v', S.mm, mvv)
                        yad = P.do('v', lambda v, kd=kd, ch=ch: v.tensor_add(
                            yacc[:, kd * 1024 + ch * 128: kd * 1024 + (ch + 1) * 128],
                            ptr[kd % 2][:],
                            yacc[:, kd * 1024 + ch * 128: kd * 1024 + (ch + 1) * 128]),
                            inc=S.ev)
                        cstate[f"ptrfree{kd % 2}"] = yad
                soft_barrier()
            if q == 1 and "yq0" in tap_t:
                P.w('y', S.ev, S.ev.v)
                P.do('y', lambda s: s.dma_start(tap_t["yq0"].ap(), yacc[:]), inc=S.out, n=16)
                P.w('t', S.out, S.out.v)
                P.w('v', S.out, S.out.v)
                P.w('s', S.out, S.out.v)
            jfree = cstate.get("jfree", 0)
            y_all = yad
            for tt in range(NTT):
                t0 = q * QTOK + tt * 512
                P.w('y', S.ev, jfree)
                if tt == 0:
                    P.w('y', S.mm, cstate["xln_used_mm"])
                lv = P.do('y', lambda s, t0=t0: s.dma_start(
                    joins.ap().rearrange("p (k t) -> p k t", k=KD),
                    src3[:, :, t0:t0 + 512]), inc=S.xld, n=16)
                P.w('v', S.xld, lv)
                P.w('v', S.ev, y_all)
                jv = None
                for kd in range(KD):
                    jv = P.do('v', lambda v, kd=kd, tt=tt: v.tensor_add(
                        joins[:, kd * 512:(kd + 1) * 512],
                        yacc[:, kd * 1024 + tt * 512: kd * 1024 + (tt + 1) * 512],
                        joins[:, kd * 512:(kd + 1) * 512]),
                        inc=(S.ev if kd == KD - 1 else None))
                P.w('v', S.ev, jv)
                jv = dve_drain()
                jfree = jv
                cstate["jfree"] = jv
                P.w('y', S.ev, jv)
                osem = S.out if dst3 is out3 else S.xst
                P.do('y', lambda s, t0=t0: s.dma_start(
                    dst3[:, :, t0:t0 + 512],
                    joins.ap().rearrange("p (k t) -> p k t", k=KD)), inc=osem, n=16)
                if dbg_key is not None:
                    P.do('y', lambda s, t0=t0, k=dbg_key: s.dma_start(
                        dbg3[k][:, :, t0:t0 + 512],
                        joins.ap().rearrange("p (k t) -> p k t", k=KD)), inc=S.out, n=16)
                cstate["jstore"] = (osem, osem.v)
                qcarry["store_sem"], qcarry["store_val"] = osem, osem.v
            soft_barrier()

    def emit_attn(src3, dst3, dbg_key=None):
        astate = {}

        def eln(bset, bias_off, dst_fn):
            for hh in range(2):
                P.do('v', lambda v, hh=hh, bias_off=bias_off: v.tensor_add(
                    kf[:, hh * 512:(hh + 1) * 512], bset[hh][:],
                    bqkv_bc[:, bias_off + hh * 512: bias_off + (hh + 1) * 512]))
            P.do('v', lambda v: v.tensor_relu(rbuf[:], kf[:]))
            mv_ = P.do('v', lambda v: v.tensor_sub(exbuf[:], kf[:], rbuf[:]), inc=S.ev)
            P.w('s', S.ev, mv_)
            scv = P.do('s', lambda s: s.activation(exbuf[:], exbuf[:], AF.Exp), inc=S.sc)
            P.w('v', S.sc, scv)
            P.do('v', lambda v: v.scalar_tensor_tensor(kf[:], exbuf[:], -1.0, rbuf[:],
                                                       op0=ALU.add, op1=ALU.add))
            P.do('v', lambda v: v.tensor_mul(rbuf[:], kf[:], kf[:]))
            nv = None
            for h in range(NH):
                nv = P.do('v', lambda v, h=h: v.reduce_sum(
                    nrmA[:, h:h + 1], rbuf[:, h * 64:(h + 1) * 64], axis=AXX),
                    inc=(S.ev if h == NH - 1 else None))
            P.w('s', S.ev, nv)
            sqv = P.do('s', lambda s: s.activation(nrmB[:], nrmA[:], AF.Sqrt), inc=S.sc)
            P.w('v', S.sc, sqv)
            rv = P.do('v', lambda v: v.reciprocal(nrmA[:], nrmB[:]), inc=S.ev)
            P.w('v', S.ev, rv)
            last = None
            for h in range(NH):
                last = P.do('v', lambda v, h=h: v.tensor_scalar_mul(
                    dst_fn(h), kf[:, h * 64:(h + 1) * 64], nrmA[:, h:h + 1]),
                    inc=(S.ev if h == NH - 1 else None))
            return last

        for s_i in range(2):
            base = s_i * SEQ
            with ExitStack() as actx:
                en = actx.enter_context
                ps = NS(mean=en(nc.psum_tensor(un("amean"), [1, 512], F32)),
                        sq=en(nc.psum_tensor(un("asq"), [1, 512], F32)),
                        rstd=en(nc.psum_tensor(un("arstd"), [128, 512], F32)),
                        mmul=en(nc.psum_tensor(un("ammul"), [128, 512], F32)))
                st = {}
                for tt in range(4):
                    t0 = base + tt * 512
                    P.w('y', S.ev, st.get("norm_done", 0))
                    lv = P.do('y', lambda s, t0=t0: s.dma_start(
                        xstage.ap().rearrange("p (k t) -> p k t", k=KD),
                        src3[:, :, t0:t0 + 512]), inc=S.xld, n=16)
                    P.w('v', S.xld, lv)
                    emit_ln_tile(ps, lambda kd: xstage[:, kd * 512:(kd + 1) * 512], st,
                                 eps_ap=eps5, keep_f32=False,
                                 dst_bf=lambda kd, tt=tt: x1ln[:, kd * 2048 + tt * 512:
                                                               kd * 2048 + (tt + 1) * 512])
                soft_barrier()
                if s_i == tseq and "aln" in tap_t:
                    P.w('y', S.ev, S.ev.v)
                    P.w('y', S.sc, S.sc.v)
                    P.do('y', lambda s: s.dma_start(tap_t["aln"].ap(), x1ln[:]), inc=S.out, n=16)
                    P.w('t', S.out, S.out.v)
                    P.w('v', S.out, S.out.v)
                    P.w('s', S.out, S.out.v)
                    P.w('g', S.out, S.out.v)

            P.w('g', S.wfr, S.wfr.v)
            P.do('g', lambda g: g.dma_start(wslot[0].ap().rearrange("p (k n) -> p k n", k=KD), wr["k"]), inc=S.wldp[0], n=16)
            wk_ld = S.wldp[0].v
            P.do('g', lambda g: g.dma_start(wslot[1].ap().rearrange("p (k n) -> p k n", k=KD), wr["v"]), inc=S.wldp[1], n=16)
            wv_ld = S.wldp[1].v
            with ExitStack() as actx:
                en = actx.enter_context
                pP = [en(nc.psum_tensor(un(f"pP{i}"), [128, 512], F32)) for i in range(4)]
                ev_done = {}
                for c in range(16):
                    for w_i in range(2):
                        P.w('t', S.wldp[0] if w_i == 0 else S.wldp[1], wk_ld if w_i == 0 else wv_ld)
                        bset = [pP[2 * w_i], pP[2 * w_i + 1]]
                        P.w('t', S.ev, ev_done.get((c - 1, w_i), 0))
                        prv = None
                        for hh in range(2):
                            for kd in range(KD):
                                prv = P.do('t', lambda t, c=c, hh=hh, kd=kd, w_i=w_i, bset=bset: t.matmul(
                                    bset[hh][:],
                                    x1ln[:, kd * 2048 + c * 128: kd * 2048 + (c + 1) * 128],
                                    wslot[w_i][:, kd * 1024 + hh * 512: kd * 1024 + (hh + 1) * 512],
                                    start=(kd == 0), stop=(kd == KD - 1)),
                                    inc=(S.mm if (hh == 1 and kd == KD - 1) else None))
                        P.w('v', S.mm, prv)
                        if w_i == 0:
                            d = eln(bset, 1024,
                                    lambda h, c=c: kn_tm[:, c * 1024 + h * 64: c * 1024 + (h + 1) * 64])
                        else:
                            d = None
                            for hh in range(2):
                                d = P.do('v', lambda v, c=c, hh=hh, bset=bset: v.tensor_add(
                                    v_tm[:, c * 1024 + hh * 512: c * 1024 + (hh + 1) * 512],
                                    bset[hh][:], bqkv_bc[:, 2048 + hh * 512: 2048 + (hh + 1) * 512]),
                                    inc=(S.ev if hh == 1 else None))
                        ev_done[(c, w_i)] = d
                wfv = P.do('t', lambda t: t.nop(), inc=S.wfr)
                soft_barrier()
                if s_i == tseq and "akn" in tap_t:
                    P.w('y', S.ev, S.ev.v)
                    P.w('y', S.sc, S.sc.v)
                    P.do('y', lambda s: s.dma_start(tap_t["akn"].ap(), kn_tm[:]), inc=S.out, n=16)
                    P.w('t', S.out, S.out.v)
                    P.w('v', S.out, S.out.v)
                    P.w('s', S.out, S.out.v)
                    P.w('g', S.out, S.out.v)
                if s_i == tseq and "avn" in tap_t:
                    P.w('y', S.ev, S.ev.v)
                    P.w('y', S.sc, S.sc.v)
                    P.do('y', lambda s: s.dma_start(tap_t["avn"].ap(), v_tm[:]), inc=S.out, n=16)
                    P.w('t', S.out, S.out.v)
                    P.w('v', S.out, S.out.v)
                    P.w('s', S.out, S.out.v)
                    P.w('g', S.out, S.out.v)

            with ExitStack() as actx:
                en = actx.enter_context
                pKV = [en(nc.psum_tensor(un(f"pKV{i}"), [128, 64], F32)) for i in range(8)]
                kvv = None
                for c in range(16):
                    for h in range(NH):
                        bank = pKV[h // 2]
                        rh = (h % 2) * 64
                        kvv = P.do('t', lambda t, c=c, h=h, bank=bank, rh=rh: t.matmul(
                            bank[rh:rh + 64, :],
                            kn_tm[:, c * 1024 + h * 64: c * 1024 + (h + 1) * 64],
                            v_tm[:, c * 1024 + h * 64: c * 1024 + (h + 1) * 64],
                            start=(c == 0), stop=(c == 15)),
                            inc=(S.mm if (c == 15 and h == NH - 1) else None))
                P.w('s', S.mm, kvv)
                kve = None
                for h in range(NH):
                    bank = pKV[h // 2]
                    rh = (h % 2) * 64
                    kve = P.do('s', lambda s, h=h, bank=bank, rh=rh: s.activation(
                        kv_sb[rh:rh + 64, (h // 2) * 64:(h // 2 + 1) * 64],
                        bank[rh:rh + 64, :], AF.Copy, scale=0.125),
                        inc=(S.sc if h == NH - 1 else None))
                soft_barrier()
                if s_i == tseq and "akv" in tap_t:
                    P.w('y', S.ev, S.ev.v)
                    P.w('y', S.sc, S.sc.v)
                    P.do('y', lambda s: s.dma_start(tap_t["akv"].ap(), kv_sb[:]), inc=S.out, n=16)
                    P.w('t', S.out, S.out.v)
                    P.w('v', S.out, S.out.v)
                    P.w('s', S.out, S.out.v)
                    P.w('g', S.out, S.out.v)

            P.w('g', S.wfr, S.wfr.v)
            P.do('g', lambda g: g.dma_start(wslot[0].ap().rearrange("p (k n) -> p k n", k=KD), wr["q"]), inc=S.wldp[0], n=16)
            wq_ld = S.wldp[0].v
            P.do('g', lambda g: g.dma_start(wslot[1].ap().rearrange("p (k n) -> p k n", k=KD), wr["o"]), inc=S.wldp[1], n=16)
            wo_ld = S.wldp[1].v
            for hs in range(2):
                with ExitStack() as actx:
                    en = actx.enter_context
                    pP = [en(nc.psum_tensor(un(f"pQ{i}"), [128, 512], F32)) for i in range(4)]
                    pT = en(nc.psum_tensor(un("pT"), [128, 128], BF16))
                    ev_done = {}
                    tp_done = {}
                    for c8 in range(8):
                        c = hs * 8 + c8
                        P.w('t', S.wldp[0], wq_ld)
                        bset = [pP[2 * (c8 % 2)], pP[2 * (c8 % 2) + 1]]
                        P.w('t', S.ev, ev_done.get(c8 - 2, 0))
                        prv = None
                        for hh in range(2):
                            for kd in range(KD):
                                prv = P.do('t', lambda t, c=c, hh=hh, kd=kd, bset=bset: t.matmul(
                                    bset[hh][:],
                                    x1ln[:, kd * 2048 + c * 128: kd * 2048 + (c + 1) * 128],
                                    wslot[0][:, kd * 1024 + hh * 512: kd * 1024 + (hh + 1) * 512],
                                    start=(kd == 0), stop=(kd == KD - 1)),
                                    inc=(S.mm if (hh == 1 and kd == KD - 1) else None))
                        P.w('v', S.mm, prv)
                        d = eln(bset, 0, lambda h: qn_ch[:, h * 64:(h + 1) * 64])
                        ev_done[c8] = d
                        for kd in range(KD):
                            if kd == 0:
                                P.w('t', S.ev, d)
                            prev_tp = tp_done.get((c8, kd - 1)) if kd > 0 else tp_done.get((c8 - 1, KD - 1))
                            P.w('t', S.ev, prev_tp or 0)
                            mv_ = P.do('t', lambda t, kd=kd: t.transpose(
                                pT[:], qn_ch[:, kd * 128:(kd + 1) * 128], cid16_sb[:]), inc=S.mm)
                            P.w('v', S.mm, mv_)
                            tv = P.do('v', lambda v, kd=kd, c8=c8: v.tensor_copy(
                                qn_fm[:, kd * 1024 + c8 * 128: kd * 1024 + (c8 + 1) * 128], pT[:]),
                                inc=S.ev)
                            tp_done[(c8, kd)] = tv
                    if hs == 1:
                        astate["wq_free"] = P.do('t', lambda t: t.nop(), inc=S.wfr)
                    soft_barrier()
                with ExitStack() as actx:
                    en = actx.enter_context
                    pC = [en(nc.psum_tensor(un(f"pC{i}"), [128, 512], F32)) for i in range(4)]
                    last_ev = {}
                    for tt in range(2):
                        for h in range(NH):
                            slot = (h // 2) % 4
                            rh = (h % 2) * 64
                            P.w('t', S.ev, last_ev.get((slot, rh), 0))
                            cv = P.do('t', lambda t, tt=tt, h=h, slot=slot, rh=rh: t.matmul(
                                pC[slot][rh:rh + 64, :],
                                kv_sb[rh:rh + 64, (h // 2) * 64:(h // 2 + 1) * 64],
                                qn_fm[rh:rh + 64, (h // 2) * 1024 + tt * 512:
                                      (h // 2) * 1024 + (tt + 1) * 512],
                                start=True, stop=True), inc=S.mm)
                            P.w('v', S.mm, cv)
                            ev_ = P.do('v', lambda v, tt=tt, h=h, slot=slot, rh=rh, hs=hs: v.tensor_copy(
                                ctx_fm[rh:rh + 64,
                                       (h // 2) * 2048 + hs * 1024 + tt * 512:
                                       (h // 2) * 2048 + hs * 1024 + (tt + 1) * 512]
                                if rh else
                                ctx_fm[0:64,
                                       (h // 2) * 2048 + hs * 1024 + tt * 512:
                                       (h // 2) * 2048 + hs * 1024 + (tt + 1) * 512],
                                pC[slot][rh:rh + 64, :]), inc=S.ev)
                            last_ev[(slot, rh)] = ev_
                    soft_barrier()
                if s_i == tseq and hs == 0 and "aqn" in tap_t:
                    P.w('y', S.ev, S.ev.v)
                    P.do('y', lambda s: s.dma_start(tap_t["aqn"].ap(), qn_fm[:]), inc=S.out, n=16)
                    P.w('t', S.out, S.out.v)
                    P.w('v', S.out, S.out.v)
                    P.w('s', S.out, S.out.v)
                    P.w('g', S.out, S.out.v)

            with ExitStack() as actx:
                en = actx.enter_context
                pW = [en(nc.psum_tensor(un(f"pW{i}"), [128, 512], F32)) for i in range(2)]
                ps = NS(mean=en(nc.psum_tensor(un("wmean"), [1, 512], F32)),
                        sq=en(nc.psum_tensor(un("wsq"), [1, 512], F32)),
                        rstd=en(nc.psum_tensor(un("wrstd"), [128, 512], F32)),
                        mmul=en(nc.psum_tensor(un("wmmul"), [128, 512], F32)))
                st = {}
                for tt in range(4):
                    P.w('t', S.wldp[1], wo_ld)
                    for mo in range(KD):
                        bp = mo % 2
                        P.w('t', S.ev, st.get(f"wo_ev{bp}", 0))
                        wvv = None
                        for kd in range(KD):
                            wvv = P.do('t', lambda t, tt=tt, mo=mo, kd=kd, bp=bp: t.matmul(
                                pW[bp][:],
                                wslot[1][:, kd * 1024 + mo * 128: kd * 1024 + (mo + 1) * 128],
                                ctx_fm[:, kd * 2048 + tt * 512: kd * 2048 + (tt + 1) * 512],
                                start=(kd == 0), stop=(kd == KD - 1)),
                                inc=(S.mm if kd == KD - 1 else None))
                        P.w('v', S.mm, wvv)
                        evv = P.do('v', lambda v, mo=mo, tt=tt, bp=bp: v.scalar_tensor_tensor(
                            pre1[:, mo * 512:(mo + 1) * 512], pW[bp][:],
                            fmv_t["bo"][:, mo:mo + 1],
                            x1ln[:, mo * 2048 + tt * 512: mo * 2048 + (tt + 1) * 512],
                            op0=ALU.add, op1=ALU.add), inc=S.ev)
                        st[f"wo_ev{bp}"] = evv
                    emit_ln_tile(ps, lambda kd: pre1[:, kd * 512:(kd + 1) * 512], st,
                                 eps_ap=eps12, keep_f32=False,
                                 gains=(fmv_t["g1"], fmv_t["be1"]),
                                 dst_gain=lambda kd, tt=tt: x1ln[:, kd * 2048 + tt * 512:
                                                                 kd * 2048 + (tt + 1) * 512])
                astate["wo_free"] = P.do('t', lambda t: t.nop(), inc=S.wfr)
                astate["wo_done_ev"] = S.ev.v
                soft_barrier()
                if s_i == tseq and "actx" in tap_t:
                    P.w('y', S.ev, S.ev.v)
                    P.w('y', S.sc, S.sc.v)
                    P.do('y', lambda s: s.dma_start(tap_t["actx"].ap(), ctx_fm[:]), inc=S.out, n=16)
                    P.w('t', S.out, S.out.v)
                    P.w('v', S.out, S.out.v)
                    P.w('s', S.out, S.out.v)
                    P.w('g', S.out, S.out.v)
                if s_i == tseq and "amid" in tap_t:
                    P.w('y', S.ev, S.ev.v)
                    P.w('y', S.sc, S.sc.v)
                    P.do('y', lambda s: s.dma_start(tap_t["amid"].ap(), x1ln[:]), inc=S.out, n=16)
                    P.w('t', S.out, S.out.v)
                    P.w('v', S.out, S.out.v)
                    P.w('s', S.out, S.out.v)
                    P.w('g', S.out, S.out.v)

            for f_i in range(4):
                fbase = base + f_i * 512
                with ExitStack() as actx:
                    en = actx.enter_context
                    pF = [en(nc.psum_tensor(un(f"pF{i}"), [128, 512], F32)) for i in range(2)]
                    for mos in range(8):
                        wp = mos % 2
                        P.w('g', S.wfr, astate.get("wq_free", 0))
                        P.w('g', S.wfr, astate.get(f"w1free{wp}", 0))
                        P.do('g', lambda g, mos=mos, wp=wp: g.dma_start(
                            w1slot[wp].ap().rearrange("p (k n) -> p k n", n=512),
                            w1_r[:, :, mos * 512:(mos + 1) * 512]), inc=S.wldp[wp], n=16)
                        w1_ld = S.wldp[wp].v
                        P.w('t', S.wldp[wp], w1_ld)
                        for moc in range(4):
                            bp = moc % 2
                            P.w('t', S.sc, astate.get(f"ff_sc{bp}", 0))
                            fvv = None
                            for kd in range(KD):
                                fvv = P.do('t', lambda t, mos=mos, moc=moc, kd=kd, bp=bp, wp=wp, f_i=f_i: t.matmul(
                                    pF[bp][:],
                                    w1slot[wp][:, kd * 512 + moc * 128: kd * 512 + (moc + 1) * 128],
                                    x1ln[:, kd * 2048 + f_i * 512: kd * 2048 + (f_i + 1) * 512],
                                    start=(kd == 0), stop=(kd == KD - 1)),
                                    inc=(S.mm if kd == KD - 1 else None))
                            P.w('s', S.mm, fvv)
                            mo = mos * 4 + moc
                            scv = P.do('s', lambda s, mo=mo, bp=bp: s.activation(
                                hff[:, mo * 512:(mo + 1) * 512], pF[bp][:], AF.Gelu,
                                bias=b1_fm[:, mo:mo + 1]), inc=S.sc)
                            astate[f"ff_sc{bp}"] = scv
                        w1f = P.do('t', lambda t: t.nop(), inc=S.wfr)
                        astate[f"w1free{wp}"] = w1f
                    soft_barrier()
                if s_i == tseq and f_i == 0 and "ahff" in tap_t:
                    P.w('y', S.sc, S.sc.v)
                    P.do('y', lambda s: s.dma_start(tap_t["ahff"].ap(), hff[:]), inc=S.out, n=16)
                    P.w('t', S.out, S.out.v)
                    P.w('v', S.out, S.out.v)
                    P.w('s', S.out, S.out.v)
                    P.w('g', S.out, S.out.v)
                with ExitStack() as actx:
                    en = actx.enter_context
                    pM = [en(nc.psum_tensor(un(f"pM{i}"), [128, 512], F32)) for i in range(8)]
                    w2v = None
                    for khs in range(8):
                        wp = khs % 2
                        P.w('g', S.wfr, astate.get("wo_free", 0))
                        P.w('g', S.wfr, astate.get(f"w2free{wp}", 0))
                        P.do('g', lambda g, khs=khs, wp=wp: g.dma_start(
                            w2slot[wp].ap().rearrange("p (k n) -> p k n", n=1024),
                            w2_r[:, khs * 4:(khs + 1) * 4, :]), inc=S.wldp[wp], n=16)
                        w2_ld = S.wldp[wp].v
                        P.w('t', S.wldp[wp], w2_ld)
                        for k4 in range(4):
                            kh = khs * 4 + k4
                            for mo in range(KD):
                                w2v = P.do('t', lambda t, k4=k4, kh=kh, mo=mo, wp=wp: t.matmul(
                                    pM[mo][:],
                                    w2slot[wp][:, k4 * 1024 + mo * 128: k4 * 1024 + (mo + 1) * 128],
                                    hff[:, kh * 512:(kh + 1) * 512],
                                    start=(kh == 0), stop=(kh == KFF - 1)),
                                    inc=(S.mm if (kh == KFF - 1 and mo == KD - 1) else None))
                        w2f = P.do('t', lambda t: t.nop(), inc=S.wfr)
                        astate[f"w2free{wp}"] = w2f
                    P.w('v', S.mm, w2v)
                    for mo in range(KD):
                        P.do('v', lambda v, mo=mo, f_i=f_i: v.scalar_tensor_tensor(
                            pre2[:, mo * 512:(mo + 1) * 512], pM[mo][:],
                            fmv_t["b2"][:, mo:mo + 1],
                            x1ln[:, mo * 2048 + f_i * 512: mo * 2048 + (f_i + 1) * 512],
                            op0=ALU.add, op1=ALU.add), inc=(S.ev if mo == KD - 1 else None))
                    soft_barrier()
                    if s_i == tseq and f_i == 0 and "apre2" in tap_t:
                        P.w('y', S.ev, S.ev.v)
                        P.do('y', lambda s: s.dma_start(tap_t["apre2"].ap(), pre2[:]), inc=S.out, n=16)
                        P.w('t', S.out, S.out.v)
                        P.w('v', S.out, S.out.v)
                        P.w('s', S.out, S.out.v)
                        P.w('g', S.out, S.out.v)
                with ExitStack() as actx:
                    en = actx.enter_context
                    ps = NS(mean=en(nc.psum_tensor(un("fmean"), [1, 512], F32)),
                            sq=en(nc.psum_tensor(un("fsq"), [1, 512], F32)),
                            rstd=en(nc.psum_tensor(un("frstd"), [128, 512], F32)),
                            mmul=en(nc.psum_tensor(un("fmmul"), [128, 512], F32)))
                    st = {}
                    P.w('y', S.ev, astate["wo_done_ev"] if f_i == 0 else 0)
                    P.w('y', S.ev, astate.get("joinx_free", 0))
                    if astate.get("joinx_store", 0):
                        P.w('y', astate["joinx_sem"], astate["joinx_store"])
                    lv = P.do('y', lambda s, fbase=fbase: s.dma_start(
                        joinx.ap().rearrange("p (k t) -> p k t", k=KD),
                        src3[:, :, fbase:fbase + 512]), inc=S.xld, n=16)
                    d = emit_ln_tile(ps, lambda kd: pre2[:, kd * 512:(kd + 1) * 512], st,
                                     eps_ap=eps12, keep_f32=False,
                                     gains=(fmv_t["g2"], fmv_t["be2"]),
                                     dst_gain=lambda kd: pre2[:, kd * 512:(kd + 1) * 512])
                    P.w('v', S.ev, d)
                    P.w('v', S.xld, lv)
                    jv = None
                    for kd in range(KD):
                        jv = P.do('v', lambda v, kd=kd: v.tensor_add(
                            joinx[:, kd * 512:(kd + 1) * 512], pre2[:, kd * 512:(kd + 1) * 512],
                            joinx[:, kd * 512:(kd + 1) * 512]), inc=(S.ev if kd == KD - 1 else None))
                    P.w('v', S.ev, jv)
                    jv = dve_drain()
                    astate["joinx_free"] = jv
                    P.w('y', S.ev, jv)
                    osem = S.out if dst3 is out3 else S.xst
                    P.do('y', lambda s, fbase=fbase: s.dma_start(
                        dst3[:, :, fbase:fbase + 512],
                        joinx.ap().rearrange("p (k t) -> p k t", k=KD)), inc=osem, n=16)
                    if dbg_key is not None:
                        P.do('y', lambda s, fbase=fbase, k=dbg_key: s.dma_start(
                            dbg3[k][:, :, fbase:fbase + 512],
                            joinx.ap().rearrange("p (k t) -> p k t", k=KD)), inc=S.out, n=16)
                    astate["joinx_sem"], astate["joinx_store"] = osem, osem.v
                    soft_barrier()

    ffn_emit = emit_ffn_sparse if sparse else emit_ffn
    chain = []
    if "ffn1" in stages:
        chain.append(("ffn1", ffn_emit))
    if "attn" in stages:
        chain.append(("attn", emit_attn))
    if "ffn2" in stages:
        chain.append(("ffn2", ffn_emit))
    bufs = [x_in3]
    internals = [xr1_3, xr2_3]
    for i in range(len(chain)):
        bufs.append(out3 if i == len(chain) - 1 else internals[i])
    for i, (nm, fn) in enumerate(chain):
        P.w('y', S.xst, S.xst.v)
        fn(bufs[i], bufs[i + 1], dbg_key=(nm if nm in dbg3 else None))

    if "xr1fin" in tap_t:
        P.w('y', S.out, S.out.v)
        P.w('y', S.xst, S.xst.v)
        P.do('y', lambda s: s.dma_start(
            tap_t["xr1fin"].ap().rearrange("(k p) t -> p k t", p=128),
            xr1_3[:, :, 3072:4096]), inc=S.out, n=16)
        P.w('t', S.out, S.out.v)
        P.w('v', S.out, S.out.v)
        P.w('s', S.out, S.out.v)
        P.w('g', S.out, S.out.v)
    if "mask" in tap_t:
        P.w('y', S.xst, S.xst.v)
        P.do('y', lambda s: s.dma_start(tap_t["mask"].ap(), maskT.ap()), inc=S.out, n=16)
        P.w('t', S.out, S.out.v)
        P.w('v', S.out, S.out.v)
        P.w('s', S.out, S.out.v)
        P.w('g', S.out, S.out.v)
    P.w('y', S.out, S.out.v)

    with nc.Block() as block:
        @block.tensor
        def _(t):
            for f in P.ops['t']:
                f(t)

        @block.vector
        def _(v):
            for f in P.ops['v']:
                f(v)

        @block.scalar
        def _(s):
            for f in P.ops['s']:
                f(s)

        @block.gpsimd
        def _(g):
            for f in P.ops['g']:
                f(g)

        @block.sync
        def _(s):
            for f in P.ops['y']:
                f(s)

    return nc


def _in_maps(inputs, nc_dummy=None):
    x = inputs["L0_attn_x"]
    ustrict = np.triu(np.ones((128, 128), dtype=np.float32), k=1)
    consts = dict(cid32=np.eye(128, dtype=np.float32),
                  custrict=ustrict,
                  ckeybase=(64.0 - np.arange(64, dtype=np.float32)).reshape(1, 64))
    wmap = dict(
        Wq=inputs["L0_attn_Wq"], Wk=inputs["L0_attn_Wk"], Wv=inputs["L0_attn_Wv"],
        Wo=inputs["L0_attn_Wo"], W1=inputs["L0_attn_W1"], W2=inputs["L0_attn_W2"],
        bq=inputs["L0_attn_bq"], bk=inputs["L0_attn_bk"], bv=inputs["L0_attn_bv"],
        bo=inputs["L0_attn_bo"], b1=inputs["L0_attn_b1"], b2=inputs["L0_attn_b2"],
        g1=inputs["L0_attn_g1"], be1=inputs["L0_attn_be1"],
        g2=inputs["L0_attn_g2"], be2=inputs["L0_attn_be2"],
        gate_w=inputs["L0_ffn_gate_w"], gate_bias=inputs["L0_ffn_gate_bias"],
        all_g=inputs["L0_ffn_all_gate_proj"], all_u=inputs["L0_ffn_all_up_proj"],
        all_d=inputs["L0_ffn_all_down_proj"],
        sh_g=inputs["L0_ffn_sh_gate"], sh_u=inputs["L0_ffn_sh_up"],
        sh_d=inputs["L0_ffn_sh_down"],
    )
    wmap = {k: np.ascontiguousarray(v, dtype=np.float32) for k, v in wmap.items()}
    maps = []
    for c in range(8):
        xc = np.ascontiguousarray(np.asarray(x[2 * c:2 * c + 2], dtype=np.float32).reshape(T, D).T)
        maps.append(dict(x_fm=xc, **wmap, **consts))
    return maps


def kernel(**inputs):
    nc = build_nc()
    maps = _in_maps(inputs)
    res = run_bass_kernel_spmd(nc, maps, core_ids=list(range(8)))
    outs = [res.results[c]["out_fm"].T.reshape(2, SEQ, D) for c in range(8)]
    return np.ascontiguousarray(np.concatenate(outs, axis=0), dtype=np.float32)
